# version 01
# baseline (speedup 1.0000x reference)
import numpy as np
import ml_dtypes
from contextlib import ExitStack

import concourse.bass as bass
from concourse.bacc import Bacc
import concourse.mybir as mybir
from concourse.bass_utils import run_bass_kernel_spmd

F32 = mybir.dt.float32
BF16 = mybir.dt.bfloat16
AF = mybir.ActivationFunctionType
ALU = mybir.AluOpType

D = 1024
KD = 8
NH = 16
HDM = 64
HF = 1408
KHF = 11
NE = 9
LN_EPS = 1e-5
ATT_EPS = 1e-6


class Chain:
    def __init__(self, nc, es, name):
        self.h = es.enter_context(nc.semaphore(name))
        self.n = 0


class B:
    def __init__(self, T, n_cores):
        self.T = T
        self.n_cores = n_cores
        self.pairs = [[2 * i, 2 * i + 1] for i in range(max(1, n_cores // 2))]
        self.nc = Bacc("TRN2", target_bir_lowering=False)
        self.es = ExitStack()
        self.dr = {}

    def sb(self, name, shape, dt=F32):
        return self.es.enter_context(self.nc.sbuf_tensor(name, shape, dt))

    def chain(self, name):
        return Chain(self.nc, self.es, name)

    def din(self, n, s, dt=F32):
        self.dr[n] = self.nc.dram_tensor(n, s, dt, kind="ExternalInput").ap()
        return self.dr[n]

    def dint(self, n, s, dt=F32):
        self.dr[n] = self.nc.dram_tensor(n, s, dt, kind="Internal").ap()
        return self.dr[n]

    def dstep(self, eng, q, ch, out, in_, qv=None):
        eng.dma_start(out=out, in_=in_).then_inc(q.h, 16)
        if qv is None:
            q.n += 16
            qv = q.n
        eng.wait_ge(q.h, qv)
        eng.sem_inc(ch.h, 1)

    def build(self):
        nc, T = self.nc, self.T
        self.din("xin", [D, T])
        self.out = nc.dram_tensor("out", [D, T], F32, kind="ExternalOutput").ap()
        self.din("meta", [1, 16])
        for n in ("wq", "wk", "wv", "wo", "bwk", "bwv"):
            self.din(n, [D, D])
        self.din("bqkvo", [1, 4 * D])
        self.din("bbkv", [1, 2 * D])
        self.din("rhw", [1, D])
        self.din("bww", [1, 8 * NH])
        self.din("gwt", [D, 8])
        self.din("gbr", [1, 8])
        self.din("w1c", [NE, D, HF]); self.din("w3c", [NE, D, HF])
        self.din("w2c", [NE, HF, D])
        self.din("b1c", [1, NE * HF]); self.din("b3c", [1, NE * HF])
        self.din("b2c", [1, NE * D])

        self.dint("rA", [D, T])
        self.dint("rB", [D, T])
        self.dint("hsd", [KD, 128, T], BF16)
        self.dint("zqfd", [KD, 128, T], BF16)
        self.dint("w1b", [NE, D, HF], BF16)
        self.dint("w3b", [NE, D, HF], BF16)
        self.dint("w2b", [NE, HF, D], BF16)
        self.dint("rwd", [NE, T])
        self.dint("kvx_i", [65, D]); self.dint("kvx_o", [65, D])
        NC = T // 128
        self.dint("r1x_i", [1, D]); self.dint("r1x_o", [2, D])
        self.dint("r2f_i", [4, D]); self.dint("r2f_o", [8, D])
        self.dint("r2b_i", [4, D]); self.dint("r2b_o", [8, D])
        self.dint("sind", [2, NC, 5, NH * HDM], BF16)
        self.dint("csrd", [2, 16, T], BF16)
        self.dint("flA", [16, NC * 4], BF16)
        self.dint("flB", [16, NC * 4], BF16)
        self.dint("flE", [1, NC * 16], BF16)

        self.ones_l = self.sb("ones_l", [1, 512], BF16)
        self.ones_c = self.sb("ones_c", [128, 1], BF16)
        self.ident = self.sb("ident", [128, 128], BF16)
        self.metat = self.sb("metat", [1, 16], F32)

        with nc.Block() as block:
            self.emit_setup(block)
            prev = self.setup_ch
            prev = self.emit_ln(block, self.dr["xin"], "n1", prev)
            prev = self.emit_attn_a(block, self.dr["xin"], self.dr["rA"], prev)
            prev = self.emit_ln(block, self.dr["rA"], "n2", prev)
            prev = self.emit_ffn(block, self.dr["rA"], self.dr["rB"], 0, prev)
            prev = self.emit_ln(block, self.dr["rB"], "n3", prev)
            prev = self.emit_attn_b(block, self.dr["rB"], self.dr["rA"], prev)
            prev = self.emit_ln(block, self.dr["rA"], "n4", prev)
            prev = self.emit_ffn(block, self.dr["rA"], self.out, 1, prev)
            self.final_ch = prev

        self.es.close()
        nc.finalize()
        return nc

    def emit_setup(self, block):
        nc = self.nc
        ch = self.chain("setup")
        self.setup_ch = ch
        self.wcast = self.chain("wcast")
        qsy = self.chain("su_qsy")
        d = self.dr

        @block.sync
        def _(sy):
            self.dstep(sy, qsy, ch, self.metat[:], d["meta"][:])
        ch.n += 1

        @block.gpsimd
        def _(g):
            g.iota(self.ident[:], pattern=[[-1, 128]], base=0, channel_multiplier=1,
                   allow_small_or_imprecise_dtypes=True).then_inc(ch.h, 1)
        ch.n += 1

        @block.vector
        def _(v):
            v.memset(self.ones_l[:], 1.0)
            v.memset(self.ones_c[:], 1.0)
            v.drain()
            v.wait_ge(ch.h, ch.n)
            v.tensor_scalar(self.ident[:], self.ident[:], 0.0, 1.0,
                            ALU.is_equal, ALU.mult)
            v.drain().then_inc(ch.h, 1)
        ch.n += 1

        @block.gpsimd
        def _(g):
            for e in range(NE):
                for a, b_ in (("w1c", "w1b"), ("w3c", "w3b"), ("w2c", "w2b")):
                    g.dma_start(out=d[b_][e], in_=d[a][e]).then_inc(self.wcast.h, 16)
                    self.wcast.n += 16

    def emit_ln(self, block, src, tag, prev):
        nc, T = self.nc, self.T
        TC = T // 512
        es = ExitStack()
        sbl = lambda n, sh, dt=F32: es.enter_context(nc.sbuf_tensor(tag + n, sh, dt))
        psl = lambda n, sh, dt=F32: es.enter_context(nc.psum_tensor(tag + n, sh, dt))
        d = self.dr

        xc = sbl("xc", [128, KD, 512])
        xb = sbl("xb", [128, KD, 512], BF16)
        sq = sbl("sq", [128, KD, 512], BF16)
        hs = sbl("hs", [128, KD, 512], BF16)
        rows = sbl("rows", [1, 6, 512])
        rbf = sbl("rbf", [1, 2, 512], BF16)
        pstat = psl("pst", [1, 2, 512])
        pbc = psl("pbc", [128, 2, 512])

        ch = self.chain(tag + "ch")
        qsy = self.chain(tag + "qsy")
        qg = self.chain(tag + "qg")
        src_r = src.rearrange("(k p) t -> p k t", p=128)
        hsd_r = d["hsd"].rearrange("k p t -> p k t")
        pn = prev.n
        DL = 10

        @block.sync
        def _(sy):
            sy.wait_ge(prev.h, pn)
            with sy.Fori(0, TC) as i:
                sy.wait_ge(ch.h, i * DL)
                self.dstep(sy, qsy, ch, xc[:], src_r[:, :, bass.ts(i, 512)],
                           qv=16 * (i + 1))

        @block.vector
        def _(v):
            with v.Fori(0, TC) as i:
                it = i * DL
                v.wait_ge(ch.h, it + 1)
                for k in range(KD):
                    inst = v.tensor_scalar(xb[:, k], xc[:, k], 1.0, None, ALU.mult)
                v.drain()
                inst.then_inc(ch.h, 1)
                v.wait_ge(ch.h, it + 4)
                v.tensor_scalar(rows[:, 0], pstat[:, 0], 1.0 / D, None, ALU.mult)
                v.tensor_scalar(rows[:, 1], pstat[:, 1], 1.0 / D, None, ALU.mult)
                v.drain()
                v.tensor_tensor(rows[:, 2], rows[:, 0], rows[:, 0], ALU.mult)
                v.drain()
                v.tensor_tensor(rows[:, 3], rows[:, 1], rows[:, 2], ALU.subtract)
                v.drain()
                inst = v.tensor_scalar(rows[:, 3], rows[:, 3], LN_EPS, None, ALU.add)
                v.drain()
                inst.then_inc(ch.h, 1)
                v.wait_ge(ch.h, it + 6)
                v.reciprocal(rows[:, 4], rows[:, 4])
                v.drain()
                v.tensor_tensor(rows[:, 5], rows[:, 0], rows[:, 4], ALU.mult)
                v.drain()
                v.tensor_scalar(rbf[:, 0], rows[:, 4], 1.0, None, ALU.mult)
                inst = v.tensor_scalar(rbf[:, 1], rows[:, 5], 1.0, None, ALU.mult)
                v.drain()
                inst.then_inc(ch.h, 1)
                v.wait_ge(ch.h, it + 8)
                for k in range(KD):
                    v.tensor_tensor(hs[:, k], xb[:, k], pbc[:, 0], ALU.mult)
                v.drain()
                for k in range(KD):
                    inst = v.tensor_tensor(hs[:, k], hs[:, k], pbc[:, 1], ALU.subtract)
                v.drain()
                inst.then_inc(ch.h, 1)

        @block.scalar
        def _(sc):
            with sc.Fori(0, TC) as i:
                it = i * DL
                sc.wait_ge(ch.h, it + 2)
                for k in range(KD):
                    inst = sc.activation(sq[:, k], xb[:, k], AF.Square)
                sc.drain()
                inst.then_inc(ch.h, 1)
                sc.wait_ge(ch.h, it + 5)
                inst = sc.activation(rows[:, 4], rows[:, 3], AF.Sqrt)
                sc.drain()
                inst.then_inc(ch.h, 1)

        @block.tensor
        def _(te):
            with te.Fori(0, TC) as i:
                it = i * DL
                te.wait_ge(ch.h, it + 3)
                for k in range(KD):
                    te.matmul(pstat[:, 0], self.ones_c[:], xb[:, k],
                              start=(k == 0), stop=(k == KD - 1))
                for k in range(KD):
                    inst = te.matmul(pstat[:, 1], self.ones_c[:], sq[:, k],
                                     start=(k == 0), stop=(k == KD - 1))
                inst.then_inc(ch.h, 1)
                te.wait_ge(ch.h, it + 7)
                te.matmul(pbc[:, 0], self.ones_l[:, 0:128], rbf[:, 0],
                          start=True, stop=True)
                inst = te.matmul(pbc[:, 1], self.ones_l[:, 0:128], rbf[:, 1],
                                 start=True, stop=True)
                inst.then_inc(ch.h, 1)

        @block.gpsimd
        def _(g):
            with g.Fori(0, TC) as i:
                it = i * DL
                g.wait_ge(ch.h, it + 9)
                self.dstep(g, qg, ch, hsd_r[:, :, bass.ts(i, 512)], hs[:],
                           qv=16 * (i + 1))

        ch.n = TC * DL
        qsy.n = 16 * TC
        qg.n = 16 * TC
        es.close()
        return ch

    def emit_attn_a(self, block, xsrc, rdst, prev):
        nc, T = self.nc, self.T
        TT = T // 128
        TC = T // 512
        es = ExitStack()
        sbl = lambda n, sh, dt=F32: es.enter_context(nc.sbuf_tensor("aa" + n, sh, dt))
        psl = lambda n, sh, dt=F32: es.enter_context(nc.psum_tensor("aa" + n, sh, dt))
        d = self.dr

        wqt = sbl("wqt", [128, KD, D], BF16)
        wkt = sbl("wkt", [128, KD, D], BF16)
        wvt = sbl("wvt", [128, KD, D], BF16)
        wot = sbl("wot", [128, KD, D], BF16)
        bflat = sbl("bflat", [1, 4 * D], BF16)
        hst = sbl("hst", [128, KD, 128], BF16)
        qf = sbl("qf", [128, TT, D], BF16)
        kft = sbl("kft", [128, D], BF16)
        vft = sbl("vft", [128, D], BF16)
        etmp = sbl("etmp", [128, 2, 512])
        ebf = sbl("ebf", [128, 2, 512], BF16)
        kvp = sbl("kvp", [65, D])
        kvbf = sbl("kvbf", [128, D], BF16)
        ksbc = sbl("ksbc", [128, D], BF16)
        zt = sbl("zt", [128, D])
        zr = sbl("zr", [128, NH])
        qstg = sbl("qstg", [128, D], BF16)
        ztile = sbl("ztile", [128, KD, 128], BF16)
        zqc = sbl("zqc", [128, KD, 512], BF16)
        xc2 = sbl("xc2", [128, KD, 512])
        ybuf = sbl("ybuf", [128, KD, 512], BF16)
        zerob = sbl("zerob", [128, 512], BF16)

        es1 = ExitStack()
        pproj = es1.enter_context(nc.psum_tensor("aapp", [128, 2, 512], F32))
        pkv = es1.enter_context(nc.psum_tensor("aapkv", [64, D], F32))
        pksum = es1.enter_context(nc.psum_tensor("aapks", [1, D], F32))

        ch = self.chain("aach")
        qsy = self.chain("aaqsy")
        qg = self.chain("aaqg")
        ccs = self.chain("aacc")
        pn = prev.n
        hsd_r = d["hsd"].rearrange("k p t -> p k t")
        xsrc_r = xsrc.rearrange("(k p) t -> p k t", p=128)
        rdst_r = rdst.rearrange("(k p) t -> p k t", p=128)
        zq_r = d["zqfd"].rearrange("k p t -> p k t")

        @block.gpsimd
        def _(g):
            g.wait_ge(prev.h, pn)
            for nm, t_ in (("wq", wqt), ("wk", wkt), ("wv", wvt), ("wo", wot)):
                self.dstep(g, qg, ch, t_[:],
                           d[nm].rearrange("(k p) n -> p k n", p=128))
                ch.n += 1
            self.dstep(g, qg, ch, bflat[:], d["bqkvo"][:])
            ch.n += 1

        @block.vector
        def _(v):
            v.wait_ge(ch.h, ch.n)
            v.memset(zerob[:], 0.0)
            v.drain().then_inc(ch.h, 1)
        ch.n += 1

        @block.tensor
        def _(te):
            te.wait_ge(ch.h, ch.n)
            for hf in range(2):
                te.matmul(pkv[:, hf * 512:(hf + 1) * 512], zerob[:, 0:64],
                          zerob[:], start=True, stop=False)
            for hf in range(2):
                inst = te.matmul(pksum[:, hf * 512:(hf + 1) * 512], self.ones_c[:],
                                 zerob[:], start=True, stop=False)
            inst.then_inc(ch.h, 1)
        ch.n += 1

        base = ch.n
        DEL = 14

        @block.sync
        def _(sy):
            with sy.Fori(0, TT) as i:
                sy.wait_ge(ch.h, base + i * DEL)
                self.dstep(sy, qsy, ch, hst[:], hsd_r[:, :, bass.ts(i, 128)],
                           qv=16 * (i + 1))

        @block.tensor
        def _(te):
            with te.Fori(0, TT) as i:
                it = base + i * DEL
                for off, wt, boff in ((1, wqt, 0), (6, wkt, D), (11, wvt, 2 * D)):
                    te.wait_ge(ch.h, it + off)
                    for hf in range(2):
                        for k in range(KD):
                            te.matmul(pproj[:, hf], hst[:, k],
                                      wt[:, k, hf * 512:(hf + 1) * 512],
                                      start=(k == 0), stop=False)
                        inst = te.matmul(pproj[:, hf], self.ones_l[:, 0:128],
                                         bflat[:, boff + hf * 512:boff + (hf + 1) * 512],
                                         start=False, stop=True)
                    inst.then_inc(ch.h, 1)
                te.wait_ge(ch.h, it + 13)
                for h in range(NH):
                    te.matmul(pkv[:, h * 64:(h + 1) * 64], kft[:, h * 64:(h + 1) * 64],
                              vft[:, h * 64:(h + 1) * 64], start=False, stop=False)
                for hf in range(2):
                    inst = te.matmul(pksum[:, hf * 512:(hf + 1) * 512], self.ones_c[:],
                                     kft[:, hf * 512:(hf + 1) * 512],
                                     start=False, stop=False)
                inst.then_inc(ch.h, 1)

        @block.scalar
        def _(sc):
            with sc.Fori(0, TT) as i:
                it = base + i * DEL
                sc.wait_ge(ch.h, it + 2)
                for hf in range(2):
                    inst = sc.activation(qf[:, i, hf * 512:(hf + 1) * 512],
                                         pproj[:, hf], AF.Relu)
                sc.drain()
                inst.then_inc(ch.h, 1)
                sc.wait_ge(ch.h, it + 4)
                for hf in range(2):
                    inst = sc.activation(ebf[:, hf], etmp[:, hf], AF.Exp)
                sc.drain()
                inst.then_inc(ch.h, 1)
                sc.wait_ge(ch.h, it + 7)
                for hf in range(2):
                    inst = sc.activation(kft[:, hf * 512:(hf + 1) * 512],
                                         pproj[:, hf], AF.Relu)
                sc.drain()
                inst.then_inc(ch.h, 1)
                sc.wait_ge(ch.h, it + 9)
                for hf in range(2):
                    inst = sc.activation(ebf[:, hf], etmp[:, hf], AF.Exp)
                sc.drain()
                inst.then_inc(ch.h, 1)

        @block.vector
        def _(v):
            with v.Fori(0, TT) as i:
                it = base + i * DEL
                v.wait_ge(ch.h, it + 3)
                for hf in range(2):
                    inst = v.tensor_scalar(etmp[:, hf], pproj[:, hf], 0.0, None, ALU.min)
                v.drain()
                inst.then_inc(ch.h, 1)
                v.wait_ge(ch.h, it + 5)
                for hf in range(2):
                    inst = v.tensor_tensor(qf[:, i, hf * 512:(hf + 1) * 512],
                                           qf[:, i, hf * 512:(hf + 1) * 512],
                                           ebf[:, hf], ALU.add)
                v.drain()
                inst.then_inc(ch.h, 1)
                v.wait_ge(ch.h, it + 8)
                for hf in range(2):
                    inst = v.tensor_scalar(etmp[:, hf], pproj[:, hf], 0.0, None, ALU.min)
                v.drain()
                inst.then_inc(ch.h, 1)
                v.wait_ge(ch.h, it + 10)
                for hf in range(2):
                    inst = v.tensor_tensor(kft[:, hf * 512:(hf + 1) * 512],
                                           kft[:, hf * 512:(hf + 1) * 512],
                                           ebf[:, hf], ALU.add)
                v.drain()
                inst.then_inc(ch.h, 1)
                v.wait_ge(ch.h, it + 12)
                for hf in range(2):
                    inst = v.tensor_scalar(vft[:, hf * 512:(hf + 1) * 512],
                                           pproj[:, hf], 1.0, None, ALU.mult)
                v.drain()
                inst.then_inc(ch.h, 1)

        ch.n = base + TT * DEL
        qsy.n = 16 * TT

        @block.tensor
        def _(te):
            te.wait_ge(ch.h, ch.n)
            for hf in range(2):
                te.matmul(pkv[:, hf * 512:(hf + 1) * 512], zerob[:, 0:64],
                          zerob[:], start=False, stop=True)
            for hf in range(2):
                inst = te.matmul(pksum[:, hf * 512:(hf + 1) * 512], self.ones_c[:],
                                 zerob[:], start=False, stop=True)
            inst.then_inc(ch.h, 1)
        ch.n += 1

        @block.vector
        def _(v):
            v.wait_ge(ch.h, ch.n)
            v.tensor_scalar(kvp[0:64, :], pkv[:], 1.0, None, ALU.mult)
            inst = v.tensor_scalar(kvp[64:65, :], pksum[:], 1.0, None, ALU.mult)
            v.drain()
            inst.then_inc(ch.h, 1)
        ch.n += 1

        es1.close()
        pksb = psl("pksb", [128, 2, 512])
        ptr = psl("ptr", [128, KD, 128], BF16)
        pyt = psl("pyt", [128, 512])
        pout = psl("pout", [128, 512])

        n0 = ch.n

        @block.gpsimd
        def _(g):
            g.wait_ge(ch.h, n0)
            self.dstep(g, qg, ch, d["kvx_i"][:], kvp[:])
            g.collective_compute("AllReduce", ALU.add, self.pairs,
                                 [d["kvx_i"].opt()], [d["kvx_o"].opt()]
                                 ).then_inc(ccs.h, 1)
            g.wait_ge(ccs.h, 1)
            g.sem_inc(ch.h, 1)
            self.dstep(g, qg, ch, kvbf[0:64, :], d["kvx_o"][0:64, :])
            self.dstep(g, qg, ch, kvbf[64:128, :], d["kvx_o"][0:64, :])
            self.dstep(g, qg, ch, kvp[64:65, :], d["kvx_o"][64:65, :])
        ch.n += 5

        @block.vector
        def _(v):
            v.wait_ge(ch.h, ch.n)
            v.tensor_scalar(ebf[0:1, 0, :], kvp[64:65, 0:512], 1.0, None, ALU.mult)
            inst = v.tensor_scalar(ebf[0:1, 1, :], kvp[64:65, 512:1024], 1.0, None,
                                   ALU.mult)
            v.drain()
            inst.then_inc(ch.h, 1)
        ch.n += 1

        @block.tensor
        def _(te):
            te.wait_ge(ch.h, ch.n)
            te.matmul(pksb[:, 0], self.ones_l[:, 0:128], ebf[0:1, 0, :],
                      start=True, stop=True)
            inst = te.matmul(pksb[:, 1], self.ones_l[:, 0:128], ebf[0:1, 1, :],
                             start=True, stop=True)
            inst.then_inc(ch.h, 1)
        ch.n += 1

        @block.vector
        def _(v):
            v.wait_ge(ch.h, ch.n)
            v.tensor_scalar(ksbc[:, 0:512], pksb[:, 0], 1.0, None, ALU.mult)
            inst = v.tensor_scalar(ksbc[:, 512:1024], pksb[:, 1], 1.0, None, ALU.mult)
            v.drain()
            inst.then_inc(ch.h, 1)
        ch.n += 1

        basez = ch.n
        DZ = 2

        @block.vector
        def _(v):
            with v.Fori(0, TT) as i:
                it = basez + i * DZ
                v.wait_ge(ch.h, it)
                v.tensor_tensor(zt[:], qf[:, i, :], ksbc[:], ALU.mult)
                v.drain()
                inst = v.tensor_reduce(zr[:, :, None],
                                       zt[:].rearrange("p (h e) -> p h e", h=NH),
                                       mybir.AxisListType.X, ALU.add)
                v.drain()
                inst.then_inc(ch.h, 1)
                v.tensor_scalar(zr[:], zr[:], ATT_EPS, None, ALU.add)
                v.drain()
                v.reciprocal(zr[:], zr[:])
                v.drain()
                inst = v.tensor_tensor(qf[:, i, :].rearrange("p (h e) -> p h e", h=NH),
                                       qf[:, i, :].rearrange("p (h e) -> p h e", h=NH),
                                       zr[:, :, None].to_broadcast([128, NH, HDM]),
                                       ALU.mult)
                v.drain()
                inst.then_inc(ch.h, 1)

        ch.n = basez + TT * DZ

        baset = ch.n
        DT_ = 4

        @block.vector
        def _(v):
            with v.Fori(0, TT) as i:
                it = baset + i * DT_
                v.wait_ge(ch.h, it)
                inst = v.tensor_scalar(qstg[:], qf[:, i, :], 1.0, None, ALU.mult)
                v.drain()
                inst.then_inc(ch.h, 1)
                v.wait_ge(ch.h, it + 2)
                for k in range(KD):
                    inst = v.tensor_scalar(ztile[:, k], ptr[:, k], 1.0, None, ALU.mult)
                v.drain()
                inst.then_inc(ch.h, 1)

        @block.tensor
        def _(te):
            with te.Fori(0, TT) as i:
                it = baset + i * DT_
                te.wait_ge(ch.h, it + 1)
                for k in range(KD):
                    inst = te.transpose(ptr[:, k], qstg[:, k * 128:(k + 1) * 128],
                                        self.ident[:])
                inst.then_inc(ch.h, 1)

        @block.gpsimd
        def _(g, q0=qg.n):
            with g.Fori(0, TT) as i:
                it = baset + i * DT_
                g.wait_ge(ch.h, it + 3)
                self.dstep(g, qg, ch, zq_r[:, :, bass.ts(i, 128)], ztile[:],
                           qv=q0 + 16 * (i + 1))

        qg.n += 16 * TT
        ch.n = baset + TT * DT_

        basey = ch.n
        DY = 35

        @block.sync
        def _(sy, q0=qsy.n):
            with sy.Fori(0, TC) as i:
                sy.wait_ge(ch.h, basey + i * DY)
                self.dstep(sy, qsy, ch, zqc[:], zq_r[:, :, bass.ts(i, 512)],
                           qv=q0 + 16 * (2 * i + 1))
                self.dstep(sy, qsy, ch, xc2[:], xsrc_r[:, :, bass.ts(i, 512)],
                           qv=q0 + 16 * (2 * i + 2))

        @block.tensor
        def _(te):
            with te.Fori(0, TC) as i:
                it = basey + i * DY
                for hm in range(KD):
                    te.wait_ge(ch.h, it + 2 + 2 * hm)
                    for sub in range(2):
                        h = 2 * hm + sub
                        hb = (h % 2) * 64
                        inst = te.matmul(pyt[sub * 64:(sub + 1) * 64, :],
                                         kvbf[hb:hb + 64, h * 64:(h + 1) * 64],
                                         zqc[hb:hb + 64, h // 2, :],
                                         start=True, stop=True)
                    inst.then_inc(ch.h, 1)
                for dt in range(KD):
                    te.wait_ge(ch.h, it + 18 + 2 * dt)
                    for k in range(KD):
                        te.matmul(pout[:], wot[:, k, dt * 128:(dt + 1) * 128],
                                  ybuf[:, k, :], start=(k == 0), stop=False)
                    inst = te.matmul(pout[:],
                                     bflat[:, 3 * D + dt * 128:3 * D + dt * 128 + 128],
                                     self.ones_l[:], start=False, stop=True)
                    inst.then_inc(ch.h, 1)

        @block.vector
        def _(v):
            with v.Fori(0, TC) as i:
                it = basey + i * DY
                for hm in range(KD):
                    v.wait_ge(ch.h, it + 3 + 2 * hm)
                    inst = v.tensor_scalar(ybuf[:, hm, :], pyt[:], 1.0, None, ALU.mult)
                    v.drain()
                    inst.then_inc(ch.h, 1)
                for dt in range(KD):
                    v.wait_ge(ch.h, it + 19 + 2 * dt)
                    inst = v.tensor_tensor(xc2[:, dt], pout[:], xc2[:, dt], ALU.add)
                    v.drain()
                    inst.then_inc(ch.h, 1)

        @block.gpsimd
        def _(g, q0=qg.n):
            with g.Fori(0, TC) as i:
                it = basey + i * DY
                g.wait_ge(ch.h, it + 34)
                self.dstep(g, qg, ch, rdst_r[:, :, bass.ts(i, 512)], xc2[:],
                           qv=q0 + 16 * (i + 1))

        qsy.n += 32 * TC
        qg.n += 16 * TC
        ch.n = basey + TC * DY
        es.close()
        return ch

    def emit_ffn(self, block, src, dst, call, prev):
        nc, T = self.nc, self.T
        TH = T // 2
        THC = TH // 512
        tg = f"f{call}"
        es = ExitStack()
        sbl = lambda n, sh, dt=F32: es.enter_context(nc.sbuf_tensor(tg + n, sh, dt))
        psl = lambda n, sh, dt=F32: es.enter_context(nc.psum_tensor(tg + n, sh, dt))
        d = self.dr

        gwtt = sbl("gwt", [128, KD, 8], BF16)
        gbbc = sbl("gbbc", [128, 8])
        b1s = sbl("b1s", [1, HF], BF16)
        b3s = sbl("b3s", [1, HF], BF16)
        b2s = sbl("b2s", [1, D], BF16)
        xst = sbl("xst", [128, KD, TH], BF16)
        w1t = sbl("w1t", [128, KD, HF], BF16)
        w3t = sbl("w3t", [128, KD, HF], BF16)
        w2t = sbl("w2t", [128, KHF, D], BF16)
        yac = sbl("yac", [128, KD, TH])
        hbuf = sbl("hbuf", [128, KHF, 512], BF16)
        ubuf = sbl("ubuf", [128, 512], BF16)
        tbuf = sbl("tbuf", [128, 512], BF16)
        lgb = sbl("lgb", [8, 512], BF16)
        sfx = sbl("sfx", [128, 4, 40])
        rwt = sbl("rwt", [128, 4, NE], BF16)
        rwev = sbl("rwev", [NE, 512])
        rwre = sbl("rwre", [1, TH])
        rwrb = sbl("rwrb", [1, TH], BF16)
        rwbcb = sbl("rwbcb", [128, 512], BF16)
        tmpo = sbl("tmpo", [128, 512])
        rc = sbl("rc", [128, KD, 128])

        pg = psl("pg", [8, 512])
        ptt = psl("ptt", [128, 4, 8], BF16)
        prw = psl("prw", [NE, 512], BF16)
        ps1 = psl("ps1", [128, 512])
        ps2 = psl("ps2", [128, 512])
        pso = psl("pso", [128, 512])
        prb = psl("prb", [128, 512])

        ch = self.chain(tg + "ch")
        qsy = self.chain(tg + "qsy")
        qg = self.chain(tg + "qg")
        pn = prev.n
        src_r = src.rearrange("(k p) t -> p k t", p=128)
        dst_r = dst.rearrange("(k p) t -> p k t", p=128)
        hsd_r = d["hsd"].rearrange("k p t -> p k t")

        @block.gpsimd
        def _(g):
            g.wait_ge(prev.h, pn)
            if call == 0:
                g.wait_ge(self.wcast.h, self.wcast.n)
            self.dstep(g, qg, ch, gwtt[:],
                       d["gwt"].rearrange("(k p) e -> p k e", p=128))
            self.dstep(g, qg, ch, lgb[0:1, 0:8], d["gbr"][:])
        ch.n += 2

        @block.tensor
        def _(te):
            te.wait_ge(ch.h, ch.n)
            te.matmul(prb[:, 0:8], self.ones_l[:, 0:128], lgb[0:1, 0:8],
                      start=True, stop=True).then_inc(ch.h, 1)
        ch.n += 1

        @block.vector
        def _(v):
            v.wait_ge(ch.h, ch.n)
            v.tensor_scalar(gbbc[:], prb[:, 0:8], 1.0, None, ALU.mult)
            v.drain().then_inc(ch.h, 1)
        ch.n += 1

        for half in range(2):
            t0 = half * TH
            n0 = ch.n

            @block.gpsimd
            def _(g, t0=t0, n0=n0):
                g.wait_ge(ch.h, n0)
                self.dstep(g, qg, ch, xst[:], hsd_r[:, :, t0:t0 + TH])
            ch.n += 1

            n1 = ch.n

            @block.vector
            def _(v, n1=n1):
                v.wait_ge(ch.h, n1)
                for k in range(KD):
                    v.memset(yac[:, k], 0.0)
                v.drain()
                inst = v.memset(rwt[:, :, 8:9], 1.0)
                v.drain()
                inst.then_inc(ch.h, 1)
            ch.n += 1

            bg = ch.n
            DG = 9

            @block.tensor
            def _(te, bg=bg):
                with te.Fori(0, THC) as i:
                    it = bg + i * DG
                    te.wait_ge(ch.h, it)
                    for k in range(KD):
                        inst = te.matmul(pg[:], gwtt[:, k], xst[:, k, bass.ts(i, 512)],
                                         start=(k == 0), stop=(k == KD - 1))
                    inst.then_inc(ch.h, 1)
                    te.wait_ge(ch.h, it + 2)
                    for q in range(4):
                        inst = te.transpose(ptt[:, q], lgb[:, q * 128:(q + 1) * 128],
                                            self.ident[0:8, 0:8])
                    inst.then_inc(ch.h, 1)
                    te.wait_ge(ch.h, it + 6)
                    for q in range(4):
                        inst = te.transpose(prw[:, bass.ts(q, 128)], rwt[:, q, :],
                                            self.ident[:])
                    inst.then_inc(ch.h, 1)

            @block.scalar
            def _(sc, bg=bg):
                with sc.Fori(0, THC) as i:
                    it = bg + i * DG
                    sc.wait_ge(ch.h, it + 4)
                    for q in range(4):
                        inst = sc.activation(sfx[:, q, 2:10], sfx[:, q, 2:10], AF.Exp)
                    sc.drain()
                    inst.then_inc(ch.h, 1)

            @block.vector
            def _(v, bg=bg):
                with v.Fori(0, THC) as i:
                    it = bg + i * DG
                    v.wait_ge(ch.h, it + 1)
                    inst = v.tensor_scalar(lgb[:], pg[:], 1.0, None, ALU.mult)
                    v.drain()
                    inst.then_inc(ch.h, 1)
                    v.wait_ge(ch.h, it + 3)
                    for q in range(4):
                        v.tensor_reduce(sfx[:, q, 0:1], ptt[:, q],
                                        mybir.AxisListType.X, ALU.max)
                    v.drain()
                    for q in range(4):
                        inst = v.tensor_scalar(sfx[:, q, 2:10], ptt[:, q],
                                               sfx[:, q, 0:1], None, ALU.subtract)
                    v.drain()
                    inst.then_inc(ch.h, 1)
                    v.wait_ge(ch.h, it + 5)
                    for q in range(4):
                        v.tensor_reduce(sfx[:, q, 0:1], sfx[:, q, 2:10],
                                        mybir.AxisListType.X, ALU.add)
                    v.drain()
                    for q in range(4):
                        v.reciprocal(sfx[:, q, 0:1], sfx[:, q, 0:1])
                    v.drain()
                    for q in range(4):
                        v.tensor_scalar(sfx[:, q, 10:18], sfx[:, q, 2:10],
                                        sfx[:, q, 0:1], None, ALU.mult)
                    v.drain()
                    for q in range(4):
                        v.tensor_tensor(sfx[:, q, 18:26], sfx[:, q, 10:18],
                                        gbbc[:], ALU.add)
                    v.drain()
                    for q in range(4):
                        v.tensor_reduce(sfx[:, q, 1:2], sfx[:, q, 18:26],
                                        mybir.AxisListType.X, ALU.max)
                    v.drain()
                    for q in range(4):
                        v.tensor_scalar(sfx[:, q, 26:34], sfx[:, q, 18:26],
                                        sfx[:, q, 1:2], None, ALU.is_equal)
                    v.drain()
                    for q in range(4):
                        v.scalar_tensor_tensor(sfx[:, q, 26:34], sfx[:, q, 26:34],
                                               -1e9, sfx[:, q, 18:26],
                                               ALU.mult, ALU.add)
                    v.drain()
                    for q in range(4):
                        v.tensor_reduce(sfx[:, q, 1:2], sfx[:, q, 26:34],
                                        mybir.AxisListType.X, ALU.max)
                    v.drain()
                    for q in range(4):
                        v.tensor_scalar(sfx[:, q, 26:34], sfx[:, q, 18:26],
                                        sfx[:, q, 1:2], None, ALU.is_ge)
                    v.drain()
                    for q in range(4):
                        inst = v.tensor_tensor(rwt[:, q, 0:8], sfx[:, q, 10:18],
                                               sfx[:, q, 26:34], ALU.mult)
                    v.drain()
                    inst.then_inc(ch.h, 1)
                    v.wait_ge(ch.h, it + 7)
                    inst = v.tensor_scalar(rwev[:], prw[:], 1.0, None, ALU.mult)
                    v.drain()
                    inst.then_inc(ch.h, 1)

            @block.gpsimd
            def _(g, bg=bg, t0=t0, q0=qg.n):
                with g.Fori(0, THC) as i:
                    it = bg + i * DG
                    g.wait_ge(ch.h, it + 8)
                    self.dstep(g, qg, ch, d["rwd"][:, t0:t0 + TH][:, bass.ts(i, 512)],
                               rwev[:], qv=q0 + 16 * (i + 1))

            qg.n += 16 * THC
            ch.n = bg + THC * DG

            be = ch.n
            DTC = 2 + KHF * 5 + KD * 2
            DE = 8 + THC * DTC

            @block.gpsimd
            def _(g, be=be, t0=t0, q0=qg.n):
                with g.Fori(0, NE) as e:
                    it = be + e * DE
                    g.wait_ge(ch.h, it)
                    qv = q0 + e * 112
                    self.dstep(g, qg, ch, w1t[:],
                               d["w1b"].rearrange("e (k p) h -> e p k h", p=128)[e],
                               qv=qv + 16)
                    self.dstep(g, qg, ch, w3t[:],
                               d["w3b"].rearrange("e (k p) h -> e p k h", p=128)[e],
                               qv=qv + 32)
                    self.dstep(g, qg, ch, w2t[:],
                               d["w2b"].rearrange("e (k p) h -> e p k h", p=128)[e],
                               qv=qv + 48)
                    self.dstep(g, qg, ch, rwre[:], d["rwd"][e, t0:t0 + TH][None, :],
                               qv=qv + 64)
                    self.dstep(g, qg, ch, b1s[:], d["b1c"][:, bass.ts(e, HF)],
                               qv=qv + 80)
                    self.dstep(g, qg, ch, b3s[:], d["b3c"][:, bass.ts(e, HF)],
                               qv=qv + 96)
                    self.dstep(g, qg, ch, b2s[:], d["b2c"][:, bass.ts(e, D)],
                               qv=qv + 112)

            @block.vector
            def _(v, be=be):
                with v.Fori(0, NE) as e:
                    it = be + e * DE
                    v.wait_ge(ch.h, it + 7)
                    inst = v.tensor_scalar(rwrb[:], rwre[:], 1.0, None, ALU.mult)
                    v.drain()
                    inst.then_inc(ch.h, 1)
                    with v.Fori(0, THC) as i:
                        jt = it + 8 + i * DTC
                        v.wait_ge(ch.h, jt + 1)
                        inst = v.tensor_scalar(rwbcb[:], prb[:], 1.0, None, ALU.mult)
                        v.drain()
                        inst.then_inc(ch.h, 1)
                        for ht in range(KHF):
                            st = jt + 2 + ht * 5
                            v.wait_ge(ch.h, st + 3)
                            inst = v.tensor_tensor(tbuf[:], ps1[:], ubuf[:], ALU.mult)
                            v.drain()
                            inst.then_inc(ch.h, 1)
                        for dt in range(KD):
                            st = jt + 2 + KHF * 5 + dt * 2
                            v.wait_ge(ch.h, st + 1)
                            v.tensor_tensor(tmpo[:], pso[:], rwbcb[:], ALU.mult)
                            v.drain()
                            inst = v.tensor_tensor(yac[:, dt, bass.ts(i, 512)],
                                                   yac[:, dt, bass.ts(i, 512)],
                                                   tmpo[:], ALU.add)
                            v.drain()
                            inst.then_inc(ch.h, 1)

            @block.scalar
            def _(sc, be=be):
                with sc.Fori(0, NE) as e:
                    it = be + e * DE
                    with sc.Fori(0, THC) as i:
                        jt = it + 8 + i * DTC
                        for ht in range(KHF):
                            st = jt + 2 + ht * 5
                            sc.wait_ge(ch.h, st + 2)
                            inst = sc.activation(ubuf[:], ps2[:], AF.Copy)
                            sc.drain()
                            inst.then_inc(ch.h, 1)
                            sc.wait_ge(ch.h, st + 4)
                            inst = sc.activation(hbuf[:, ht], tbuf[:], AF.Silu)
                            sc.drain()
                            inst.then_inc(ch.h, 1)

            @block.tensor
            def _(te, be=be):
                with te.Fori(0, NE) as e:
                    it = be + e * DE
                    with te.Fori(0, THC) as i:
                        jt = it + 8 + i * DTC
                        te.wait_ge(ch.h, jt)
                        te.matmul(prb[:], self.ones_l[:, 0:128],
                                  rwrb[:, bass.ts(i, 512)],
                                  start=True, stop=True).then_inc(ch.h, 1)
                        for ht in range(KHF):
                            st = jt + 2 + ht * 5
                            te.wait_ge(ch.h, st)
                            for k in range(KD):
                                te.matmul(ps1[:], w1t[:, k, ht * 128:(ht + 1) * 128],
                                          xst[:, k, bass.ts(i, 512)],
                                          start=(k == 0), stop=False)
                            te.matmul(ps1[:], b1s[:, ht * 128:(ht + 1) * 128],
                                      self.ones_l[:], start=False, stop=True
                                      ).then_inc(ch.h, 1)
                            for k in range(KD):
                                te.matmul(ps2[:], w3t[:, k, ht * 128:(ht + 1) * 128],
                                          xst[:, k, bass.ts(i, 512)],
                                          start=(k == 0), stop=False)
                            te.matmul(ps2[:], b3s[:, ht * 128:(ht + 1) * 128],
                                      self.ones_l[:], start=False, stop=True
                                      ).then_inc(ch.h, 1)
                        for dt in range(KD):
                            st = jt + 2 + KHF * 5 + dt * 2
                            te.wait_ge(ch.h, st)
                            for k in range(KHF):
                                te.matmul(pso[:], w2t[:, k, dt * 128:(dt + 1) * 128],
                                          hbuf[:, k], start=(k == 0), stop=False)
                            te.matmul(pso[:], b2s[:, dt * 128:(dt + 1) * 128],
                                      self.ones_l[:], start=False, stop=True
                                      ).then_inc(ch.h, 1)

            qg.n += 112 * NE
            ch.n = be + NE * DE

            br = ch.n
            DR = 3
            NRC = TH // 128

            @block.sync
            def _(sy, br=br, t0=t0, q0=qsy.n):
                with sy.Fori(0, NRC) as i:
                    sy.wait_ge(ch.h, br + i * DR)
                    self.dstep(sy, qsy, ch,
                               rc[:], src_r[:, :, t0:t0 + TH][:, :, bass.ts(i, 128)],
                               qv=q0 + 16 * (i + 1))

            @block.vector
            def _(v, br=br):
                with v.Fori(0, NRC) as i:
                    it = br + i * DR
                    v.wait_ge(ch.h, it + 1)
                    for k in range(KD):
                        inst = v.tensor_tensor(rc[:, k],
                                               yac[:, k, bass.ts(i, 128)],
                                               rc[:, k], ALU.add)
                    v.drain()
                    inst.then_inc(ch.h, 1)

            @block.gpsimd
            def _(g, br=br, t0=t0, q0=qg.n):
                with g.Fori(0, NRC) as i:
                    it = br + i * DR
                    g.wait_ge(ch.h, it + 2)
                    self.dstep(g, qg, ch,
                               dst_r[:, :, t0:t0 + TH][:, :, bass.ts(i, 128)], rc[:],
                               qv=q0 + 16 * (i + 1))

            qsy.n += 16 * NRC
            qg.n += 16 * NRC
            ch.n = br + NRC * DR

        es.close()
        return ch

    def emit_attn_b(self, block, src, dst, prev):
        nc, T = self.nc, self.T
        NC = T // 128
        POI = 8192.0
        BIG = 30000.0
        es = ExitStack()
        sbl = lambda n, sh, dt=F32: es.enter_context(nc.sbuf_tensor("ab" + n, sh, dt))
        d = self.dr

        bwr = sbl("bwr", [1, 128])
        bwb = sbl("bwb", [1, 128], BF16)
        i4b = sbl("i4b", [1, 4], BF16)
        iota5c = sbl("iota5c", [5, 1])
        mcol = sbl("mcol", [16, 2])
        mcol2 = sbl("mcol2", [16, 1])
        metab = sbl("metab", [1, 2], BF16)
        swbB = sbl("swbB", [1, 16])
        e0sw = sbl("e0sw", [1, NH], BF16)
        V_tm = sbl("V_tm", [128, NC, D], BF16)
        validr = sbl("validr", [16, T], BF16)
        zq16 = sbl("zq16", [16, 1])
        csum = sbl("csum", [16, T])
        csvS = sbl("csvS", [16, T // 128])
        csum_reb = sbl("csumr", [16, T], BF16)
        csv_reb = sbl("csvr", [16, T], BF16)
        arow = sbl("arow", [16, T], BF16)
        Cfr = sbl("Cfr", [16, T], BF16)
        trr = sbl("trr", [16, T], BF16)
        Cbr = trr
        tmpI = sbl("tmpI", [16, NC * 4])
        tmpJ = sbl("tmpJ", [16, NC * 4], BF16)
        tmpJ2 = sbl("tmpJ2", [16, NC * 4], BF16)
        cntst4f = sbl("cst4", [1, 16 * NC * 4], BF16)
        cntch4f = sbl("cch4", [1, 16 * NC * 4], BF16)
        Erow = sbl("Erow", [1, NC * 16], BF16)
        Cbs = sbl("Cbs", [5, NC * 16], BF16)
        Ccol_f = sbl("Ccf", [128, NC * 16])
        Ccol_b = sbl("Ccb", [128, NC * 16])
        r1t = sbl("r1t", [2, D], BF16)
        r1tb = sbl("r1tb", [1, D], BF16)
        ownl = sbl("ownl", [1, D], BF16)
        vshfix = sbl("vshf", [1, D], BF16)

        esA = ExitStack()
        pA = esA.enter_context(nc.psum_tensor("abpA", [128, 2048], F32))

        ch = self.chain("abch")
        qsy = self.chain("abqsy")
        qg = self.chain("abqg")
        ccs = self.chain("abcc")
        pn = prev.n
        hsd_r = d["hsd"].rearrange("k p t -> p k t")
        src_r = src.rearrange("(k p) t -> p k t", p=128)
        dst_r = dst.rearrange("(k p) t -> p k t", p=128)

        esP = ExitStack()
        pT = esP.enter_context(nc.psum_tensor("abpT", [128, 1024], BF16))
        sblP = lambda n, sh, dt=F32: esP.enter_context(nc.sbuf_tensor("ab" + n, sh, dt))
        bwkt = sblP("bwkt", [128, KD, D], BF16)
        bwvt = sblP("bwvt", [128, KD, D], BF16)
        bbf = sblP("bbf", [1, 2 * D], BF16)
        rhr = sblP("rhr", [1, D], BF16)
        rhbc = sblP("rhbc", [128, D], BF16)
        hst2 = sblP("hst2", [128, KD, 128], BF16)
        k1t = sblP("k1t", [128, D], BF16)
        kdt = sblP("kdt", [128, 512])
        dcol = sblP("dcol", [128, NH])
        vt_tm = sblP("vt_tm", [128, NH], BF16)
        @block.gpsimd
        def _(g):
            g.wait_ge(prev.h, pn)
            self.dstep(g, qg, ch, bwkt[:], d["bwk"].rearrange("(k p) n -> p k n", p=128))
            self.dstep(g, qg, ch, bwvt[:], d["bwv"].rearrange("(k p) n -> p k n", p=128))
            self.dstep(g, qg, ch, bbf[:], d["bbkv"][:])
            self.dstep(g, qg, ch, rhr[:], d["rhw"][:])
            self.dstep(g, qg, ch, bwr[:], d["bww"][:])
            g.iota(tmpI[:], pattern=[[0, NC], [1, 4]], base=0, channel_multiplier=0,
                   allow_small_or_imprecise_dtypes=True)
            g.iota(iota5c[:], pattern=[[0, 1]], base=0, channel_multiplier=1,
                   allow_small_or_imprecise_dtypes=True).then_inc(ch.h, 1)
        ch.n += 6

        @block.vector
        def _(v):
            v.wait_ge(ch.h, ch.n)
            v.tensor_scalar(metab[:], self.metat[:, 0:2], 1.0, None, ALU.mult)
            v.tensor_scalar(bwb[:], bwr[:], 1.0, None, ALU.mult)
            v.memset(i4b[:], 0.0)
            v.memset(zq16[:], 0.0)
            v.memset(e0sw[:], 0.0)
            v.drain()
            for j in range(4):
                v.memset(i4b[:, j:j + 1], float(j))
            inst = v.tensor_reduce(swbB[:, :, None],
                                   bwr[:, 64:128].rearrange("o (r c) -> o c r", c=16),
                                   mybir.AxisListType.X, ALU.add)
            v.drain()
            inst.then_inc(ch.h, 1)
        ch.n += 1

        @block.tensor
        def _(te):
            te.wait_ge(ch.h, ch.n)
            te.matmul(pA[:, 0:512], self.ones_l[:, 0:128], rhr[:, 0:512],
                      start=True, stop=True)
            te.matmul(pA[:, 512:1024], self.ones_l[:, 0:128], rhr[:, 512:1024],
                      start=True, stop=True)
            inst = te.matmul(pA[0:16, 1024:1026], self.ones_l[:, 0:16], metab[:],
                             start=True, stop=True)
            inst.then_inc(ch.h, 1)
        ch.n += 1

        @block.vector
        def _(v):
            v.wait_ge(ch.h, ch.n)
            v.tensor_scalar(rhbc[:, 0:512], pA[:, 0:512], 1.0, None, ALU.mult)
            v.tensor_scalar(rhbc[:, 512:1024], pA[:, 512:1024], 1.0, None, ALU.mult)
            v.tensor_scalar(mcol[:], pA[0:16, 1024:1026], 1.0, None, ALU.mult)
            v.drain()
            v.tensor_scalar(mcol2[:], mcol[:, 1:2], BIG, None, ALU.mult)
            v.drain()
            v.tensor_scalar(e0sw[:], swbB[:], self.metat[:, 1:2], None, ALU.mult)
            v.drain()
            v.sem_inc(ch.h, 1)
        ch.n += 1


        bp = ch.n
        DP = 9

        @block.sync
        def _(sy, q0=qsy.n):
            with sy.Fori(0, NC) as i:
                sy.wait_ge(ch.h, bp + i * DP)
                self.dstep(sy, qsy, ch, hst2[:], hsd_r[:, :, bass.ts(i, 128)],
                           qv=q0 + 16 * (i + 1))

        @block.tensor
        def _(te):
            with te.Fori(0, NC) as i:
                it = bp + i * DP
                te.wait_ge(ch.h, it + 1)
                for hf in range(2):
                    for k in range(KD):
                        te.matmul(pA[:, hf * 512:(hf + 1) * 512], hst2[:, k],
                                  bwvt[:, k, hf * 512:(hf + 1) * 512],
                                  start=(k == 0), stop=False)
                    inst = te.matmul(pA[:, hf * 512:(hf + 1) * 512],
                                     self.ones_l[:, 0:128],
                                     bbf[:, D + hf * 512:D + (hf + 1) * 512],
                                     start=False, stop=True)
                inst.then_inc(ch.h, 1)
                te.wait_ge(ch.h, it + 3)
                for hf in range(2):
                    for k in range(KD):
                        te.matmul(pA[:, hf * 512:(hf + 1) * 512], hst2[:, k],
                                  bwkt[:, k, hf * 512:(hf + 1) * 512],
                                  start=(k == 0), stop=False)
                    inst = te.matmul(pA[:, hf * 512:(hf + 1) * 512],
                                     self.ones_l[:, 0:128],
                                     bbf[:, hf * 512:(hf + 1) * 512],
                                     start=False, stop=True)
                inst.then_inc(ch.h, 1)
                te.wait_ge(ch.h, it + 7)
                inst = te.transpose(pT[0:16, 0:128], vt_tm[:], self.ident[:])
                inst.then_inc(ch.h, 1)

        @block.scalar
        def _(sc):
            with sc.Fori(0, NC) as i:
                it = bp + i * DP
                sc.wait_ge(ch.h, it + 2)
                for hf in range(2):
                    inst = sc.activation(V_tm[:, i, hf * 512:(hf + 1) * 512],
                                         pA[:, hf * 512:(hf + 1) * 512], AF.Relu)
                sc.drain()
                inst.then_inc(ch.h, 1)
                sc.wait_ge(ch.h, it + 4)
                for hf in range(2):
                    inst = sc.activation(k1t[:, hf * 512:(hf + 1) * 512],
                                         pA[:, hf * 512:(hf + 1) * 512], AF.Relu)
                sc.drain()
                inst.then_inc(ch.h, 1)

        @block.vector
        def _(v):
            with v.Fori(0, NC) as i:
                it = bp + i * DP
                v.wait_ge(ch.h, it + 5)
                inst = v.tensor_tensor(kdt[:], k1t[:, 0:512], rhbc[:, 0:512], ALU.mult)
                v.drain()
                inst = v.tensor_reduce(dcol[:, 0:8][:, :, None],
                                       kdt[:].rearrange("p (h e) -> p h e", h=8),
                                       mybir.AxisListType.X, ALU.add)
                v.drain()
                inst = v.tensor_tensor(kdt[:], k1t[:, 512:1024], rhbc[:, 512:1024], ALU.mult)
                v.drain()
                inst.then_inc(ch.h, 1)
                inst = v.tensor_reduce(dcol[:, 8:16][:, :, None],
                                       kdt[:].rearrange("p (h e) -> p h e", h=8),
                                       mybir.AxisListType.X, ALU.add)
                v.drain()
                inst = v.tensor_scalar(vt_tm[:], dcol[:], 0.5, None, ALU.is_gt)
                v.drain()
                inst.then_inc(ch.h, 1)
                v.wait_ge(ch.h, it + 8)
                inst = v.tensor_scalar(validr[:, bass.ts(i, 128)], pT[0:16, 0:128],
                                       1.0, None, ALU.mult)
                v.drain()
                inst.then_inc(ch.h, 1)

        qsy.n += 16 * NC
        ch.n = bp + NC * DP

        n0 = ch.n

        @block.gpsimd
        def _(g, n0=n0):
            g.wait_ge(ch.h, n0)
            self.dstep(g, qg, ch, d["r1x_i"][:], V_tm[0:1, 0, :])
            g.collective_compute("AllGather", ALU.bypass, self.pairs,
                                 [d["r1x_i"].opt()], [d["r1x_o"].opt()]
                                 ).then_inc(ccs.h, 1)
            g.wait_ge(ccs.h, 1)
            g.sem_inc(ch.h, 1)
            self.dstep(g, qg, ch, r1t[:], d["r1x_o"][:])
            self.dstep(g, qg, ch, r1tb[:], d["r1x_o"][1:2, :])
            self.dstep(g, qg, ch, ownl[:], V_tm[127:128, NC - 1, :])
        ch.n += 5

        @block.vector
        def _(v):
            v.wait_ge(ch.h, ch.n)
            v.tensor_scalar(vshfix[:], r1tb[:], self.metat[:, 1:2], None, ALU.mult)
            v.tensor_scalar(ownl[:], ownl[:], self.metat[:, 0:1], None, ALU.mult)
            v.drain()
            v.tensor_tensor(vshfix[:], vshfix[:], ownl[:], ALU.add)
            v.drain()
            inst = v.tensor_scalar(validr[:, 0:1], validr[:, 0:1],
                                   mcol[:, 0:1], None, ALU.mult)
            v.drain()
            inst.then_inc(ch.h, 1)
            v.tensor_tensor_scan(csum[:], validr[:],
                                 zq16[:].to_broadcast([16, T]), 0.0,
                                 ALU.add, ALU.add)
            v.drain()
            v.tensor_tensor(csvS[:], csum[:, 0:T:128], validr[:, 0:T:128],
                            ALU.subtract)
            v.drain()
            v.tensor_tensor(csum_reb[:].rearrange("p (c u) -> p c u", u=128),
                            csum[:].rearrange("p (c u) -> p c u", u=128),
                            csvS[:, :, None].to_broadcast([16, NC, 128]),
                            ALU.subtract)
            v.drain()
            v.tensor_tensor(csv_reb[:], csum_reb[:], validr[:], ALU.subtract)
            v.drain()
            v.tensor_scalar(arow[:], validr[:], -POI, POI, ALU.mult, ALU.add)
            v.drain()
            v.tensor_tensor(Cfr[:], csum_reb[:], arow[:], ALU.add)
            v.tensor_scalar(trr[:], csum_reb[:], 1.0, None, ALU.subtract)
            v.drain()
            v.tensor_tensor(trr[:], trr[:], arow[:], ALU.subtract)
            v.drain()
            v.tensor_scalar(csv_reb[:, 0:1], csv_reb[:, 0:1], mcol2[:, 0:1],
                            None, ALU.add)
            v.drain()
            v.tensor_tensor(tmpJ[:].rearrange("p (c j) -> p c j", j=4),
                            csum_reb[:, 127:T:128][:, :, None].to_broadcast([16, NC, 4]),
                            tmpI[:].rearrange("p (c j) -> p c j", j=4),
                            ALU.subtract)
            v.drain()
            inst = v.tensor_scalar(tmpJ2[:], tmpJ[:], -1.0, None, ALU.mult)
            v.drain()
            inst.then_inc(ch.h, 1)
        ch.n += 2

        @block.gpsimd
        def _(g):
            g.wait_ge(ch.h, ch.n)
            self.dstep(g, qg, ch, d["csrd"][0], csum_reb[:])
            self.dstep(g, qg, ch, d["csrd"][1], csv_reb[:])
            self.dstep(g, qg, ch, d["flA"][:], tmpJ[:])
            self.dstep(g, qg, ch, d["flB"][:], tmpJ2[:])
            g.dma_start(out=d["flE"].rearrange("o (c ch) -> o c ch", ch=16),
                        in_=d["csrd"][0].rearrange("ch (c u) -> u c ch", u=128)[127:128, :, :],
                        allow_slow_non_contiguous=True).then_inc(qg.h, 16)
            qg.n += 16
            g.wait_ge(qg.h, qg.n)
            g.sem_inc(ch.h, 1)
            self.dstep(g, qg, ch, cntst4f[:],
                       d["flA"].rearrange("a b -> (a b)")[None, :])
            self.dstep(g, qg, ch, cntch4f[:],
                       d["flB"].rearrange("a b -> (a b)")[None, :])
            self.dstep(g, qg, ch, Erow[:], d["flE"][:])
        ch.n += 8

        @block.tensor
        def _(te):
            te.wait_ge(ch.h, ch.n)
            inst = te.matmul(pA[0:5, 0:NC * 16], self.ones_l[:, 0:5], Erow[:],
                             start=True, stop=True)
            inst.then_inc(ch.h, 1)
        ch.n += 1

        @block.vector
        def _(v):
            v.wait_ge(ch.h, ch.n)
            inst = v.tensor_scalar(Cbs[:], pA[0:5, 0:NC * 16], iota5c[:], None,
                                   ALU.add)
            v.drain()
            inst.then_inc(ch.h, 1)
        ch.n += 1

        cstg = sblP("cstg", [16, 128], BF16)
        bc = ch.n
        DC = 6

        @block.tensor
        def _(te):
            with te.Fori(0, NC) as i:
                it = bc + i * DC
                te.wait_ge(ch.h, it + 1)
                te.transpose(pT[:, 0:16], cstg[:],
                             self.ident[0:16, 0:16]).then_inc(ch.h, 1)
                te.wait_ge(ch.h, it + 4)
                te.transpose(pT[:, 16:32], cstg[:],
                             self.ident[0:16, 0:16]).then_inc(ch.h, 1)

        @block.vector
        def _(v):
            with v.Fori(0, NC) as i:
                it = bc + i * DC
                v.wait_ge(ch.h, it)
                inst = v.tensor_scalar(cstg[:], Cfr[:, bass.ts(i, 128)],
                                       1.0, None, ALU.mult)
                v.drain()
                inst.then_inc(ch.h, 1)
                v.wait_ge(ch.h, it + 2)
                inst = v.tensor_scalar(Ccol_f[:, bass.ts(i, 16)], pT[:, 0:16],
                                       1.0, None, ALU.mult)
                v.drain()
                inst.then_inc(ch.h, 1)
                inst = v.tensor_scalar(cstg[:], Cbr[:, bass.ts(i, 128)],
                                       1.0, None, ALU.mult)
                v.drain()
                inst.then_inc(ch.h, 1)
                v.wait_ge(ch.h, it + 5)
                inst = v.tensor_scalar(Ccol_b[:, bass.ts(i, 16)], pT[:, 16:32],
                                       1.0, None, ALU.mult)
                v.drain()
                inst.then_inc(ch.h, 1)

        ch.n = bc + NC * DC
        esP.close()

        Scur_f = sbl("Scf", [5, NH * HDM], BF16)
        Scur_b = sbl("Scb", [5, NH * HDM], BF16)
        msf = sbl("msf", [128, NH * 4], BF16)
        mss = sbl("mss", [5, NH * 4], BF16)
        Vsb = sbl("Vsb", [128, D], BF16)
        Vtmp = sbl("Vtmp", [128, D], BF16)
        Vsf = sbl("Vsf", [128, D], BF16)
        sfs = sbl("sfs", [5, NH * HDM], BF16)
        sbs = sbl("sbs", [5, NH * HDM], BF16)
        stage5f = sbl("st5f", [5, 16 * 128], BF16)
        stage5b = sbl("st5b", [5, 16 * 128], BF16)
        Dlf = sbl("Dlf", [128, 2048], BF16)
        Mf = sbl("Mf", [128, 2048], BF16)
        Dfs = sbl("Dfs", [5, 2048], BF16)
        Mfs = sbl("Mfs", [5, 2048], BF16)
        tmp5 = sbl("tmp5", [5, 2048], BF16)
        wex = [sbl(f"wex{k}", [128, 16], BF16) for k in range(8)]
        zg = sbl("zg", [128, 512], BF16)
        l0f = sbl("l0f", [4, D], BF16)
        l1b = sbl("l1b", [4, D], BF16)
        rstg = sbl("rstg", [128, KD, 128])
        padrep = sbl("padrep", [4, D], BF16)

        for k in range(8):
            nk = ch.n

            @block.tensor
            def _(te, k=k, nk=nk):
                te.wait_ge(ch.h, nk)
                inst = te.matmul(pA[:, 0:16], self.ones_l[:, 0:128],
                                 bwb[:, k * 16:(k + 1) * 16],
                                 start=True, stop=True)
                inst.then_inc(ch.h, 1)
            ch.n += 1
            nk = ch.n

            @block.vector
            def _(v, k=k, nk=nk):
                v.wait_ge(ch.h, nk)
                inst = v.tensor_scalar(wex[k][:], pA[:, 0:16], 1.0, None, ALU.mult)
                v.drain()
                inst.then_inc(ch.h, 1)
            ch.n += 1

        esC = ExitStack()
        psS = esC.enter_context(nc.psum_tensor("abpsS", [4, 1024], F32))

        n0 = ch.n

        @block.gpsimd
        def _(g, n0=n0):
            g.wait_ge(ch.h, n0)
            self.dstep(g, qg, ch, padrep[:, None, :],
                       d["r1x_o"][None, 0:1, :].to_broadcast([4, 1, D]))
            self.dstep(g, qg, ch, Scur_f[:, None, :],
                       d["r1x_o"][None, 0:1, :].to_broadcast([5, 1, D]))
            self.dstep(g, qg, ch, Scur_b[:, None, :],
                       d["r1x_o"][None, 0:1, :].to_broadcast([5, 1, D]))
        ch.n += 3

        def emit_chain(fwd):
            b0 = ch.n
            DCH = 10 if not fwd else 7
            Scur = Scur_f if fwd else Scur_b
            dirIdx = 0 if fwd else 1
            Ccol = Ccol_f if fwd else Ccol_b
            Vget = V_tm if fwd else Vsb

            @block.gpsimd
            def _(g, b0=b0, q0=qg.n):
                with g.Fori(0, NC) as c:
                    it = b0 + c * DCH
                    cc = nc.s_assert_le(nc.s_assert_ge(NC - 1 - c, 0), NC - 1) if not fwd else c
                    g.wait_ge(ch.h, it)
                    npd = 4 if not fwd else 1
                    self.dstep(g, qg, ch, d["sind"][dirIdx, cc], Scur[:],
                               qv=q0 + 16 * (npd * c + 1))
                    if not fwd:
                        self.dstep(g, qg, ch, Vtmp[:], V_tm[:, cc, :],
                                   qv=q0 + 16 * (npd * c + 2))
                        self.dstep(g, qg, ch, Vsb[0:127, :], Vtmp[1:128, :],
                                   qv=q0 + 16 * (npd * c + 3))
                        with g.If(cc < NC - 1):
                            ccp = nc.s_assert_le(cc + 1, NC - 1)
                            g.dma_start(out=Vsb[127:128, :], in_=V_tm[0:1, ccp, :]
                                        ).then_inc(qg.h, 16)
                        with g.Else():
                            g.dma_start(out=Vsb[127:128, :], in_=vshfix[:]
                                        ).then_inc(qg.h, 16)
                        g.wait_ge(qg.h, q0 + 16 * (npd * c + 4))
                        g.sem_inc(ch.h, 1)

            of = 1 if fwd else 4

            @block.tensor
            def _(te, b0=b0, of=of):
                with te.Fori(0, NC) as c:
                    it = b0 + c * DCH
                    cc = nc.s_assert_le(nc.s_assert_ge(NC - 1 - c, 0), NC - 1) if not fwd else c
                    te.wait_ge(ch.h, it + of)
                    if fwd:
                        for c16 in range(16):
                            inst = te.matmul(pA[:, c16 * 4:(c16 + 1) * 4],
                                             self.ones_l[:, 0:128],
                                             cntst4f[:, bass.ts(c16 * NC + cc, 4)],
                                             start=(c16 == 0), stop=(c16 == 15))
                    else:
                        inst = te.matmul(pA[:, 0:4], self.ones_l[:, 0:128],
                                         i4b[:], start=True, stop=True)
                    inst.then_inc(ch.h, 1)
                    te.wait_ge(ch.h, it + of + 2)
                    for c16 in range(16):
                        inst = te.matmul(pA[0:5, 512 + c16 * 4:512 + (c16 + 1) * 4],
                                         self.ones_l[:, 0:5],
                                         cntch4f[:, bass.ts(c16 * NC + cc, 4)],
                                         start=(c16 == 0), stop=(c16 == 15))
                    inst.then_inc(ch.h, 1)
                    te.wait_ge(ch.h, it + of + 4)
                    for c16 in range(16):
                        te.matmul(psS[:, c16 * 64:(c16 + 1) * 64],
                                  msf[:, c16 * 4:(c16 + 1) * 4][:, 0:4] if False
                                  else msf[:, c16 * 4:(c16 + 1) * 4],
                                  Vget[:, cc, c16 * 64:(c16 + 1) * 64] if fwd
                                  else Vsb[:, c16 * 64:(c16 + 1) * 64],
                                  start=(c16 == 0 or c16 == 8), stop=False)
                        inst = te.matmul(psS[:, c16 * 64:(c16 + 1) * 64],
                                         mss[:, c16 * 4:(c16 + 1) * 4],
                                         Scur[:, c16 * 64:(c16 + 1) * 64],
                                         start=False,
                                         stop=(c16 == 7 or c16 == 15))
                    inst.then_inc(ch.h, 1)

            @block.vector
            def _(v, b0=b0, of=of):
                with v.Fori(0, NC) as c:
                    it = b0 + c * DCH
                    cc = nc.s_assert_le(nc.s_assert_ge(NC - 1 - c, 0), NC - 1) if not fwd else c
                    v.wait_ge(ch.h, it + of + 1)
                    if fwd:
                        for c16 in range(16):
                            inst = v.tensor_scalar(
                                msf[:, c16 * 4:(c16 + 1) * 4],
                                pA[:, c16 * 4:(c16 + 1) * 4],
                                Ccol[:, cc * 16 + c16][:, None], 0.0,
                                ALU.subtract, ALU.is_equal)
                    else:
                        for c16 in range(16):
                            inst = v.tensor_scalar(
                                msf[:, c16 * 4:(c16 + 1) * 4],
                                pA[:, 0:4],
                                Ccol[:, cc * 16 + c16][:, None], 0.0,
                                ALU.subtract, ALU.is_equal)
                    v.drain()
                    inst.then_inc(ch.h, 1)
                    v.wait_ge(ch.h, it + of + 3)
                    for c16 in range(16):
                        inst = v.tensor_scalar(
                            mss[:, c16 * 4:(c16 + 1) * 4],
                            pA[0:5, 512 + c16 * 4:512 + (c16 + 1) * 4],
                            iota5c[:], 0.0, ALU.subtract, ALU.is_equal)
                    v.drain()
                    inst.then_inc(ch.h, 1)
                    v.wait_ge(ch.h, it + of + 5)
                    inst = v.tensor_scalar(Scur[0:4, :], psS[:], 1.0, None, ALU.mult)
                    v.drain()
                    inst.then_inc(ch.h, 1)

            if fwd:
                qg.n += 16 * NC
            else:
                qg.n += 64 * NC
            ch.n = b0 + NC * DCH

        emit_chain(True)
        emit_chain(False)

        n0 = ch.n

        @block.gpsimd
        def _(g, n0=n0):
            g.wait_ge(ch.h, n0)
            self.dstep(g, qg, ch, d["r2f_i"][:], Scur_f[0:4, :])
            self.dstep(g, qg, ch, d["r2b_i"][:], Scur_b[0:4, :])
            g.collective_compute("AllGather", ALU.bypass, self.pairs,
                                 [d["r2f_i"].opt()], [d["r2f_o"].opt()]
                                 ).then_inc(ccs.h, 1)
            g.wait_ge(ccs.h, 2)
            g.sem_inc(ch.h, 1)
            g.collective_compute("AllGather", ALU.bypass, self.pairs,
                                 [d["r2b_i"].opt()], [d["r2b_o"].opt()]
                                 ).then_inc(ccs.h, 1)
            g.wait_ge(ccs.h, 3)
            g.sem_inc(ch.h, 1)
            self.dstep(g, qg, ch, l0f[:], d["r2f_o"][0:4, :])
            self.dstep(g, qg, ch, l1b[:], d["r2b_o"][4:8, :])
        ch.n += 6

        @block.vector
        def _(v):
            v.wait_ge(ch.h, ch.n)
            v.tensor_scalar(l0f[:], l0f[:], mcol[0:4, 0:1], None, ALU.mult)
            v.tensor_scalar(Vsf[0:4, 0:D], padrep[:], mcol[0:4, 1:2], None, ALU.mult)
            v.drain()
            v.tensor_tensor(Scur_f[0:4, :], l0f[:], Vsf[0:4, 0:D], ALU.add)
            v.drain()
            v.tensor_scalar(l1b[:], l1b[:], mcol[0:4, 1:2], None, ALU.mult)
            v.tensor_scalar(Vsf[0:4, 0:D], padrep[:], mcol[0:4, 0:1], None, ALU.mult)
            v.drain()
            inst = v.tensor_tensor(Scur_b[0:4, :], l1b[:], Vsf[0:4, 0:D], ALU.add)
            v.drain()
            inst.then_inc(ch.h, 1)
        ch.n += 1

        emit_chain(True)
        emit_chain(False)
        esC.close()

        esG = ExitStack()
        pO = esG.enter_context(nc.psum_tensor("abpO", [64, 2048], F32))

        n0 = ch.n

        @block.vector
        def _(v, n0=n0):
            v.wait_ge(ch.h, n0)
            v.memset(zg[:], 0.0)
            v.drain()
            v.sem_inc(ch.h, 1)
        ch.n += 1

        bgt = ch.n
        DGT = 28

        @block.vector
        def _(v):
            with v.Fori(0, NC) as c:
                it = bgt + c * DGT
                v.wait_ge(ch.h, it)
                inst = v.tensor_scalar(Vsf[:], V_tm[:, c, :], 1.0, None, ALU.mult)
                v.drain()
                inst.then_inc(ch.h, 1)
                v.wait_ge(ch.h, it + 9)
                inst = v.tensor_tensor(
                    Dlf[:].rearrange("p (a b) -> p a b", b=128),
                    pA[:].rearrange("p (a b) -> p a b", b=128),
                    Ccol_f[:, bass.ts(c, 16)][:, :, None].to_broadcast([128, 16, 128]),
                    ALU.subtract)
                v.drain()
                inst.then_inc(ch.h, 1)
                v.wait_ge(ch.h, it + 10)
                inst = v.scalar_tensor_tensor(
                    Mf[:].rearrange("p (a b) -> p a b", b=128),
                    Dlf[:].rearrange("p (a b) -> p a b", b=128), 0.0,
                    wex[0][:, :, None].to_broadcast([128, 16, 128]),
                    ALU.is_equal, ALU.mult)
                v.drain()
                inst.then_inc(ch.h, 1)
                for k in range(1, 4):
                    v.wait_ge(ch.h, it + 10 + k)
                    v.scalar_tensor_tensor(
                        pA[:].rearrange("p (a b) -> p a b", b=128),
                        Dlf[:].rearrange("p (a b) -> p a b", b=128), float(k),
                        wex[k][:, :, None].to_broadcast([128, 16, 128]),
                        ALU.is_equal, ALU.mult)
                    v.drain()
                    inst = v.tensor_tensor(Mf[:], Mf[:], pA[:], ALU.add)
                    v.drain()
                    inst.then_inc(ch.h, 1)
                v.wait_ge(ch.h, it + 14)
                v.tensor_scalar(Dfs[:], stage5f[:], iota5c[:], None, ALU.add)
                v.drain()
                v.scalar_tensor_tensor(
                    Mfs[:].rearrange("p (a b) -> p a b", b=128),
                    Dfs[:].rearrange("p (a b) -> p a b", b=128), 0.0,
                    wex[0][0:5, :, None].to_broadcast([5, 16, 128]),
                    ALU.is_equal, ALU.mult)
                v.drain()
                for k in range(1, 4):
                    v.scalar_tensor_tensor(
                        tmp5[:].rearrange("p (a b) -> p a b", b=128),
                        Dfs[:].rearrange("p (a b) -> p a b", b=128), float(k),
                        wex[k][0:5, :, None].to_broadcast([5, 16, 128]),
                        ALU.is_equal, ALU.mult)
                    v.drain()
                    v.tensor_tensor(Mfs[:], Mfs[:], tmp5[:], ALU.add)
                    v.drain()
                v.sem_inc(ch.h, 1)
                v.wait_ge(ch.h, it + 18)
                inst = v.tensor_tensor(
                    Dlf[:].rearrange("p (a b) -> p a b", b=128),
                    Ccol_b[:, bass.ts(c, 16)][:, :, None].to_broadcast([128, 16, 128]),
                    pA[:].rearrange("p (a b) -> p a b", b=128),
                    ALU.subtract)
                v.drain()
                inst.then_inc(ch.h, 1)
                v.wait_ge(ch.h, it + 19)
                inst = v.scalar_tensor_tensor(
                    Mf[:].rearrange("p (a b) -> p a b", b=128),
                    Dlf[:].rearrange("p (a b) -> p a b", b=128), 0.0,
                    wex[4][:, :, None].to_broadcast([128, 16, 128]),
                    ALU.is_equal, ALU.mult)
                v.drain()
                inst.then_inc(ch.h, 1)
                for k in range(1, 4):
                    v.wait_ge(ch.h, it + 19 + k)
                    v.scalar_tensor_tensor(
                        pA[:].rearrange("p (a b) -> p a b", b=128),
                        Dlf[:].rearrange("p (a b) -> p a b", b=128), float(k),
                        wex[4 + k][:, :, None].to_broadcast([128, 16, 128]),
                        ALU.is_equal, ALU.mult)
                    v.drain()
                    inst = v.tensor_tensor(Mf[:], Mf[:], pA[:], ALU.add)
                    v.drain()
                    inst.then_inc(ch.h, 1)
                v.wait_ge(ch.h, it + 23)
                v.tensor_tensor(Dfs[:].rearrange("p (a b) -> p a b", b=128),
                                Cbs[:, bass.ts(c, 16)][:, :, None]
                                .to_broadcast([5, 16, 128]),
                                stage5b[:].rearrange("p (a b) -> p a b", b=128),
                                ALU.subtract)
                v.drain()
                v.scalar_tensor_tensor(
                    Mfs[:].rearrange("p (a b) -> p a b", b=128),
                    Dfs[:].rearrange("p (a b) -> p a b", b=128), 0.0,
                    wex[4][0:5, :, None].to_broadcast([5, 16, 128]),
                    ALU.is_equal, ALU.mult)
                v.drain()
                for k in range(1, 4):
                    v.scalar_tensor_tensor(
                        tmp5[:].rearrange("p (a b) -> p a b", b=128),
                        Dfs[:].rearrange("p (a b) -> p a b", b=128), float(k),
                        wex[4 + k][0:5, :, None].to_broadcast([5, 16, 128]),
                        ALU.is_equal, ALU.mult)
                    v.drain()
                    v.tensor_tensor(Mfs[:], Mfs[:], tmp5[:], ALU.add)
                    v.drain()
                v.sem_inc(ch.h, 1)
                v.wait_ge(ch.h, it + 26)
                for c16 in range(16):
                    inst = v.tensor_tensor(
                        rstg[(c16 % 2) * 64:(c16 % 2) * 64 + 64, c16 // 2, :],
                        rstg[(c16 % 2) * 64:(c16 % 2) * 64 + 64, c16 // 2, :],
                        pO[:, c16 * 128:(c16 + 1) * 128], ALU.add)
                v.drain()
                inst.then_inc(ch.h, 1)

        @block.gpsimd
        def _(g, q0=qg.n):
            with g.Fori(0, NC) as c:
                it = bgt + c * DGT
                g.wait_ge(ch.h, it + 1)
                self.dstep(g, qg, ch, Vtmp[:], V_tm[:, c, :],
                           qv=q0 + 16 * (8 * c + 1))
                self.dstep(g, qg, ch, Vsb[0:127, :], Vtmp[1:128, :],
                           qv=q0 + 16 * (8 * c + 2))
                with g.If(c < NC - 1):
                    ccp = nc.s_assert_le(c + 1, NC - 1)
                    g.dma_start(out=Vsb[127:128, :], in_=V_tm[0:1, ccp, :]
                                ).then_inc(qg.h, 16)
                with g.Else():
                    g.dma_start(out=Vsb[127:128, :], in_=vshfix[:]
                                ).then_inc(qg.h, 16)
                g.wait_ge(qg.h, q0 + 16 * (8 * c + 3))
                g.sem_inc(ch.h, 1)
                self.dstep(g, qg, ch, sfs[:], d["sind"][0, c],
                           qv=q0 + 16 * (8 * c + 4))
                self.dstep(g, qg, ch, sbs[:], d["sind"][1, c],
                           qv=q0 + 16 * (8 * c + 5))
                self.dstep(g, qg, ch, stage5f[:],
                           d["csrd"][0][None, :, bass.ts(c, 128)]
                           .to_broadcast([5, 16, 128]),
                           qv=q0 + 16 * (8 * c + 6))
                self.dstep(g, qg, ch, stage5b[:],
                           d["csrd"][1][None, :, bass.ts(c, 128)]
                           .to_broadcast([5, 16, 128]),
                           qv=q0 + 16 * (8 * c + 7))
                g.wait_ge(ch.h, it + 27)
                self.dstep(g, qg, ch, dst_r[:, :, bass.ts(c, 128)], rstg[:],
                           qv=q0 + 16 * (8 * c + 8))

        @block.tensor
        def _(te):
            with te.Fori(0, NC) as c:
                it = bgt + c * DGT
                te.wait_ge(ch.h, it + 8)
                for q in range(4):
                    inst = te.matmul(pA[:, q * 512:(q + 1) * 512],
                                     self.ones_l[:, 0:128],
                                     stage5f[0:1, q * 512:(q + 1) * 512],
                                     start=True, stop=True)
                inst.then_inc(ch.h, 1)
                te.wait_ge(ch.h, it + 15)
                for q in range(4):
                    inst = te.matmul(pO[:, q * 512:(q + 1) * 512], zg[:, 0:64], zg[:],
                                     start=True, stop=False)
                inst.then_inc(ch.h, 1)
                te.wait_ge(ch.h, it + 16)
                for c16 in range(16):
                    te.matmul(pO[:, c16 * 128:(c16 + 1) * 128],
                              Vsf[:, c16 * 64:(c16 + 1) * 64],
                              Mf[:, c16 * 128:(c16 + 1) * 128],
                              start=False, stop=False)
                    inst = te.matmul(pO[:, c16 * 128:(c16 + 1) * 128],
                                     sfs[:, c16 * 64:(c16 + 1) * 64],
                                     Mfs[:, c16 * 128:(c16 + 1) * 128],
                                     start=False, stop=False)
                inst.then_inc(ch.h, 1)
                te.wait_ge(ch.h, it + 17)
                for q in range(4):
                    inst = te.matmul(pA[:, q * 512:(q + 1) * 512],
                                     self.ones_l[:, 0:128],
                                     stage5b[0:1, q * 512:(q + 1) * 512],
                                     start=True, stop=True)
                inst.then_inc(ch.h, 1)
                te.wait_ge(ch.h, it + 24)
                for c16 in range(16):
                    te.matmul(pO[:, c16 * 128:(c16 + 1) * 128],
                              Vsb[:, c16 * 64:(c16 + 1) * 64],
                              Mf[:, c16 * 128:(c16 + 1) * 128],
                              start=False, stop=False)
                    inst = te.matmul(pO[:, c16 * 128:(c16 + 1) * 128],
                                     sbs[:, c16 * 64:(c16 + 1) * 64],
                                     Mfs[:, c16 * 128:(c16 + 1) * 128],
                                     start=False, stop=False)
                with te.If(c == 0):
                    for c16 in range(16):
                        te.matmul(pO[:, c16 * 128:c16 * 128 + 1],
                                  r1t[0:1, c16 * 64:(c16 + 1) * 64],
                                  e0sw[:, c16:c16 + 1],
                                  start=False, stop=False)
                for q in range(4):
                    inst = te.matmul(pO[:, q * 512:(q + 1) * 512], zg[:, 0:64],
                                     zg[:], start=False, stop=True)
                inst.then_inc(ch.h, 1)

        @block.sync
        def _(sy, q0=qsy.n):
            with sy.Fori(0, NC) as c:
                it = bgt + c * DGT
                sy.wait_ge(ch.h, it + 25)
                self.dstep(sy, qsy, ch, rstg[:], src_r[:, :, bass.ts(c, 128)],
                           qv=q0 + 16 * (c + 1))

        qsy.n += 16 * NC
        qg.n += 112 * NC + 16 * NC
        ch.n = bgt + NC * DGT
        esG.close()
        es.close()
        esA.close()
        return ch


def _prep_inputs(inputs, T, n_cores):
    f32 = np.float32
    x = inputs["L0_attn_a_x"]
    per_core = []
    wq = np.ascontiguousarray(inputs["L0_attn_a_Wq"].reshape(D, D))
    wk = np.ascontiguousarray(inputs["L0_attn_a_Wk"].reshape(D, D))
    wv = np.ascontiguousarray(inputs["L0_attn_a_Wv"].reshape(D, D))
    wo = np.ascontiguousarray(inputs["L0_attn_a_Wo"].reshape(D, D))
    bqkvo = np.concatenate([inputs["L0_attn_a_bq"].reshape(-1),
                            inputs["L0_attn_a_bk"].reshape(-1),
                            inputs["L0_attn_a_bv"].reshape(-1),
                            inputs["L0_attn_a_bo"].reshape(-1)])[None, :]
    bwk = np.ascontiguousarray(inputs["L0_attn_b_Wk"])
    bwv = np.ascontiguousarray(inputs["L0_attn_b_Wv"])
    bbkv = np.concatenate([inputs["L0_attn_b_bk"].reshape(-1),
                           inputs["L0_attn_b_bv"].reshape(-1)])[None, :]
    rhw = inputs["L0_attn_b_ReadingHead"].reshape(1, D)
    bww = np.ascontiguousarray(inputs["L0_attn_b_bidirection_weight"].T).reshape(1, -1)
    gwt = np.ascontiguousarray(inputs["L0_ffn_gate_w"].T)
    gbr = inputs["L0_ffn_gate_b"][None, :]
    w1c = np.concatenate([inputs["L0_ffn_w1"], inputs["L0_ffn_sw1"][None]], 0)
    w3c = np.concatenate([inputs["L0_ffn_w3"], inputs["L0_ffn_sw3"][None]], 0)
    w2c = np.concatenate([inputs["L0_ffn_w2"], inputs["L0_ffn_sw2"][None]], 0)
    b1c = np.concatenate([inputs["L0_ffn_b1"].reshape(-1),
                          inputs["L0_ffn_sb1"].reshape(-1)])[None, :]
    b3c = np.concatenate([inputs["L0_ffn_b3"].reshape(-1),
                          inputs["L0_ffn_sb3"].reshape(-1)])[None, :]
    b2c = np.concatenate([inputs["L0_ffn_b2"].reshape(-1),
                          inputs["L0_ffn_sb2"].reshape(-1)])[None, :]
    shared = dict(wq=wq, wk=wk, wv=wv, wo=wo, bqkvo=bqkvo, bwk=bwk, bwv=bwv,
                  bbkv=bbkv, rhw=rhw, bww=bww, gwt=gwt, gbr=gbr,
                  w1c=w1c, w3c=w3c, w2c=w2c, b1c=b1c, b3c=b3c, b2c=b2c)
    shared = {k: np.ascontiguousarray(v, dtype=f32) for k, v in shared.items()}
    for core in range(n_cores):
        b, half = core // 2, core % 2
        xin = np.ascontiguousarray(x[b, half * T:(half + 1) * T, :].T, dtype=f32)
        m = np.zeros((1, 16), f32)
        m[0, 0] = float(half)
        m[0, 1] = 1.0 - float(half)
        per_core.append(dict(shared, xin=xin, meta=m))
    return per_core


def kernel(**inputs):
    T = 4096
    n_cores = 8
    b = B(T, n_cores)
    nc = b.build()
    in_maps = _prep_inputs(inputs, T, n_cores)
    res = run_bass_kernel_spmd(nc, in_maps, core_ids=list(range(n_cores)))
    x = inputs["L0_attn_a_x"]
    Bsz, S, _ = x.shape
    out = np.empty((Bsz, S, D), np.float32)
    for core in range(n_cores):
        bb, half = core // 2, core % 2
        out[bb, half * T:(half + 1) * T, :] = np.asarray(res.results[core]["out"]).T
    return out


# version 02
# speedup vs baseline: 1.2807x; 1.2807x over previous
import numpy as np
import ml_dtypes
from contextlib import ExitStack

import concourse.bass as bass
from concourse.bacc import Bacc
import concourse.mybir as mybir
from concourse.bass_utils import run_bass_kernel_spmd

F32 = mybir.dt.float32
BF16 = mybir.dt.bfloat16
AF = mybir.ActivationFunctionType
ALU = mybir.AluOpType

D = 1024
KD = 8
NH = 16
HDM = 64
HF = 1408
KHF = 11
NE = 9
LN_EPS = 1e-5
ATT_EPS = 1e-6


class Chain:
    def __init__(self, nc, es, name):
        self.h = es.enter_context(nc.semaphore(name))
        self.n = 0


class B:
    def __init__(self, T, n_cores):
        self.T = T
        self.n_cores = n_cores
        self.pairs = [[2 * i, 2 * i + 1] for i in range(max(1, n_cores // 2))]
        self.nc = Bacc("TRN2", target_bir_lowering=False)
        self.es = ExitStack()
        self.dr = {}

    def sb(self, name, shape, dt=F32):
        return self.es.enter_context(self.nc.sbuf_tensor(name, shape, dt))

    def chain(self, name):
        return Chain(self.nc, self.es, name)

    def din(self, n, s, dt=F32):
        self.dr[n] = self.nc.dram_tensor(n, s, dt, kind="ExternalInput").ap()
        return self.dr[n]

    def dint(self, n, s, dt=F32):
        self.dr[n] = self.nc.dram_tensor(n, s, dt, kind="Internal").ap()
        return self.dr[n]

    def dstep(self, eng, q, ch, out, in_, qv=None):
        eng.dma_start(out=out, in_=in_).then_inc(q.h, 16)
        if qv is None:
            q.n += 16
            qv = q.n
        eng.wait_ge(q.h, qv)
        eng.sem_inc(ch.h, 1)

    def build(self):
        nc, T = self.nc, self.T
        self.din("xin", [D, T])
        self.out = nc.dram_tensor("out", [D, T], F32, kind="ExternalOutput").ap()
        self.din("meta", [1, 16])
        for n in ("wq", "wk", "wv", "wo", "bwk", "bwv"):
            self.din(n, [D, D])
        self.din("bqkvo", [1, 4 * D])
        self.din("bbkv", [1, 2 * D])
        self.din("rhw", [1, D])
        self.din("bww", [1, 8 * NH])
        self.din("gwt", [D, 8])
        self.din("gbr", [1, 8])
        self.din("w1c", [NE, D, HF]); self.din("w3c", [NE, D, HF])
        self.din("w2c", [NE, HF, D])
        self.din("b1c", [1, NE * HF]); self.din("b3c", [1, NE * HF])
        self.din("b2c", [1, NE * D])

        self.dint("rA", [D, T])
        self.dint("rB", [D, T])
        self.dint("hsd", [KD, 128, T], BF16)
        self.dint("zqfd", [KD, 128, T], BF16)
        self.dint("w1b", [NE, D, HF], BF16)
        self.dint("w3b", [NE, D, HF], BF16)
        self.dint("w2b", [NE, HF, D], BF16)
        self.dint("rwd", [NE, T])
        self.dint("kvx_i", [65, D]); self.dint("kvx_o", [65, D])
        NC = T // 128
        self.dint("r1x_i", [1, D]); self.dint("r1x_o", [2, D])
        self.dint("r2f_i", [4, D]); self.dint("r2f_o", [8, D])
        self.dint("r2b_i", [4, D]); self.dint("r2b_o", [8, D])
        self.dint("sind", [2, NC, 5, NH * HDM], BF16)
        self.dint("csrd", [2, 16, T], BF16)
        self.dint("flA", [16, NC * 4], BF16)
        self.dint("flB", [16, NC * 4], BF16)
        self.dint("flE", [1, NC * 16], BF16)

        self.ones_l = self.sb("ones_l", [1, 512], BF16)
        self.ones_c = self.sb("ones_c", [128, 1], BF16)
        self.ident = self.sb("ident", [128, 128], BF16)
        self.metat = self.sb("metat", [1, 16], F32)

        with nc.Block() as block:
            self.emit_setup(block)
            prev = self.setup_ch
            prev = self.emit_ln(block, self.dr["xin"], "n1", prev)
            prev = self.emit_attn_a(block, self.dr["xin"], self.dr["rA"], prev)
            prev = self.emit_ln(block, self.dr["rA"], "n2", prev)
            prev = self.emit_ffn(block, self.dr["rA"], self.dr["rB"], 0, prev)
            prev = self.emit_ln(block, self.dr["rB"], "n3", prev)
            prev = self.emit_attn_b(block, self.dr["rB"], self.dr["rA"], prev)
            prev = self.emit_ln(block, self.dr["rA"], "n4", prev)
            prev = self.emit_ffn(block, self.dr["rA"], self.out, 1, prev)
            self.final_ch = prev

        self.es.close()
        nc.finalize()
        return nc

    def emit_setup(self, block):
        nc = self.nc
        ch = self.chain("setup")
        self.setup_ch = ch
        self.wcast = self.chain("wcast")
        qsy = self.chain("su_qsy")
        d = self.dr

        @block.sync
        def _(sy):
            self.dstep(sy, qsy, ch, self.metat[:], d["meta"][:])
        ch.n += 1

        @block.gpsimd
        def _(g):
            g.iota(self.ident[:], pattern=[[-1, 128]], base=0, channel_multiplier=1,
                   allow_small_or_imprecise_dtypes=True).then_inc(ch.h, 1)
        ch.n += 1

        @block.vector
        def _(v):
            v.memset(self.ones_l[:], 1.0)
            v.memset(self.ones_c[:], 1.0)
            v.drain()
            v.wait_ge(ch.h, ch.n)
            v.tensor_scalar(self.ident[:], self.ident[:], 0.0, 1.0,
                            ALU.is_equal, ALU.mult)
            v.drain().then_inc(ch.h, 1)
        ch.n += 1

        @block.gpsimd
        def _(g):
            for e in range(NE):
                for a, b_ in (("w1c", "w1b"), ("w3c", "w3b"), ("w2c", "w2b")):
                    g.dma_start(out=d[b_][e], in_=d[a][e]).then_inc(self.wcast.h, 16)
                    self.wcast.n += 16

    def emit_ln(self, block, src, tag, prev):
        nc, T = self.nc, self.T
        TC = T // 512
        es = ExitStack()
        sbl = lambda n, sh, dt=F32: es.enter_context(nc.sbuf_tensor(tag + n, sh, dt))
        psl = lambda n, sh, dt=F32: es.enter_context(nc.psum_tensor(tag + n, sh, dt))
        d = self.dr

        xc = sbl("xc", [128, KD, 512])
        xb = sbl("xb", [128, KD, 512], BF16)
        sq = sbl("sq", [128, KD, 512], BF16)
        hs = sbl("hs", [128, KD, 512], BF16)
        rows = sbl("rows", [1, 6, 512])
        rbf = sbl("rbf", [1, 2, 512], BF16)
        pstat = psl("pst", [1, 2, 512])
        pbc = psl("pbc", [128, 2, 512])

        ch = self.chain(tag + "ch")
        qsy = self.chain(tag + "qsy")
        qg = self.chain(tag + "qg")
        src_r = src.rearrange("(k p) t -> p k t", p=128)
        hsd_r = d["hsd"].rearrange("k p t -> p k t")
        pn = prev.n
        DL = 10

        @block.sync
        def _(sy):
            sy.wait_ge(prev.h, pn)
            with sy.Fori(0, TC) as i:
                sy.wait_ge(ch.h, i * DL)
                self.dstep(sy, qsy, ch, xc[:], src_r[:, :, bass.ts(i, 512)],
                           qv=16 * (i + 1))

        @block.vector
        def _(v):
            with v.Fori(0, TC) as i:
                it = i * DL
                v.wait_ge(ch.h, it + 1)
                for k in range(KD):
                    inst = v.tensor_scalar(xb[:, k], xc[:, k], 1.0, None, ALU.mult)
                v.drain()
                inst.then_inc(ch.h, 1)
                v.wait_ge(ch.h, it + 4)
                v.tensor_scalar(rows[:, 0], pstat[:, 0], 1.0 / D, None, ALU.mult)
                v.tensor_scalar(rows[:, 1], pstat[:, 1], 1.0 / D, None, ALU.mult)
                v.drain()
                v.tensor_tensor(rows[:, 2], rows[:, 0], rows[:, 0], ALU.mult)
                v.drain()
                v.tensor_tensor(rows[:, 3], rows[:, 1], rows[:, 2], ALU.subtract)
                v.drain()
                inst = v.tensor_scalar(rows[:, 3], rows[:, 3], LN_EPS, None, ALU.add)
                v.drain()
                inst.then_inc(ch.h, 1)
                v.wait_ge(ch.h, it + 6)
                v.reciprocal(rows[:, 4], rows[:, 4])
                v.drain()
                v.tensor_tensor(rows[:, 5], rows[:, 0], rows[:, 4], ALU.mult)
                v.drain()
                v.tensor_scalar(rbf[:, 0], rows[:, 4], 1.0, None, ALU.mult)
                inst = v.tensor_scalar(rbf[:, 1], rows[:, 5], 1.0, None, ALU.mult)
                v.drain()
                inst.then_inc(ch.h, 1)
                v.wait_ge(ch.h, it + 8)
                for k in range(KD):
                    v.tensor_tensor(hs[:, k], xb[:, k], pbc[:, 0], ALU.mult)
                v.drain()
                for k in range(KD):
                    inst = v.tensor_tensor(hs[:, k], hs[:, k], pbc[:, 1], ALU.subtract)
                v.drain()
                inst.then_inc(ch.h, 1)

        @block.scalar
        def _(sc):
            with sc.Fori(0, TC) as i:
                it = i * DL
                sc.wait_ge(ch.h, it + 2)
                for k in range(KD):
                    inst = sc.activation(sq[:, k], xb[:, k], AF.Square)
                sc.drain()
                inst.then_inc(ch.h, 1)
                sc.wait_ge(ch.h, it + 5)
                inst = sc.activation(rows[:, 4], rows[:, 3], AF.Sqrt)
                sc.drain()
                inst.then_inc(ch.h, 1)

        @block.tensor
        def _(te):
            with te.Fori(0, TC) as i:
                it = i * DL
                te.wait_ge(ch.h, it + 3)
                for k in range(KD):
                    te.matmul(pstat[:, 0], self.ones_c[:], xb[:, k],
                              start=(k == 0), stop=(k == KD - 1))
                for k in range(KD):
                    inst = te.matmul(pstat[:, 1], self.ones_c[:], sq[:, k],
                                     start=(k == 0), stop=(k == KD - 1))
                inst.then_inc(ch.h, 1)
                te.wait_ge(ch.h, it + 7)
                te.matmul(pbc[:, 0], self.ones_l[:, 0:128], rbf[:, 0],
                          start=True, stop=True)
                inst = te.matmul(pbc[:, 1], self.ones_l[:, 0:128], rbf[:, 1],
                                 start=True, stop=True)
                inst.then_inc(ch.h, 1)

        @block.gpsimd
        def _(g):
            with g.Fori(0, TC) as i:
                it = i * DL
                g.wait_ge(ch.h, it + 9)
                self.dstep(g, qg, ch, hsd_r[:, :, bass.ts(i, 512)], hs[:],
                           qv=16 * (i + 1))

        ch.n = TC * DL
        qsy.n = 16 * TC
        qg.n = 16 * TC
        es.close()
        return ch

    def emit_attn_a(self, block, xsrc, rdst, prev):
        nc, T = self.nc, self.T
        TT = T // 128
        TC = T // 512
        es = ExitStack()
        sbl = lambda n, sh, dt=F32: es.enter_context(nc.sbuf_tensor("aa" + n, sh, dt))
        psl = lambda n, sh, dt=F32: es.enter_context(nc.psum_tensor("aa" + n, sh, dt))
        d = self.dr

        wqt = sbl("wqt", [128, KD, D], BF16)
        wkt = sbl("wkt", [128, KD, D], BF16)
        wvt = sbl("wvt", [128, KD, D], BF16)
        wot = sbl("wot", [128, KD, D], BF16)
        bflat = sbl("bflat", [1, 4 * D], BF16)
        hst = sbl("hst", [128, KD, 128], BF16)
        qf = sbl("qf", [128, TT, D], BF16)
        kft = sbl("kft", [128, D], BF16)
        vft = sbl("vft", [128, D], BF16)
        etmp = sbl("etmp", [128, 2, 512])
        ebf = sbl("ebf", [128, 2, 512], BF16)
        kvp = sbl("kvp", [65, D])
        kvbf = sbl("kvbf", [128, D], BF16)
        ksbc = sbl("ksbc", [128, D], BF16)
        zt = sbl("zt", [128, D])
        zr = sbl("zr", [128, NH])
        qstg = sbl("qstg", [128, D], BF16)
        ztile = sbl("ztile", [128, KD, 128], BF16)
        zqc = sbl("zqc", [128, KD, 512], BF16)
        xc2 = sbl("xc2", [128, KD, 512])
        ybuf = sbl("ybuf", [128, KD, 512], BF16)
        zerob = sbl("zerob", [128, 512], BF16)

        es1 = ExitStack()
        pproj = es1.enter_context(nc.psum_tensor("aapp", [128, 2, 512], F32))
        pkv = es1.enter_context(nc.psum_tensor("aapkv", [64, D], F32))
        pksum = es1.enter_context(nc.psum_tensor("aapks", [1, D], F32))

        ch = self.chain("aach")
        qsy = self.chain("aaqsy")
        qg = self.chain("aaqg")
        ccs = self.chain("aacc")
        pn = prev.n
        hsd_r = d["hsd"].rearrange("k p t -> p k t")
        xsrc_r = xsrc.rearrange("(k p) t -> p k t", p=128)
        rdst_r = rdst.rearrange("(k p) t -> p k t", p=128)
        zq_r = d["zqfd"].rearrange("k p t -> p k t")

        @block.gpsimd
        def _(g):
            g.wait_ge(prev.h, pn)
            for nm, t_ in (("wq", wqt), ("wk", wkt), ("wv", wvt), ("wo", wot)):
                self.dstep(g, qg, ch, t_[:],
                           d[nm].rearrange("(k p) n -> p k n", p=128))
                ch.n += 1
            self.dstep(g, qg, ch, bflat[:], d["bqkvo"][:])
            ch.n += 1

        @block.vector
        def _(v):
            v.wait_ge(ch.h, ch.n)
            v.memset(zerob[:], 0.0)
            v.drain().then_inc(ch.h, 1)
        ch.n += 1

        @block.tensor
        def _(te):
            te.wait_ge(ch.h, ch.n)
            for hf in range(2):
                te.matmul(pkv[:, hf * 512:(hf + 1) * 512], zerob[:, 0:64],
                          zerob[:], start=True, stop=False)
            for hf in range(2):
                inst = te.matmul(pksum[:, hf * 512:(hf + 1) * 512], self.ones_c[:],
                                 zerob[:], start=True, stop=False)
            inst.then_inc(ch.h, 1)
        ch.n += 1

        base = ch.n
        DEL = 14

        @block.sync
        def _(sy):
            with sy.Fori(0, TT) as i:
                sy.wait_ge(ch.h, base + i * DEL)
                self.dstep(sy, qsy, ch, hst[:], hsd_r[:, :, bass.ts(i, 128)],
                           qv=16 * (i + 1))

        @block.tensor
        def _(te):
            with te.Fori(0, TT) as i:
                it = base + i * DEL
                for off, wt, boff in ((1, wqt, 0), (6, wkt, D), (11, wvt, 2 * D)):
                    te.wait_ge(ch.h, it + off)
                    for hf in range(2):
                        for k in range(KD):
                            te.matmul(pproj[:, hf], hst[:, k],
                                      wt[:, k, hf * 512:(hf + 1) * 512],
                                      start=(k == 0), stop=False)
                        inst = te.matmul(pproj[:, hf], self.ones_l[:, 0:128],
                                         bflat[:, boff + hf * 512:boff + (hf + 1) * 512],
                                         start=False, stop=True)
                    inst.then_inc(ch.h, 1)
                te.wait_ge(ch.h, it + 13)
                for h in range(NH):
                    te.matmul(pkv[:, h * 64:(h + 1) * 64], kft[:, h * 64:(h + 1) * 64],
                              vft[:, h * 64:(h + 1) * 64], start=False, stop=False)
                for hf in range(2):
                    inst = te.matmul(pksum[:, hf * 512:(hf + 1) * 512], self.ones_c[:],
                                     kft[:, hf * 512:(hf + 1) * 512],
                                     start=False, stop=False)
                inst.then_inc(ch.h, 1)

        @block.scalar
        def _(sc):
            with sc.Fori(0, TT) as i:
                it = base + i * DEL
                sc.wait_ge(ch.h, it + 2)
                for hf in range(2):
                    inst = sc.activation(qf[:, i, hf * 512:(hf + 1) * 512],
                                         pproj[:, hf], AF.Relu)
                sc.drain()
                inst.then_inc(ch.h, 1)
                sc.wait_ge(ch.h, it + 4)
                for hf in range(2):
                    inst = sc.activation(ebf[:, hf], etmp[:, hf], AF.Exp)
                sc.drain()
                inst.then_inc(ch.h, 1)
                sc.wait_ge(ch.h, it + 7)
                for hf in range(2):
                    inst = sc.activation(kft[:, hf * 512:(hf + 1) * 512],
                                         pproj[:, hf], AF.Relu)
                sc.drain()
                inst.then_inc(ch.h, 1)
                sc.wait_ge(ch.h, it + 9)
                for hf in range(2):
                    inst = sc.activation(ebf[:, hf], etmp[:, hf], AF.Exp)
                sc.drain()
                inst.then_inc(ch.h, 1)

        @block.vector
        def _(v):
            with v.Fori(0, TT) as i:
                it = base + i * DEL
                v.wait_ge(ch.h, it + 3)
                for hf in range(2):
                    inst = v.tensor_scalar(etmp[:, hf], pproj[:, hf], 0.0, None, ALU.min)
                v.drain()
                inst.then_inc(ch.h, 1)
                v.wait_ge(ch.h, it + 5)
                for hf in range(2):
                    inst = v.tensor_tensor(qf[:, i, hf * 512:(hf + 1) * 512],
                                           qf[:, i, hf * 512:(hf + 1) * 512],
                                           ebf[:, hf], ALU.add)
                v.drain()
                inst.then_inc(ch.h, 1)
                v.wait_ge(ch.h, it + 8)
                for hf in range(2):
                    inst = v.tensor_scalar(etmp[:, hf], pproj[:, hf], 0.0, None, ALU.min)
                v.drain()
                inst.then_inc(ch.h, 1)
                v.wait_ge(ch.h, it + 10)
                for hf in range(2):
                    inst = v.tensor_tensor(kft[:, hf * 512:(hf + 1) * 512],
                                           kft[:, hf * 512:(hf + 1) * 512],
                                           ebf[:, hf], ALU.add)
                v.drain()
                inst.then_inc(ch.h, 1)
                v.wait_ge(ch.h, it + 12)
                for hf in range(2):
                    inst = v.tensor_scalar(vft[:, hf * 512:(hf + 1) * 512],
                                           pproj[:, hf], 1.0, None, ALU.mult)
                v.drain()
                inst.then_inc(ch.h, 1)

        ch.n = base + TT * DEL
        qsy.n = 16 * TT

        @block.tensor
        def _(te):
            te.wait_ge(ch.h, ch.n)
            for hf in range(2):
                te.matmul(pkv[:, hf * 512:(hf + 1) * 512], zerob[:, 0:64],
                          zerob[:], start=False, stop=True)
            for hf in range(2):
                inst = te.matmul(pksum[:, hf * 512:(hf + 1) * 512], self.ones_c[:],
                                 zerob[:], start=False, stop=True)
            inst.then_inc(ch.h, 1)
        ch.n += 1

        @block.vector
        def _(v):
            v.wait_ge(ch.h, ch.n)
            v.tensor_scalar(kvp[0:64, :], pkv[:], 1.0, None, ALU.mult)
            inst = v.tensor_scalar(kvp[64:65, :], pksum[:], 1.0, None, ALU.mult)
            v.drain()
            inst.then_inc(ch.h, 1)
        ch.n += 1

        es1.close()
        pksb = psl("pksb", [128, 2, 512])
        ptr = psl("ptr", [128, KD, 128], BF16)
        pyt = psl("pyt", [128, 512])
        pout = psl("pout", [128, 512])

        n0 = ch.n

        @block.gpsimd
        def _(g):
            g.wait_ge(ch.h, n0)
            self.dstep(g, qg, ch, d["kvx_i"][:], kvp[:])
            g.collective_compute("AllReduce", ALU.add, self.pairs,
                                 [d["kvx_i"].opt()], [d["kvx_o"].opt()]
                                 ).then_inc(ccs.h, 1)
            g.wait_ge(ccs.h, 1)
            g.sem_inc(ch.h, 1)
            self.dstep(g, qg, ch, kvbf[0:64, :], d["kvx_o"][0:64, :])
            self.dstep(g, qg, ch, kvbf[64:128, :], d["kvx_o"][0:64, :])
            self.dstep(g, qg, ch, kvp[64:65, :], d["kvx_o"][64:65, :])
        ch.n += 5

        @block.vector
        def _(v):
            v.wait_ge(ch.h, ch.n)
            v.tensor_scalar(ebf[0:1, 0, :], kvp[64:65, 0:512], 1.0, None, ALU.mult)
            inst = v.tensor_scalar(ebf[0:1, 1, :], kvp[64:65, 512:1024], 1.0, None,
                                   ALU.mult)
            v.drain()
            inst.then_inc(ch.h, 1)
        ch.n += 1

        @block.tensor
        def _(te):
            te.wait_ge(ch.h, ch.n)
            te.matmul(pksb[:, 0], self.ones_l[:, 0:128], ebf[0:1, 0, :],
                      start=True, stop=True)
            inst = te.matmul(pksb[:, 1], self.ones_l[:, 0:128], ebf[0:1, 1, :],
                             start=True, stop=True)
            inst.then_inc(ch.h, 1)
        ch.n += 1

        @block.vector
        def _(v):
            v.wait_ge(ch.h, ch.n)
            v.tensor_scalar(ksbc[:, 0:512], pksb[:, 0], 1.0, None, ALU.mult)
            inst = v.tensor_scalar(ksbc[:, 512:1024], pksb[:, 1], 1.0, None, ALU.mult)
            v.drain()
            inst.then_inc(ch.h, 1)
        ch.n += 1

        basez = ch.n
        DZ = 2

        @block.vector
        def _(v):
            with v.Fori(0, TT) as i:
                it = basez + i * DZ
                v.wait_ge(ch.h, it)
                v.tensor_tensor(zt[:], qf[:, i, :], ksbc[:], ALU.mult)
                v.drain()
                inst = v.tensor_reduce(zr[:, :, None],
                                       zt[:].rearrange("p (h e) -> p h e", h=NH),
                                       mybir.AxisListType.X, ALU.add)
                v.drain()
                inst.then_inc(ch.h, 1)
                v.tensor_scalar(zr[:], zr[:], ATT_EPS, None, ALU.add)
                v.drain()
                v.reciprocal(zr[:], zr[:])
                v.drain()
                inst = v.tensor_tensor(qf[:, i, :].rearrange("p (h e) -> p h e", h=NH),
                                       qf[:, i, :].rearrange("p (h e) -> p h e", h=NH),
                                       zr[:, :, None].to_broadcast([128, NH, HDM]),
                                       ALU.mult)
                v.drain()
                inst.then_inc(ch.h, 1)

        ch.n = basez + TT * DZ

        baset = ch.n
        DT_ = 4

        @block.vector
        def _(v):
            with v.Fori(0, TT) as i:
                it = baset + i * DT_
                v.wait_ge(ch.h, it)
                inst = v.tensor_scalar(qstg[:], qf[:, i, :], 1.0, None, ALU.mult)
                v.drain()
                inst.then_inc(ch.h, 1)
                v.wait_ge(ch.h, it + 2)
                for k in range(KD):
                    inst = v.tensor_scalar(ztile[:, k], ptr[:, k], 1.0, None, ALU.mult)
                v.drain()
                inst.then_inc(ch.h, 1)

        @block.tensor
        def _(te):
            with te.Fori(0, TT) as i:
                it = baset + i * DT_
                te.wait_ge(ch.h, it + 1)
                for k in range(KD):
                    inst = te.transpose(ptr[:, k], qstg[:, k * 128:(k + 1) * 128],
                                        self.ident[:])
                inst.then_inc(ch.h, 1)

        @block.gpsimd
        def _(g, q0=qg.n):
            with g.Fori(0, TT) as i:
                it = baset + i * DT_
                g.wait_ge(ch.h, it + 3)
                self.dstep(g, qg, ch, zq_r[:, :, bass.ts(i, 128)], ztile[:],
                           qv=q0 + 16 * (i + 1))

        qg.n += 16 * TT
        ch.n = baset + TT * DT_

        basey = ch.n
        DY = 35

        @block.sync
        def _(sy, q0=qsy.n):
            with sy.Fori(0, TC) as i:
                sy.wait_ge(ch.h, basey + i * DY)
                self.dstep(sy, qsy, ch, zqc[:], zq_r[:, :, bass.ts(i, 512)],
                           qv=q0 + 16 * (2 * i + 1))
                self.dstep(sy, qsy, ch, xc2[:], xsrc_r[:, :, bass.ts(i, 512)],
                           qv=q0 + 16 * (2 * i + 2))

        @block.tensor
        def _(te):
            with te.Fori(0, TC) as i:
                it = basey + i * DY
                for hm in range(KD):
                    te.wait_ge(ch.h, it + 2 + 2 * hm)
                    for sub in range(2):
                        h = 2 * hm + sub
                        hb = (h % 2) * 64
                        inst = te.matmul(pyt[sub * 64:(sub + 1) * 64, :],
                                         kvbf[hb:hb + 64, h * 64:(h + 1) * 64],
                                         zqc[hb:hb + 64, h // 2, :],
                                         start=True, stop=True)
                    inst.then_inc(ch.h, 1)
                for dt in range(KD):
                    te.wait_ge(ch.h, it + 18 + 2 * dt)
                    for k in range(KD):
                        te.matmul(pout[:], wot[:, k, dt * 128:(dt + 1) * 128],
                                  ybuf[:, k, :], start=(k == 0), stop=False)
                    inst = te.matmul(pout[:],
                                     bflat[:, 3 * D + dt * 128:3 * D + dt * 128 + 128],
                                     self.ones_l[:], start=False, stop=True)
                    inst.then_inc(ch.h, 1)

        @block.vector
        def _(v):
            with v.Fori(0, TC) as i:
                it = basey + i * DY
                for hm in range(KD):
                    v.wait_ge(ch.h, it + 3 + 2 * hm)
                    inst = v.tensor_scalar(ybuf[:, hm, :], pyt[:], 1.0, None, ALU.mult)
                    v.drain()
                    inst.then_inc(ch.h, 1)
                for dt in range(KD):
                    v.wait_ge(ch.h, it + 19 + 2 * dt)
                    inst = v.tensor_tensor(xc2[:, dt], pout[:], xc2[:, dt], ALU.add)
                    v.drain()
                    inst.then_inc(ch.h, 1)

        @block.gpsimd
        def _(g, q0=qg.n):
            with g.Fori(0, TC) as i:
                it = basey + i * DY
                g.wait_ge(ch.h, it + 34)
                self.dstep(g, qg, ch, rdst_r[:, :, bass.ts(i, 512)], xc2[:],
                           qv=q0 + 16 * (i + 1))

        qsy.n += 32 * TC
        qg.n += 16 * TC
        ch.n = basey + TC * DY
        es.close()
        return ch

    def emit_ffn(self, block, src, dst, call, prev):
        nc, T = self.nc, self.T
        TH = T // 2
        THC = TH // 512
        tg = f"f{call}"
        es = ExitStack()
        sbl = lambda n, sh, dt=F32: es.enter_context(nc.sbuf_tensor(tg + n, sh, dt))
        psl = lambda n, sh, dt=F32: es.enter_context(nc.psum_tensor(tg + n, sh, dt))
        d = self.dr

        gwtt = sbl("gwt", [128, KD, 8], BF16)
        gbbc = sbl("gbbc", [128, 8])
        b1s = sbl("b1s", [1, HF], BF16)
        b3s = sbl("b3s", [1, HF], BF16)
        b2s = sbl("b2s", [1, D], BF16)
        xst = sbl("xst", [128, KD, TH], BF16)
        w1t = sbl("w1t", [128, KD, HF], BF16)
        w3t = sbl("w3t", [128, KD, HF], BF16)
        w2t = sbl("w2t", [128, KHF, D], BF16)
        yac = sbl("yac", [128, KD, TH])
        hbuf = sbl("hbuf", [128, KHF, 512], BF16)
        ubuf = sbl("ubuf", [128, 512], BF16)
        tbuf = sbl("tbuf", [128, 512], BF16)
        lgb = sbl("lgb", [8, 512], BF16)
        sfx = sbl("sfx", [128, 4, 40])
        rwt = sbl("rwt", [128, 4, NE], BF16)
        rwev = sbl("rwev", [NE, 512])
        rwre = sbl("rwre", [1, TH])
        rwrb = sbl("rwrb", [1, TH], BF16)
        rwbcb = sbl("rwbcb", [128, 512], BF16)
        rc = sbl("rc", [128, KD, 128])

        pg = psl("pg", [8, 512])
        pboth = psl("pboth", [128, 544], BF16)
        ptt = pboth[:, 0:32].rearrange("p (a b) -> p a b", b=8)
        prw = pboth[0:NE, 32:544]
        ps1 = psl("ps1", [128, 2, 512])
        ps2 = psl("ps2", [128, 2, 512])
        pmix = psl("pmix", [128, 2, 512])
        prb = pmix[:, 0]

        ch = self.chain(tg + "ch")
        qsy = self.chain(tg + "qsy")
        qg = self.chain(tg + "qg")
        pn = prev.n
        src_r = src.rearrange("(k p) t -> p k t", p=128)
        dst_r = dst.rearrange("(k p) t -> p k t", p=128)
        hsd_r = d["hsd"].rearrange("k p t -> p k t")

        @block.gpsimd
        def _(g):
            g.wait_ge(prev.h, pn)
            if call == 0:
                g.wait_ge(self.wcast.h, self.wcast.n)
            self.dstep(g, qg, ch, gwtt[:],
                       d["gwt"].rearrange("(k p) e -> p k e", p=128))
            self.dstep(g, qg, ch, lgb[0:1, 0:8], d["gbr"][:])
        ch.n += 2

        @block.tensor
        def _(te):
            te.wait_ge(ch.h, ch.n)
            te.matmul(prb[:, 0:8], self.ones_l[:, 0:128], lgb[0:1, 0:8],
                      start=True, stop=True).then_inc(ch.h, 1)
        ch.n += 1

        @block.vector
        def _(v):
            v.wait_ge(ch.h, ch.n)
            v.tensor_scalar(gbbc[:], prb[:, 0:8], 1.0, None, ALU.mult)
            v.drain().then_inc(ch.h, 1)
        ch.n += 1

        for half in range(2):
            t0 = half * TH
            n0 = ch.n

            @block.gpsimd
            def _(g, t0=t0, n0=n0):
                g.wait_ge(ch.h, n0)
                self.dstep(g, qg, ch, xst[:], hsd_r[:, :, t0:t0 + TH])
            ch.n += 1

            n1 = ch.n

            @block.vector
            def _(v, n1=n1):
                v.wait_ge(ch.h, n1)
                for k in range(KD):
                    v.memset(yac[:, k], 0.0)
                v.drain()
                inst = v.memset(rwt[:, :, 8:9], 1.0)
                v.drain()
                inst.then_inc(ch.h, 1)
            ch.n += 1

            bg = ch.n
            DG = 9

            @block.tensor
            def _(te, bg=bg):
                with te.Fori(0, THC) as i:
                    it = bg + i * DG
                    te.wait_ge(ch.h, it)
                    for k in range(KD):
                        inst = te.matmul(pg[:], gwtt[:, k], xst[:, k, bass.ts(i, 512)],
                                         start=(k == 0), stop=(k == KD - 1))
                    inst.then_inc(ch.h, 1)
                    te.wait_ge(ch.h, it + 2)
                    for q in range(4):
                        inst = te.transpose(ptt[:, q], lgb[:, q * 128:(q + 1) * 128],
                                            self.ident[0:8, 0:8])
                    inst.then_inc(ch.h, 1)
                    te.wait_ge(ch.h, it + 6)
                    for q in range(4):
                        inst = te.transpose(prw[:, bass.ts(q, 128)], rwt[:, q, :],
                                            self.ident[:])
                    inst.then_inc(ch.h, 1)

            @block.scalar
            def _(sc, bg=bg):
                with sc.Fori(0, THC) as i:
                    it = bg + i * DG
                    sc.wait_ge(ch.h, it + 4)
                    for q in range(4):
                        inst = sc.activation(sfx[:, q, 2:10], sfx[:, q, 2:10], AF.Exp)
                    sc.drain()
                    inst.then_inc(ch.h, 1)

            @block.vector
            def _(v, bg=bg):
                with v.Fori(0, THC) as i:
                    it = bg + i * DG
                    v.wait_ge(ch.h, it + 1)
                    inst = v.tensor_scalar(lgb[:], pg[:], 1.0, None, ALU.mult)
                    v.drain()
                    inst.then_inc(ch.h, 1)
                    v.wait_ge(ch.h, it + 3)
                    for q in range(4):
                        v.tensor_reduce(sfx[:, q, 0:1], ptt[:, q],
                                        mybir.AxisListType.X, ALU.max)
                    v.drain()
                    for q in range(4):
                        inst = v.tensor_scalar(sfx[:, q, 2:10], ptt[:, q],
                                               sfx[:, q, 0:1], None, ALU.subtract)
                    v.drain()
                    inst.then_inc(ch.h, 1)
                    v.wait_ge(ch.h, it + 5)
                    for q in range(4):
                        v.tensor_reduce(sfx[:, q, 0:1], sfx[:, q, 2:10],
                                        mybir.AxisListType.X, ALU.add)
                    v.drain()
                    for q in range(4):
                        v.reciprocal(sfx[:, q, 0:1], sfx[:, q, 0:1])
                    v.drain()
                    for q in range(4):
                        v.tensor_scalar(sfx[:, q, 10:18], sfx[:, q, 2:10],
                                        sfx[:, q, 0:1], None, ALU.mult)
                    v.drain()
                    for q in range(4):
                        v.tensor_tensor(sfx[:, q, 18:26], sfx[:, q, 10:18],
                                        gbbc[:], ALU.add)
                    v.drain()
                    for q in range(4):
                        v.tensor_reduce(sfx[:, q, 1:2], sfx[:, q, 18:26],
                                        mybir.AxisListType.X, ALU.max)
                    v.drain()
                    for q in range(4):
                        v.tensor_scalar(sfx[:, q, 26:34], sfx[:, q, 18:26],
                                        sfx[:, q, 1:2], None, ALU.is_equal)
                    v.drain()
                    for q in range(4):
                        v.scalar_tensor_tensor(sfx[:, q, 26:34], sfx[:, q, 26:34],
                                               -1e9, sfx[:, q, 18:26],
                                               ALU.mult, ALU.add)
                    v.drain()
                    for q in range(4):
                        v.tensor_reduce(sfx[:, q, 1:2], sfx[:, q, 26:34],
                                        mybir.AxisListType.X, ALU.max)
                    v.drain()
                    for q in range(4):
                        v.tensor_scalar(sfx[:, q, 26:34], sfx[:, q, 18:26],
                                        sfx[:, q, 1:2], None, ALU.is_ge)
                    v.drain()
                    for q in range(4):
                        inst = v.tensor_tensor(rwt[:, q, 0:8], sfx[:, q, 10:18],
                                               sfx[:, q, 26:34], ALU.mult)
                    v.drain()
                    inst.then_inc(ch.h, 1)
                    v.wait_ge(ch.h, it + 7)
                    inst = v.tensor_scalar(rwev[:], prw[:], 1.0, None, ALU.mult)
                    v.drain()
                    inst.then_inc(ch.h, 1)

            @block.gpsimd
            def _(g, bg=bg, t0=t0, q0=qg.n):
                with g.Fori(0, THC) as i:
                    it = bg + i * DG
                    g.wait_ge(ch.h, it + 8)
                    self.dstep(g, qg, ch, d["rwd"][:, t0:t0 + TH][:, bass.ts(i, 512)],
                               rwev[:], qv=q0 + 16 * (i + 1))

            qg.n += 16 * THC
            ch.n = bg + THC * DG

            be = ch.n
            DTC = 3
            DE = 8 + THC * DTC

            @block.gpsimd
            def _(g, be=be, t0=t0, q0=qg.n):
                with g.Fori(0, NE) as e:
                    it = be + e * DE
                    g.wait_ge(ch.h, it)
                    qv = q0 + e * 112
                    self.dstep(g, qg, ch, w1t[:],
                               d["w1b"].rearrange("e (k p) h -> e p k h", p=128)[e],
                               qv=qv + 16)
                    self.dstep(g, qg, ch, w3t[:],
                               d["w3b"].rearrange("e (k p) h -> e p k h", p=128)[e],
                               qv=qv + 32)
                    self.dstep(g, qg, ch, w2t[:],
                               d["w2b"].rearrange("e (k p) h -> e p k h", p=128)[e],
                               qv=qv + 48)
                    self.dstep(g, qg, ch, rwre[:], d["rwd"][e, t0:t0 + TH][None, :],
                               qv=qv + 64)
                    self.dstep(g, qg, ch, b1s[:], d["b1c"][:, bass.ts(e, HF)],
                               qv=qv + 80)
                    self.dstep(g, qg, ch, b3s[:], d["b3c"][:, bass.ts(e, HF)],
                               qv=qv + 96)
                    self.dstep(g, qg, ch, b2s[:], d["b2c"][:, bass.ts(e, D)],
                               qv=qv + 112)

            shm = self.chain(tg + f"hm{half}")
            sha_ = self.chain(tg + f"ha{half}")
            shd = self.chain(tg + f"hd{half}")
            she = self.chain(tg + f"he{half}")
            swm = self.chain(tg + f"wm{half}")
            swe = self.chain(tg + f"we{half}")

            @block.vector
            def _(v, be=be):
                with v.Fori(0, NE) as e:
                    it = be + e * DE
                    v.wait_ge(ch.h, it + 7)
                    inst = v.tensor_scalar(rwrb[:], rwre[:], 1.0, None, ALU.mult)
                    v.drain()
                    inst.then_inc(ch.h, 1)
                    with v.Fori(0, THC) as i:
                        jt = it + 8 + i * DTC
                        u11 = (e * THC + i) * KHF
                        u8 = (e * THC + i) * KD
                        v.wait_ge(ch.h, jt + 1)
                        inst = v.tensor_scalar(rwbcb[:], prb[:], 1.0, None, ALU.mult)
                        v.drain()
                        inst.then_inc(ch.h, 1)
                        for ht in range(KHF):
                            par = ht % 2
                            v.wait_ge(sha_.h, u11 + ht + 1)
                            v.wait_ge(shm.h, u11 + ht + 1)
                            if ht >= 1:
                                v.wait_ge(she.h, u11 + ht)
                            inst = v.tensor_tensor(tbuf[:], ps1[:, par], ubuf[:],
                                                   ALU.mult)
                            v.drain()
                            inst.then_inc(shd.h, 1)
                        for dt in range(KD):
                            par = dt % 2
                            v.wait_ge(swm.h, u8 + dt + 1)
                            v.tensor_tensor(ubuf[:], pmix[:, par], rwbcb[:], ALU.mult)
                            v.drain()
                            inst = v.tensor_tensor(yac[:, dt, bass.ts(i, 512)],
                                                   yac[:, dt, bass.ts(i, 512)],
                                                   ubuf[:], ALU.add)
                            v.drain()
                            inst.then_inc(swe.h, 1)
                            if dt == KD - 1:
                                v.sem_inc(ch.h, 1)

            @block.scalar
            def _(sc, be=be):
                with sc.Fori(0, NE) as e:
                    with sc.Fori(0, THC) as i:
                        u11 = (e * THC + i) * KHF
                        for ht in range(KHF):
                            par = ht % 2
                            sc.wait_ge(shm.h, u11 + ht + 1)
                            if ht >= 1:
                                sc.wait_ge(shd.h, u11 + ht)
                            inst = sc.activation(ubuf[:], ps2[:, par], AF.Copy)
                            sc.drain()
                            inst.then_inc(sha_.h, 1)
                            sc.wait_ge(shd.h, u11 + ht + 1)
                            inst = sc.activation(hbuf[:, ht], tbuf[:], AF.Silu)
                            sc.drain()
                            inst.then_inc(she.h, 1)

            @block.tensor
            def _(te, be=be):
                with te.Fori(0, NE) as e:
                    it = be + e * DE
                    with te.Fori(0, THC) as i:
                        jt = it + 8 + i * DTC
                        u11 = (e * THC + i) * KHF
                        u8 = (e * THC + i) * KD
                        te.wait_ge(ch.h, jt)
                        te.matmul(prb[:], self.ones_l[:, 0:128],
                                  rwrb[:, bass.ts(i, 512)],
                                  start=True, stop=True).then_inc(ch.h, 1)
                        te.wait_ge(ch.h, jt + 2)
                        for ht in range(KHF):
                            par = ht % 2
                            if ht >= 2:
                                te.wait_ge(sha_.h, u11 + ht - 1)
                                te.wait_ge(shd.h, u11 + ht - 1)
                            for k in range(KD):
                                te.matmul(ps1[:, par],
                                          w1t[:, k, ht * 128:(ht + 1) * 128],
                                          xst[:, k, bass.ts(i, 512)],
                                          start=(k == 0), stop=False)
                            te.matmul(ps1[:, par], b1s[:, ht * 128:(ht + 1) * 128],
                                      self.ones_l[:], start=False, stop=True)
                            for k in range(KD):
                                te.matmul(ps2[:, par],
                                          w3t[:, k, ht * 128:(ht + 1) * 128],
                                          xst[:, k, bass.ts(i, 512)],
                                          start=(k == 0), stop=False)
                            inst = te.matmul(ps2[:, par],
                                             b3s[:, ht * 128:(ht + 1) * 128],
                                             self.ones_l[:], start=False, stop=True)
                            inst.then_inc(shm.h, 1)
                        te.wait_ge(she.h, u11 + KHF)
                        for dt in range(KD):
                            par = dt % 2
                            if dt >= 2:
                                te.wait_ge(swe.h, u8 + dt - 1)
                            for k in range(KHF):
                                te.matmul(pmix[:, par],
                                          w2t[:, k, dt * 128:(dt + 1) * 128],
                                          hbuf[:, k], start=(k == 0), stop=False)
                            inst = te.matmul(pmix[:, par],
                                             b2s[:, dt * 128:(dt + 1) * 128],
                                             self.ones_l[:], start=False, stop=True)
                            inst.then_inc(swm.h, 1)

            qg.n += 112 * NE
            ch.n = be + NE * DE

            br = ch.n
            DR = 3
            NRC = TH // 128

            @block.sync
            def _(sy, br=br, t0=t0, q0=qsy.n):
                with sy.Fori(0, NRC) as i:
                    sy.wait_ge(ch.h, br + i * DR)
                    self.dstep(sy, qsy, ch,
                               rc[:], src_r[:, :, t0:t0 + TH][:, :, bass.ts(i, 128)],
                               qv=q0 + 16 * (i + 1))

            @block.vector
            def _(v, br=br):
                with v.Fori(0, NRC) as i:
                    it = br + i * DR
                    v.wait_ge(ch.h, it + 1)
                    for k in range(KD):
                        inst = v.tensor_tensor(rc[:, k],
                                               yac[:, k, bass.ts(i, 128)],
                                               rc[:, k], ALU.add)
                    v.drain()
                    inst.then_inc(ch.h, 1)

            @block.gpsimd
            def _(g, br=br, t0=t0, q0=qg.n):
                with g.Fori(0, NRC) as i:
                    it = br + i * DR
                    g.wait_ge(ch.h, it + 2)
                    self.dstep(g, qg, ch,
                               dst_r[:, :, t0:t0 + TH][:, :, bass.ts(i, 128)], rc[:],
                               qv=q0 + 16 * (i + 1))

            qsy.n += 16 * NRC
            qg.n += 16 * NRC
            ch.n = br + NRC * DR

        es.close()
        return ch

    def emit_attn_b(self, block, src, dst, prev):
        nc, T = self.nc, self.T
        NC = T // 128
        POI = 8192.0
        BIG = 30000.0
        es = ExitStack()
        sbl = lambda n, sh, dt=F32: es.enter_context(nc.sbuf_tensor("ab" + n, sh, dt))
        d = self.dr

        bwr = sbl("bwr", [1, 128])
        bwb = sbl("bwb", [1, 128], BF16)
        i4b = sbl("i4b", [1, 4], BF16)
        iota5c = sbl("iota5c", [5, 1])
        mcol = sbl("mcol", [16, 2])
        mcol2 = sbl("mcol2", [16, 1])
        metab = sbl("metab", [1, 2], BF16)
        swbB = sbl("swbB", [1, 16])
        e0sw = sbl("e0sw", [1, NH], BF16)
        V_tm = sbl("V_tm", [128, NC, D], BF16)
        validr = sbl("validr", [16, T], BF16)
        zq16 = sbl("zq16", [16, 1])
        csum = sbl("csum", [16, T])
        csvS = sbl("csvS", [16, T // 128])
        csum_reb = sbl("csumr", [16, T], BF16)
        csv_reb = sbl("csvr", [16, T], BF16)
        arow = sbl("arow", [16, T], BF16)
        Cfr = sbl("Cfr", [16, T], BF16)
        trr = sbl("trr", [16, T], BF16)
        Cbr = trr
        tmpI = sbl("tmpI", [16, NC * 4])
        tmpJ = sbl("tmpJ", [16, NC * 4], BF16)
        tmpJ2 = sbl("tmpJ2", [16, NC * 4], BF16)
        cntst4f = sbl("cst4", [1, 16 * NC * 4], BF16)
        cntch4f = sbl("cch4", [1, 16 * NC * 4], BF16)
        Erow = sbl("Erow", [1, NC * 16], BF16)
        Cbs = sbl("Cbs", [5, NC * 16], BF16)
        Ccol_f = sbl("Ccf", [128, NC * 16])
        Ccol_b = sbl("Ccb", [128, NC * 16])
        r1t = sbl("r1t", [2, D], BF16)
        r1tb = sbl("r1tb", [1, D], BF16)
        ownl = sbl("ownl", [1, D], BF16)
        vshfix = sbl("vshf", [1, D], BF16)

        esA = ExitStack()
        pA = esA.enter_context(nc.psum_tensor("abpA", [128, 2048], F32))

        ch = self.chain("abch")
        qsy = self.chain("abqsy")
        qg = self.chain("abqg")
        ccs = self.chain("abcc")
        pn = prev.n
        hsd_r = d["hsd"].rearrange("k p t -> p k t")
        src_r = src.rearrange("(k p) t -> p k t", p=128)
        dst_r = dst.rearrange("(k p) t -> p k t", p=128)

        esP = ExitStack()
        pT = esP.enter_context(nc.psum_tensor("abpT", [128, 1024], BF16))
        sblP = lambda n, sh, dt=F32: esP.enter_context(nc.sbuf_tensor("ab" + n, sh, dt))
        bwkt = sblP("bwkt", [128, KD, D], BF16)
        bwvt = sblP("bwvt", [128, KD, D], BF16)
        bbf = sblP("bbf", [1, 2 * D], BF16)
        rhr = sblP("rhr", [1, D], BF16)
        rhbc = sblP("rhbc", [128, D], BF16)
        hst2 = sblP("hst2", [128, KD, 128], BF16)
        k1t = sblP("k1t", [128, D], BF16)
        kdt = sblP("kdt", [128, 512])
        dcol = sblP("dcol", [128, NH])
        vt_tm = sblP("vt_tm", [128, NH], BF16)
        @block.gpsimd
        def _(g):
            g.wait_ge(prev.h, pn)
            self.dstep(g, qg, ch, bwkt[:], d["bwk"].rearrange("(k p) n -> p k n", p=128))
            self.dstep(g, qg, ch, bwvt[:], d["bwv"].rearrange("(k p) n -> p k n", p=128))
            self.dstep(g, qg, ch, bbf[:], d["bbkv"][:])
            self.dstep(g, qg, ch, rhr[:], d["rhw"][:])
            self.dstep(g, qg, ch, bwr[:], d["bww"][:])
            g.iota(tmpI[:], pattern=[[0, NC], [1, 4]], base=0, channel_multiplier=0,
                   allow_small_or_imprecise_dtypes=True)
            g.iota(iota5c[:], pattern=[[0, 1]], base=0, channel_multiplier=1,
                   allow_small_or_imprecise_dtypes=True).then_inc(ch.h, 1)
        ch.n += 6

        @block.vector
        def _(v):
            v.wait_ge(ch.h, ch.n)
            v.tensor_scalar(metab[:], self.metat[:, 0:2], 1.0, None, ALU.mult)
            v.tensor_scalar(bwb[:], bwr[:], 1.0, None, ALU.mult)
            v.memset(i4b[:], 0.0)
            v.memset(zq16[:], 0.0)
            v.memset(e0sw[:], 0.0)
            v.drain()
            for j in range(4):
                v.memset(i4b[:, j:j + 1], float(j))
            inst = v.tensor_reduce(swbB[:, :, None],
                                   bwr[:, 64:128].rearrange("o (r c) -> o c r", c=16),
                                   mybir.AxisListType.X, ALU.add)
            v.drain()
            inst.then_inc(ch.h, 1)
        ch.n += 1

        @block.tensor
        def _(te):
            te.wait_ge(ch.h, ch.n)
            te.matmul(pA[:, 0:512], self.ones_l[:, 0:128], rhr[:, 0:512],
                      start=True, stop=True)
            te.matmul(pA[:, 512:1024], self.ones_l[:, 0:128], rhr[:, 512:1024],
                      start=True, stop=True)
            inst = te.matmul(pA[0:16, 1024:1026], self.ones_l[:, 0:16], metab[:],
                             start=True, stop=True)
            inst.then_inc(ch.h, 1)
        ch.n += 1

        @block.vector
        def _(v):
            v.wait_ge(ch.h, ch.n)
            v.tensor_scalar(rhbc[:, 0:512], pA[:, 0:512], 1.0, None, ALU.mult)
            v.tensor_scalar(rhbc[:, 512:1024], pA[:, 512:1024], 1.0, None, ALU.mult)
            v.tensor_scalar(mcol[:], pA[0:16, 1024:1026], 1.0, None, ALU.mult)
            v.drain()
            v.tensor_scalar(mcol2[:], mcol[:, 1:2], BIG, None, ALU.mult)
            v.drain()
            v.tensor_scalar(e0sw[:], swbB[:], self.metat[:, 1:2], None, ALU.mult)
            v.drain()
            v.sem_inc(ch.h, 1)
        ch.n += 1


        bp = ch.n
        DP = 9

        @block.sync
        def _(sy, q0=qsy.n):
            with sy.Fori(0, NC) as i:
                sy.wait_ge(ch.h, bp + i * DP)
                self.dstep(sy, qsy, ch, hst2[:], hsd_r[:, :, bass.ts(i, 128)],
                           qv=q0 + 16 * (i + 1))

        @block.tensor
        def _(te):
            with te.Fori(0, NC) as i:
                it = bp + i * DP
                te.wait_ge(ch.h, it + 1)
                for hf in range(2):
                    for k in range(KD):
                        te.matmul(pA[:, hf * 512:(hf + 1) * 512], hst2[:, k],
                                  bwvt[:, k, hf * 512:(hf + 1) * 512],
                                  start=(k == 0), stop=False)
                    inst = te.matmul(pA[:, hf * 512:(hf + 1) * 512],
                                     self.ones_l[:, 0:128],
                                     bbf[:, D + hf * 512:D + (hf + 1) * 512],
                                     start=False, stop=True)
                inst.then_inc(ch.h, 1)
                te.wait_ge(ch.h, it + 3)
                for hf in range(2):
                    for k in range(KD):
                        te.matmul(pA[:, hf * 512:(hf + 1) * 512], hst2[:, k],
                                  bwkt[:, k, hf * 512:(hf + 1) * 512],
                                  start=(k == 0), stop=False)
                    inst = te.matmul(pA[:, hf * 512:(hf + 1) * 512],
                                     self.ones_l[:, 0:128],
                                     bbf[:, hf * 512:(hf + 1) * 512],
                                     start=False, stop=True)
                inst.then_inc(ch.h, 1)
                te.wait_ge(ch.h, it + 7)
                inst = te.transpose(pT[0:16, 0:128], vt_tm[:], self.ident[:])
                inst.then_inc(ch.h, 1)

        @block.scalar
        def _(sc):
            with sc.Fori(0, NC) as i:
                it = bp + i * DP
                sc.wait_ge(ch.h, it + 2)
                for hf in range(2):
                    inst = sc.activation(V_tm[:, i, hf * 512:(hf + 1) * 512],
                                         pA[:, hf * 512:(hf + 1) * 512], AF.Relu)
                sc.drain()
                inst.then_inc(ch.h, 1)
                sc.wait_ge(ch.h, it + 4)
                for hf in range(2):
                    inst = sc.activation(k1t[:, hf * 512:(hf + 1) * 512],
                                         pA[:, hf * 512:(hf + 1) * 512], AF.Relu)
                sc.drain()
                inst.then_inc(ch.h, 1)

        @block.vector
        def _(v):
            with v.Fori(0, NC) as i:
                it = bp + i * DP
                v.wait_ge(ch.h, it + 5)
                inst = v.tensor_tensor(kdt[:], k1t[:, 0:512], rhbc[:, 0:512], ALU.mult)
                v.drain()
                inst = v.tensor_reduce(dcol[:, 0:8][:, :, None],
                                       kdt[:].rearrange("p (h e) -> p h e", h=8),
                                       mybir.AxisListType.X, ALU.add)
                v.drain()
                inst = v.tensor_tensor(kdt[:], k1t[:, 512:1024], rhbc[:, 512:1024], ALU.mult)
                v.drain()
                inst.then_inc(ch.h, 1)
                inst = v.tensor_reduce(dcol[:, 8:16][:, :, None],
                                       kdt[:].rearrange("p (h e) -> p h e", h=8),
                                       mybir.AxisListType.X, ALU.add)
                v.drain()
                inst = v.tensor_scalar(vt_tm[:], dcol[:], 0.5, None, ALU.is_gt)
                v.drain()
                inst.then_inc(ch.h, 1)
                v.wait_ge(ch.h, it + 8)
                inst = v.tensor_scalar(validr[:, bass.ts(i, 128)], pT[0:16, 0:128],
                                       1.0, None, ALU.mult)
                v.drain()
                inst.then_inc(ch.h, 1)

        qsy.n += 16 * NC
        ch.n = bp + NC * DP

        n0 = ch.n

        @block.gpsimd
        def _(g, n0=n0):
            g.wait_ge(ch.h, n0)
            self.dstep(g, qg, ch, d["r1x_i"][:], V_tm[0:1, 0, :])
            g.collective_compute("AllGather", ALU.bypass, self.pairs,
                                 [d["r1x_i"].opt()], [d["r1x_o"].opt()]
                                 ).then_inc(ccs.h, 1)
            g.wait_ge(ccs.h, 1)
            g.sem_inc(ch.h, 1)
            self.dstep(g, qg, ch, r1t[:], d["r1x_o"][:])
            self.dstep(g, qg, ch, r1tb[:], d["r1x_o"][1:2, :])
            self.dstep(g, qg, ch, ownl[:], V_tm[127:128, NC - 1, :])
        ch.n += 5

        @block.vector
        def _(v):
            v.wait_ge(ch.h, ch.n)
            v.tensor_scalar(vshfix[:], r1tb[:], self.metat[:, 1:2], None, ALU.mult)
            v.tensor_scalar(ownl[:], ownl[:], self.metat[:, 0:1], None, ALU.mult)
            v.drain()
            v.tensor_tensor(vshfix[:], vshfix[:], ownl[:], ALU.add)
            v.drain()
            inst = v.tensor_scalar(validr[:, 0:1], validr[:, 0:1],
                                   mcol[:, 0:1], None, ALU.mult)
            v.drain()
            inst.then_inc(ch.h, 1)
            v.tensor_tensor_scan(csum[:], validr[:],
                                 zq16[:].to_broadcast([16, T]), 0.0,
                                 ALU.add, ALU.add)
            v.drain()
            v.tensor_tensor(csvS[:], csum[:, 0:T:128], validr[:, 0:T:128],
                            ALU.subtract)
            v.drain()
            v.tensor_tensor(csum_reb[:].rearrange("p (c u) -> p c u", u=128),
                            csum[:].rearrange("p (c u) -> p c u", u=128),
                            csvS[:, :, None].to_broadcast([16, NC, 128]),
                            ALU.subtract)
            v.drain()
            v.tensor_tensor(csv_reb[:], csum_reb[:], validr[:], ALU.subtract)
            v.drain()
            v.tensor_scalar(arow[:], validr[:], -POI, POI, ALU.mult, ALU.add)
            v.drain()
            v.tensor_tensor(Cfr[:], csum_reb[:], arow[:], ALU.add)
            v.tensor_scalar(trr[:], csum_reb[:], 1.0, None, ALU.subtract)
            v.drain()
            v.tensor_tensor(trr[:], trr[:], arow[:], ALU.subtract)
            v.drain()
            v.tensor_scalar(csv_reb[:, 0:1], csv_reb[:, 0:1], mcol2[:, 0:1],
                            None, ALU.add)
            v.drain()
            v.tensor_tensor(tmpJ[:].rearrange("p (c j) -> p c j", j=4),
                            csum_reb[:, 127:T:128][:, :, None].to_broadcast([16, NC, 4]),
                            tmpI[:].rearrange("p (c j) -> p c j", j=4),
                            ALU.subtract)
            v.drain()
            inst = v.tensor_scalar(tmpJ2[:], tmpJ[:], -1.0, None, ALU.mult)
            v.drain()
            inst.then_inc(ch.h, 1)
        ch.n += 2

        @block.gpsimd
        def _(g):
            g.wait_ge(ch.h, ch.n)
            self.dstep(g, qg, ch, d["csrd"][0], csum_reb[:])
            self.dstep(g, qg, ch, d["csrd"][1], csv_reb[:])
            self.dstep(g, qg, ch, d["flA"][:], tmpJ[:])
            self.dstep(g, qg, ch, d["flB"][:], tmpJ2[:])
            g.dma_start(out=d["flE"].rearrange("o (c ch) -> o c ch", ch=16),
                        in_=d["csrd"][0].rearrange("ch (c u) -> u c ch", u=128)[127:128, :, :],
                        allow_slow_non_contiguous=True).then_inc(qg.h, 16)
            qg.n += 16
            g.wait_ge(qg.h, qg.n)
            g.sem_inc(ch.h, 1)
            self.dstep(g, qg, ch, cntst4f[:],
                       d["flA"].rearrange("a b -> (a b)")[None, :])
            self.dstep(g, qg, ch, cntch4f[:],
                       d["flB"].rearrange("a b -> (a b)")[None, :])
            self.dstep(g, qg, ch, Erow[:], d["flE"][:])
        ch.n += 8

        @block.tensor
        def _(te):
            te.wait_ge(ch.h, ch.n)
            inst = te.matmul(pA[0:5, 0:NC * 16], self.ones_l[:, 0:5], Erow[:],
                             start=True, stop=True)
            inst.then_inc(ch.h, 1)
        ch.n += 1

        @block.vector
        def _(v):
            v.wait_ge(ch.h, ch.n)
            inst = v.tensor_scalar(Cbs[:], pA[0:5, 0:NC * 16], iota5c[:], None,
                                   ALU.add)
            v.drain()
            inst.then_inc(ch.h, 1)
        ch.n += 1

        cstg = sblP("cstg", [16, 128], BF16)
        bc = ch.n
        DC = 6

        @block.tensor
        def _(te):
            with te.Fori(0, NC) as i:
                it = bc + i * DC
                te.wait_ge(ch.h, it + 1)
                te.transpose(pT[:, 0:16], cstg[:],
                             self.ident[0:16, 0:16]).then_inc(ch.h, 1)
                te.wait_ge(ch.h, it + 4)
                te.transpose(pT[:, 16:32], cstg[:],
                             self.ident[0:16, 0:16]).then_inc(ch.h, 1)

        @block.vector
        def _(v):
            with v.Fori(0, NC) as i:
                it = bc + i * DC
                v.wait_ge(ch.h, it)
                inst = v.tensor_scalar(cstg[:], Cfr[:, bass.ts(i, 128)],
                                       1.0, None, ALU.mult)
                v.drain()
                inst.then_inc(ch.h, 1)
                v.wait_ge(ch.h, it + 2)
                inst = v.tensor_scalar(Ccol_f[:, bass.ts(i, 16)], pT[:, 0:16],
                                       1.0, None, ALU.mult)
                v.drain()
                inst.then_inc(ch.h, 1)
                inst = v.tensor_scalar(cstg[:], Cbr[:, bass.ts(i, 128)],
                                       1.0, None, ALU.mult)
                v.drain()
                inst.then_inc(ch.h, 1)
                v.wait_ge(ch.h, it + 5)
                inst = v.tensor_scalar(Ccol_b[:, bass.ts(i, 16)], pT[:, 16:32],
                                       1.0, None, ALU.mult)
                v.drain()
                inst.then_inc(ch.h, 1)

        ch.n = bc + NC * DC
        esP.close()

        Scur_f = sbl("Scf", [5, NH * HDM], BF16)
        Scur_b = sbl("Scb", [5, NH * HDM], BF16)
        msf = sbl("msf", [128, NH * 4], BF16)
        mss = sbl("mss", [5, NH * 4], BF16)
        Vsb = sbl("Vsb", [128, D], BF16)
        Vtmp = sbl("Vtmp", [128, D], BF16)
        Vsf = sbl("Vsf", [128, D], BF16)
        sfs = sbl("sfs", [5, NH * HDM], BF16)
        sbs = sbl("sbs", [5, NH * HDM], BF16)
        stage5f = sbl("st5f", [5, 16 * 128], BF16)
        stage5b = sbl("st5b", [5, 16 * 128], BF16)
        Dlf = sbl("Dlf", [128, 2048], BF16)
        Mf = sbl("Mf", [128, 2048], BF16)
        Dfs = sbl("Dfs", [5, 2048], BF16)
        Mfs = sbl("Mfs", [5, 2048], BF16)
        tmp5 = sbl("tmp5", [5, 2048], BF16)
        wex = [sbl(f"wex{k}", [128, 16], BF16) for k in range(8)]
        zg = sbl("zg", [128, 512], BF16)
        l0f = sbl("l0f", [4, D], BF16)
        l1b = sbl("l1b", [4, D], BF16)
        rstg = sbl("rstg", [128, KD, 128])
        padrep = sbl("padrep", [4, D], BF16)

        for k in range(8):
            nk = ch.n

            @block.tensor
            def _(te, k=k, nk=nk):
                te.wait_ge(ch.h, nk)
                inst = te.matmul(pA[:, 0:16], self.ones_l[:, 0:128],
                                 bwb[:, k * 16:(k + 1) * 16],
                                 start=True, stop=True)
                inst.then_inc(ch.h, 1)
            ch.n += 1
            nk = ch.n

            @block.vector
            def _(v, k=k, nk=nk):
                v.wait_ge(ch.h, nk)
                inst = v.tensor_scalar(wex[k][:], pA[:, 0:16], 1.0, None, ALU.mult)
                v.drain()
                inst.then_inc(ch.h, 1)
            ch.n += 1

        esC = ExitStack()
        psS = esC.enter_context(nc.psum_tensor("abpsS", [4, 1024], F32))

        n0 = ch.n

        @block.gpsimd
        def _(g, n0=n0):
            g.wait_ge(ch.h, n0)
            self.dstep(g, qg, ch, padrep[:, None, :],
                       d["r1x_o"][None, 0:1, :].to_broadcast([4, 1, D]))
            self.dstep(g, qg, ch, Scur_f[:, None, :],
                       d["r1x_o"][None, 0:1, :].to_broadcast([5, 1, D]))
            self.dstep(g, qg, ch, Scur_b[:, None, :],
                       d["r1x_o"][None, 0:1, :].to_broadcast([5, 1, D]))
        ch.n += 3

        def emit_chain(fwd):
            b0 = ch.n
            DCH = 10 if not fwd else 7
            Scur = Scur_f if fwd else Scur_b
            dirIdx = 0 if fwd else 1
            Ccol = Ccol_f if fwd else Ccol_b
            Vget = V_tm if fwd else Vsb

            @block.gpsimd
            def _(g, b0=b0, q0=qg.n):
                with g.Fori(0, NC) as c:
                    it = b0 + c * DCH
                    cc = nc.s_assert_le(nc.s_assert_ge(NC - 1 - c, 0), NC - 1) if not fwd else c
                    g.wait_ge(ch.h, it)
                    npd = 4 if not fwd else 1
                    self.dstep(g, qg, ch, d["sind"][dirIdx, cc], Scur[:],
                               qv=q0 + 16 * (npd * c + 1))
                    if not fwd:
                        self.dstep(g, qg, ch, Vtmp[:], V_tm[:, cc, :],
                                   qv=q0 + 16 * (npd * c + 2))
                        self.dstep(g, qg, ch, Vsb[0:127, :], Vtmp[1:128, :],
                                   qv=q0 + 16 * (npd * c + 3))
                        with g.If(cc < NC - 1):
                            ccp = nc.s_assert_le(cc + 1, NC - 1)
                            g.dma_start(out=Vsb[127:128, :], in_=V_tm[0:1, ccp, :]
                                        ).then_inc(qg.h, 16)
                        with g.Else():
                            g.dma_start(out=Vsb[127:128, :], in_=vshfix[:]
                                        ).then_inc(qg.h, 16)
                        g.wait_ge(qg.h, q0 + 16 * (npd * c + 4))
                        g.sem_inc(ch.h, 1)

            of = 1 if fwd else 4

            @block.tensor
            def _(te, b0=b0, of=of):
                with te.Fori(0, NC) as c:
                    it = b0 + c * DCH
                    cc = nc.s_assert_le(nc.s_assert_ge(NC - 1 - c, 0), NC - 1) if not fwd else c
                    te.wait_ge(ch.h, it + of)
                    if fwd:
                        for c16 in range(16):
                            inst = te.matmul(pA[:, c16 * 4:(c16 + 1) * 4],
                                             self.ones_l[:, 0:128],
                                             cntst4f[:, bass.ts(c16 * NC + cc, 4)],
                                             start=(c16 == 0), stop=(c16 == 15))
                    else:
                        inst = te.matmul(pA[:, 0:4], self.ones_l[:, 0:128],
                                         i4b[:], start=True, stop=True)
                    inst.then_inc(ch.h, 1)
                    te.wait_ge(ch.h, it + of + 2)
                    for c16 in range(16):
                        inst = te.matmul(pA[0:5, 512 + c16 * 4:512 + (c16 + 1) * 4],
                                         self.ones_l[:, 0:5],
                                         cntch4f[:, bass.ts(c16 * NC + cc, 4)],
                                         start=(c16 == 0), stop=(c16 == 15))
                    inst.then_inc(ch.h, 1)
                    te.wait_ge(ch.h, it + of + 4)
                    for c16 in range(16):
                        te.matmul(psS[:, c16 * 64:(c16 + 1) * 64],
                                  msf[:, c16 * 4:(c16 + 1) * 4][:, 0:4] if False
                                  else msf[:, c16 * 4:(c16 + 1) * 4],
                                  Vget[:, cc, c16 * 64:(c16 + 1) * 64] if fwd
                                  else Vsb[:, c16 * 64:(c16 + 1) * 64],
                                  start=(c16 == 0 or c16 == 8), stop=False)
                        inst = te.matmul(psS[:, c16 * 64:(c16 + 1) * 64],
                                         mss[:, c16 * 4:(c16 + 1) * 4],
                                         Scur[:, c16 * 64:(c16 + 1) * 64],
                                         start=False,
                                         stop=(c16 == 7 or c16 == 15))
                    inst.then_inc(ch.h, 1)

            @block.vector
            def _(v, b0=b0, of=of):
                with v.Fori(0, NC) as c:
                    it = b0 + c * DCH
                    cc = nc.s_assert_le(nc.s_assert_ge(NC - 1 - c, 0), NC - 1) if not fwd else c
                    v.wait_ge(ch.h, it + of + 1)
                    if fwd:
                        for c16 in range(16):
                            inst = v.tensor_scalar(
                                msf[:, c16 * 4:(c16 + 1) * 4],
                                pA[:, c16 * 4:(c16 + 1) * 4],
                                Ccol[:, cc * 16 + c16][:, None], 0.0,
                                ALU.subtract, ALU.is_equal)
                    else:
                        for c16 in range(16):
                            inst = v.tensor_scalar(
                                msf[:, c16 * 4:(c16 + 1) * 4],
                                pA[:, 0:4],
                                Ccol[:, cc * 16 + c16][:, None], 0.0,
                                ALU.subtract, ALU.is_equal)
                    v.drain()
                    inst.then_inc(ch.h, 1)
                    v.wait_ge(ch.h, it + of + 3)
                    for c16 in range(16):
                        inst = v.tensor_scalar(
                            mss[:, c16 * 4:(c16 + 1) * 4],
                            pA[0:5, 512 + c16 * 4:512 + (c16 + 1) * 4],
                            iota5c[:], 0.0, ALU.subtract, ALU.is_equal)
                    v.drain()
                    inst.then_inc(ch.h, 1)
                    v.wait_ge(ch.h, it + of + 5)
                    inst = v.tensor_scalar(Scur[0:4, :], psS[:], 1.0, None, ALU.mult)
                    v.drain()
                    inst.then_inc(ch.h, 1)

            if fwd:
                qg.n += 16 * NC
            else:
                qg.n += 64 * NC
            ch.n = b0 + NC * DCH

        emit_chain(True)
        emit_chain(False)

        n0 = ch.n

        @block.gpsimd
        def _(g, n0=n0):
            g.wait_ge(ch.h, n0)
            self.dstep(g, qg, ch, d["r2f_i"][:], Scur_f[0:4, :])
            self.dstep(g, qg, ch, d["r2b_i"][:], Scur_b[0:4, :])
            g.collective_compute("AllGather", ALU.bypass, self.pairs,
                                 [d["r2f_i"].opt()], [d["r2f_o"].opt()]
                                 ).then_inc(ccs.h, 1)
            g.wait_ge(ccs.h, 2)
            g.sem_inc(ch.h, 1)
            g.collective_compute("AllGather", ALU.bypass, self.pairs,
                                 [d["r2b_i"].opt()], [d["r2b_o"].opt()]
                                 ).then_inc(ccs.h, 1)
            g.wait_ge(ccs.h, 3)
            g.sem_inc(ch.h, 1)
            self.dstep(g, qg, ch, l0f[:], d["r2f_o"][0:4, :])
            self.dstep(g, qg, ch, l1b[:], d["r2b_o"][4:8, :])
        ch.n += 6

        @block.vector
        def _(v):
            v.wait_ge(ch.h, ch.n)
            v.tensor_scalar(l0f[:], l0f[:], mcol[0:4, 0:1], None, ALU.mult)
            v.tensor_scalar(Vsf[0:4, 0:D], padrep[:], mcol[0:4, 1:2], None, ALU.mult)
            v.drain()
            v.tensor_tensor(Scur_f[0:4, :], l0f[:], Vsf[0:4, 0:D], ALU.add)
            v.drain()
            v.tensor_scalar(l1b[:], l1b[:], mcol[0:4, 1:2], None, ALU.mult)
            v.tensor_scalar(Vsf[0:4, 0:D], padrep[:], mcol[0:4, 0:1], None, ALU.mult)
            v.drain()
            inst = v.tensor_tensor(Scur_b[0:4, :], l1b[:], Vsf[0:4, 0:D], ALU.add)
            v.drain()
            inst.then_inc(ch.h, 1)
        ch.n += 1

        emit_chain(True)
        emit_chain(False)
        esC.close()

        esG = ExitStack()
        pO = esG.enter_context(nc.psum_tensor("abpO", [64, 2048], F32))

        n0 = ch.n

        @block.vector
        def _(v, n0=n0):
            v.wait_ge(ch.h, n0)
            v.memset(zg[:], 0.0)
            v.drain()
            v.sem_inc(ch.h, 1)
        ch.n += 1

        bgt = ch.n
        DGT = 28

        @block.vector
        def _(v):
            with v.Fori(0, NC) as c:
                it = bgt + c * DGT
                v.wait_ge(ch.h, it)
                inst = v.tensor_scalar(Vsf[:], V_tm[:, c, :], 1.0, None, ALU.mult)
                v.drain()
                inst.then_inc(ch.h, 1)
                v.wait_ge(ch.h, it + 9)
                inst = v.tensor_tensor(
                    Dlf[:].rearrange("p (a b) -> p a b", b=128),
                    pA[:].rearrange("p (a b) -> p a b", b=128),
                    Ccol_f[:, bass.ts(c, 16)][:, :, None].to_broadcast([128, 16, 128]),
                    ALU.subtract)
                v.drain()
                inst.then_inc(ch.h, 1)
                v.wait_ge(ch.h, it + 10)
                inst = v.scalar_tensor_tensor(
                    Mf[:].rearrange("p (a b) -> p a b", b=128),
                    Dlf[:].rearrange("p (a b) -> p a b", b=128), 0.0,
                    wex[0][:, :, None].to_broadcast([128, 16, 128]),
                    ALU.is_equal, ALU.mult)
                v.drain()
                inst.then_inc(ch.h, 1)
                for k in range(1, 4):
                    v.wait_ge(ch.h, it + 10 + k)
                    v.scalar_tensor_tensor(
                        pA[:].rearrange("p (a b) -> p a b", b=128),
                        Dlf[:].rearrange("p (a b) -> p a b", b=128), float(k),
                        wex[k][:, :, None].to_broadcast([128, 16, 128]),
                        ALU.is_equal, ALU.mult)
                    v.drain()
                    inst = v.tensor_tensor(Mf[:], Mf[:], pA[:], ALU.add)
                    v.drain()
                    inst.then_inc(ch.h, 1)
                v.wait_ge(ch.h, it + 14)
                v.tensor_scalar(Dfs[:], stage5f[:], iota5c[:], None, ALU.add)
                v.drain()
                v.scalar_tensor_tensor(
                    Mfs[:].rearrange("p (a b) -> p a b", b=128),
                    Dfs[:].rearrange("p (a b) -> p a b", b=128), 0.0,
                    wex[0][0:5, :, None].to_broadcast([5, 16, 128]),
                    ALU.is_equal, ALU.mult)
                v.drain()
                for k in range(1, 4):
                    v.scalar_tensor_tensor(
                        tmp5[:].rearrange("p (a b) -> p a b", b=128),
                        Dfs[:].rearrange("p (a b) -> p a b", b=128), float(k),
                        wex[k][0:5, :, None].to_broadcast([5, 16, 128]),
                        ALU.is_equal, ALU.mult)
                    v.drain()
                    v.tensor_tensor(Mfs[:], Mfs[:], tmp5[:], ALU.add)
                    v.drain()
                v.sem_inc(ch.h, 1)
                v.wait_ge(ch.h, it + 18)
                inst = v.tensor_tensor(
                    Dlf[:].rearrange("p (a b) -> p a b", b=128),
                    Ccol_b[:, bass.ts(c, 16)][:, :, None].to_broadcast([128, 16, 128]),
                    pA[:].rearrange("p (a b) -> p a b", b=128),
                    ALU.subtract)
                v.drain()
                inst.then_inc(ch.h, 1)
                v.wait_ge(ch.h, it + 19)
                inst = v.scalar_tensor_tensor(
                    Mf[:].rearrange("p (a b) -> p a b", b=128),
                    Dlf[:].rearrange("p (a b) -> p a b", b=128), 0.0,
                    wex[4][:, :, None].to_broadcast([128, 16, 128]),
                    ALU.is_equal, ALU.mult)
                v.drain()
                inst.then_inc(ch.h, 1)
                for k in range(1, 4):
                    v.wait_ge(ch.h, it + 19 + k)
                    v.scalar_tensor_tensor(
                        pA[:].rearrange("p (a b) -> p a b", b=128),
                        Dlf[:].rearrange("p (a b) -> p a b", b=128), float(k),
                        wex[4 + k][:, :, None].to_broadcast([128, 16, 128]),
                        ALU.is_equal, ALU.mult)
                    v.drain()
                    inst = v.tensor_tensor(Mf[:], Mf[:], pA[:], ALU.add)
                    v.drain()
                    inst.then_inc(ch.h, 1)
                v.wait_ge(ch.h, it + 23)
                v.tensor_tensor(Dfs[:].rearrange("p (a b) -> p a b", b=128),
                                Cbs[:, bass.ts(c, 16)][:, :, None]
                                .to_broadcast([5, 16, 128]),
                                stage5b[:].rearrange("p (a b) -> p a b", b=128),
                                ALU.subtract)
                v.drain()
                v.scalar_tensor_tensor(
                    Mfs[:].rearrange("p (a b) -> p a b", b=128),
                    Dfs[:].rearrange("p (a b) -> p a b", b=128), 0.0,
                    wex[4][0:5, :, None].to_broadcast([5, 16, 128]),
                    ALU.is_equal, ALU.mult)
                v.drain()
                for k in range(1, 4):
                    v.scalar_tensor_tensor(
                        tmp5[:].rearrange("p (a b) -> p a b", b=128),
                        Dfs[:].rearrange("p (a b) -> p a b", b=128), float(k),
                        wex[4 + k][0:5, :, None].to_broadcast([5, 16, 128]),
                        ALU.is_equal, ALU.mult)
                    v.drain()
                    v.tensor_tensor(Mfs[:], Mfs[:], tmp5[:], ALU.add)
                    v.drain()
                v.sem_inc(ch.h, 1)
                v.wait_ge(ch.h, it + 26)
                for c16 in range(16):
                    inst = v.tensor_tensor(
                        rstg[(c16 % 2) * 64:(c16 % 2) * 64 + 64, c16 // 2, :],
                        rstg[(c16 % 2) * 64:(c16 % 2) * 64 + 64, c16 // 2, :],
                        pO[:, c16 * 128:(c16 + 1) * 128], ALU.add)
                v.drain()
                inst.then_inc(ch.h, 1)

        @block.gpsimd
        def _(g, q0=qg.n):
            with g.Fori(0, NC) as c:
                it = bgt + c * DGT
                g.wait_ge(ch.h, it + 1)
                self.dstep(g, qg, ch, Vtmp[:], V_tm[:, c, :],
                           qv=q0 + 16 * (8 * c + 1))
                self.dstep(g, qg, ch, Vsb[0:127, :], Vtmp[1:128, :],
                           qv=q0 + 16 * (8 * c + 2))
                with g.If(c < NC - 1):
                    ccp = nc.s_assert_le(c + 1, NC - 1)
                    g.dma_start(out=Vsb[127:128, :], in_=V_tm[0:1, ccp, :]
                                ).then_inc(qg.h, 16)
                with g.Else():
                    g.dma_start(out=Vsb[127:128, :], in_=vshfix[:]
                                ).then_inc(qg.h, 16)
                g.wait_ge(qg.h, q0 + 16 * (8 * c + 3))
                g.sem_inc(ch.h, 1)
                self.dstep(g, qg, ch, sfs[:], d["sind"][0, c],
                           qv=q0 + 16 * (8 * c + 4))
                self.dstep(g, qg, ch, sbs[:], d["sind"][1, c],
                           qv=q0 + 16 * (8 * c + 5))
                self.dstep(g, qg, ch, stage5f[:],
                           d["csrd"][0][None, :, bass.ts(c, 128)]
                           .to_broadcast([5, 16, 128]),
                           qv=q0 + 16 * (8 * c + 6))
                self.dstep(g, qg, ch, stage5b[:],
                           d["csrd"][1][None, :, bass.ts(c, 128)]
                           .to_broadcast([5, 16, 128]),
                           qv=q0 + 16 * (8 * c + 7))
                g.wait_ge(ch.h, it + 27)
                self.dstep(g, qg, ch, dst_r[:, :, bass.ts(c, 128)], rstg[:],
                           qv=q0 + 16 * (8 * c + 8))

        @block.tensor
        def _(te):
            with te.Fori(0, NC) as c:
                it = bgt + c * DGT
                te.wait_ge(ch.h, it + 8)
                for q in range(4):
                    inst = te.matmul(pA[:, q * 512:(q + 1) * 512],
                                     self.ones_l[:, 0:128],
                                     stage5f[0:1, q * 512:(q + 1) * 512],
                                     start=True, stop=True)
                inst.then_inc(ch.h, 1)
                te.wait_ge(ch.h, it + 15)
                for q in range(4):
                    inst = te.matmul(pO[:, q * 512:(q + 1) * 512], zg[:, 0:64], zg[:],
                                     start=True, stop=False)
                inst.then_inc(ch.h, 1)
                te.wait_ge(ch.h, it + 16)
                for c16 in range(16):
                    te.matmul(pO[:, c16 * 128:(c16 + 1) * 128],
                              Vsf[:, c16 * 64:(c16 + 1) * 64],
                              Mf[:, c16 * 128:(c16 + 1) * 128],
                              start=False, stop=False)
                    inst = te.matmul(pO[:, c16 * 128:(c16 + 1) * 128],
                                     sfs[:, c16 * 64:(c16 + 1) * 64],
                                     Mfs[:, c16 * 128:(c16 + 1) * 128],
                                     start=False, stop=False)
                inst.then_inc(ch.h, 1)
                te.wait_ge(ch.h, it + 17)
                for q in range(4):
                    inst = te.matmul(pA[:, q * 512:(q + 1) * 512],
                                     self.ones_l[:, 0:128],
                                     stage5b[0:1, q * 512:(q + 1) * 512],
                                     start=True, stop=True)
                inst.then_inc(ch.h, 1)
                te.wait_ge(ch.h, it + 24)
                for c16 in range(16):
                    te.matmul(pO[:, c16 * 128:(c16 + 1) * 128],
                              Vsb[:, c16 * 64:(c16 + 1) * 64],
                              Mf[:, c16 * 128:(c16 + 1) * 128],
                              start=False, stop=False)
                    inst = te.matmul(pO[:, c16 * 128:(c16 + 1) * 128],
                                     sbs[:, c16 * 64:(c16 + 1) * 64],
                                     Mfs[:, c16 * 128:(c16 + 1) * 128],
                                     start=False, stop=False)
                with te.If(c == 0):
                    for c16 in range(16):
                        te.matmul(pO[:, c16 * 128:c16 * 128 + 1],
                                  r1t[0:1, c16 * 64:(c16 + 1) * 64],
                                  e0sw[:, c16:c16 + 1],
                                  start=False, stop=False)
                for q in range(4):
                    inst = te.matmul(pO[:, q * 512:(q + 1) * 512], zg[:, 0:64],
                                     zg[:], start=False, stop=True)
                inst.then_inc(ch.h, 1)

        @block.sync
        def _(sy, q0=qsy.n):
            with sy.Fori(0, NC) as c:
                it = bgt + c * DGT
                sy.wait_ge(ch.h, it + 25)
                self.dstep(sy, qsy, ch, rstg[:], src_r[:, :, bass.ts(c, 128)],
                           qv=q0 + 16 * (c + 1))

        qsy.n += 16 * NC
        qg.n += 112 * NC + 16 * NC
        ch.n = bgt + NC * DGT
        esG.close()
        es.close()
        esA.close()
        return ch


def _prep_inputs(inputs, T, n_cores):
    f32 = np.float32
    x = inputs["L0_attn_a_x"]
    per_core = []
    wq = np.ascontiguousarray(inputs["L0_attn_a_Wq"].reshape(D, D))
    wk = np.ascontiguousarray(inputs["L0_attn_a_Wk"].reshape(D, D))
    wv = np.ascontiguousarray(inputs["L0_attn_a_Wv"].reshape(D, D))
    wo = np.ascontiguousarray(inputs["L0_attn_a_Wo"].reshape(D, D))
    bqkvo = np.concatenate([inputs["L0_attn_a_bq"].reshape(-1),
                            inputs["L0_attn_a_bk"].reshape(-1),
                            inputs["L0_attn_a_bv"].reshape(-1),
                            inputs["L0_attn_a_bo"].reshape(-1)])[None, :]
    bwk = np.ascontiguousarray(inputs["L0_attn_b_Wk"])
    bwv = np.ascontiguousarray(inputs["L0_attn_b_Wv"])
    bbkv = np.concatenate([inputs["L0_attn_b_bk"].reshape(-1),
                           inputs["L0_attn_b_bv"].reshape(-1)])[None, :]
    rhw = inputs["L0_attn_b_ReadingHead"].reshape(1, D)
    bww = np.ascontiguousarray(inputs["L0_attn_b_bidirection_weight"].T).reshape(1, -1)
    gwt = np.ascontiguousarray(inputs["L0_ffn_gate_w"].T)
    gbr = inputs["L0_ffn_gate_b"][None, :]
    w1c = np.concatenate([inputs["L0_ffn_w1"], inputs["L0_ffn_sw1"][None]], 0)
    w3c = np.concatenate([inputs["L0_ffn_w3"], inputs["L0_ffn_sw3"][None]], 0)
    w2c = np.concatenate([inputs["L0_ffn_w2"], inputs["L0_ffn_sw2"][None]], 0)
    b1c = np.concatenate([inputs["L0_ffn_b1"].reshape(-1),
                          inputs["L0_ffn_sb1"].reshape(-1)])[None, :]
    b3c = np.concatenate([inputs["L0_ffn_b3"].reshape(-1),
                          inputs["L0_ffn_sb3"].reshape(-1)])[None, :]
    b2c = np.concatenate([inputs["L0_ffn_b2"].reshape(-1),
                          inputs["L0_ffn_sb2"].reshape(-1)])[None, :]
    shared = dict(wq=wq, wk=wk, wv=wv, wo=wo, bqkvo=bqkvo, bwk=bwk, bwv=bwv,
                  bbkv=bbkv, rhw=rhw, bww=bww, gwt=gwt, gbr=gbr,
                  w1c=w1c, w3c=w3c, w2c=w2c, b1c=b1c, b3c=b3c, b2c=b2c)
    shared = {k: np.ascontiguousarray(v, dtype=f32) for k, v in shared.items()}
    for core in range(n_cores):
        b, half = core // 2, core % 2
        xin = np.ascontiguousarray(x[b, half * T:(half + 1) * T, :].T, dtype=f32)
        m = np.zeros((1, 16), f32)
        m[0, 0] = float(half)
        m[0, 1] = 1.0 - float(half)
        per_core.append(dict(shared, xin=xin, meta=m))
    return per_core


def kernel(**inputs):
    T = 4096
    n_cores = 8
    b = B(T, n_cores)
    nc = b.build()
    in_maps = _prep_inputs(inputs, T, n_cores)
    res = run_bass_kernel_spmd(nc, in_maps, core_ids=list(range(n_cores)))
    x = inputs["L0_attn_a_x"]
    Bsz, S, _ = x.shape
    out = np.empty((Bsz, S, D), np.float32)
    for core in range(n_cores):
        bb, half = core // 2, core % 2
        out[bb, half * T:(half + 1) * T, :] = np.asarray(res.results[core]["out"]).T
    return out


# version 03
# speedup vs baseline: 1.3153x; 1.0270x over previous
import numpy as np
import ml_dtypes
from contextlib import ExitStack

import concourse.bass as bass
from concourse.bacc import Bacc
import concourse.mybir as mybir
from concourse.bass_utils import run_bass_kernel_spmd

F32 = mybir.dt.float32
BF16 = mybir.dt.bfloat16
AF = mybir.ActivationFunctionType
ALU = mybir.AluOpType

D = 1024
KD = 8
NH = 16
HDM = 64
HF = 1408
KHF = 11
NE = 9
LN_EPS = 1e-5
ATT_EPS = 1e-6


class Chain:
    def __init__(self, nc, es, name):
        self.h = es.enter_context(nc.semaphore(name))
        self.n = 0


class B:
    def __init__(self, T, n_cores):
        self.T = T
        self.n_cores = n_cores
        self.pairs = [[2 * i, 2 * i + 1] for i in range(max(1, n_cores // 2))]
        self.nc = Bacc("TRN2", target_bir_lowering=False)
        self.es = ExitStack()
        self.dr = {}

    def sb(self, name, shape, dt=F32):
        return self.es.enter_context(self.nc.sbuf_tensor(name, shape, dt))

    def chain(self, name):
        return Chain(self.nc, self.es, name)

    def din(self, n, s, dt=F32):
        self.dr[n] = self.nc.dram_tensor(n, s, dt, kind="ExternalInput").ap()
        return self.dr[n]

    def dint(self, n, s, dt=F32):
        self.dr[n] = self.nc.dram_tensor(n, s, dt, kind="Internal").ap()
        return self.dr[n]

    def dstep(self, eng, q, ch, out, in_, qv=None):
        eng.dma_start(out=out, in_=in_).then_inc(q.h, 16)
        if qv is None:
            q.n += 16
            qv = q.n
        eng.wait_ge(q.h, qv)
        eng.sem_inc(ch.h, 1)

    def build(self):
        nc, T = self.nc, self.T
        self.din("xin", [D, T])
        self.out = nc.dram_tensor("out", [D, T], F32, kind="ExternalOutput").ap()
        self.din("meta", [1, 16])
        for n in ("wq", "wk", "wv", "wo", "bwk", "bwv"):
            self.din(n, [D, D])
        self.din("bqkvo", [1, 4 * D])
        self.din("bbkv", [1, 2 * D])
        self.din("rhw", [1, D])
        self.din("bww", [1, 8 * NH])
        self.din("gwt", [D, 8])
        self.din("gbr", [1, 8])
        self.din("w1c", [NE, D, HF]); self.din("w3c", [NE, D, HF])
        self.din("w2c", [NE, HF, D])
        self.din("b1c", [1, NE * HF]); self.din("b3c", [1, NE * HF])
        self.din("b2c", [1, NE * D])

        self.dint("rA", [D, T])
        self.dint("rB", [D, T])
        self.dint("hsd", [KD, 128, T], BF16)
        self.dint("zqfd", [KD, 128, T], BF16)
        self.dint("w1b", [NE, D, HF], BF16)
        self.dint("w3b", [NE, D, HF], BF16)
        self.dint("w2b", [NE, HF, D], BF16)
        self.dint("rwd", [NE, T])
        self.dint("kvx_i", [65, D]); self.dint("kvx_o", [65, D])
        NC = T // 128
        self.dint("r1x_i", [1, D]); self.dint("r1x_o", [2, D])
        self.dint("r2f_i", [4, D]); self.dint("r2f_o", [8, D])
        self.dint("r2b_i", [4, D]); self.dint("r2b_o", [8, D])
        self.dint("sind", [2, NC, 5, NH * HDM], BF16)
        self.dint("csrd", [2, 16, T], BF16)
        self.dint("flA", [16, NC * 4], BF16)
        self.dint("flB", [16, NC * 4], BF16)
        self.dint("flE", [1, NC * 16], BF16)

        self.ones_l = self.sb("ones_l", [1, 512], BF16)
        self.ones_c = self.sb("ones_c", [128, 1], BF16)
        self.ident = self.sb("ident", [128, 128], BF16)
        self.metat = self.sb("metat", [1, 16], F32)

        with nc.Block() as block:
            self.emit_setup(block)
            prev = self.setup_ch
            prev = self.emit_ln(block, self.dr["xin"], "n1", prev)
            prev = self.emit_attn_a(block, self.dr["xin"], self.dr["rA"], prev)
            prev = self.emit_ln(block, self.dr["rA"], "n2", prev)
            prev = self.emit_ffn(block, self.dr["rA"], self.dr["rB"], 0, prev)
            prev = self.emit_ln(block, self.dr["rB"], "n3", prev)
            prev = self.emit_attn_b(block, self.dr["rB"], self.dr["rA"], prev)
            prev = self.emit_ln(block, self.dr["rA"], "n4", prev)
            prev = self.emit_ffn(block, self.dr["rA"], self.out, 1, prev)
            self.final_ch = prev

        self.es.close()
        nc.finalize()
        return nc

    def emit_setup(self, block):
        nc = self.nc
        ch = self.chain("setup")
        self.setup_ch = ch
        self.wcast = self.chain("wcast")
        qsy = self.chain("su_qsy")
        d = self.dr

        @block.sync
        def _(sy):
            self.dstep(sy, qsy, ch, self.metat[:], d["meta"][:])
        ch.n += 1

        @block.gpsimd
        def _(g):
            g.iota(self.ident[:], pattern=[[-1, 128]], base=0, channel_multiplier=1,
                   allow_small_or_imprecise_dtypes=True).then_inc(ch.h, 1)
        ch.n += 1

        @block.vector
        def _(v):
            v.memset(self.ones_l[:], 1.0)
            v.memset(self.ones_c[:], 1.0)
            v.drain()
            v.wait_ge(ch.h, ch.n)
            v.tensor_scalar(self.ident[:], self.ident[:], 0.0, 1.0,
                            ALU.is_equal, ALU.mult)
            v.drain().then_inc(ch.h, 1)
        ch.n += 1

        @block.gpsimd
        def _(g):
            for e in range(NE):
                for a, b_ in (("w1c", "w1b"), ("w3c", "w3b"), ("w2c", "w2b")):
                    g.dma_start(out=d[b_][e], in_=d[a][e]).then_inc(self.wcast.h, 16)
                    self.wcast.n += 16

    def emit_ln(self, block, src, tag, prev):
        nc, T = self.nc, self.T
        TC = T // 512
        es = ExitStack()
        sbl = lambda n, sh, dt=F32: es.enter_context(nc.sbuf_tensor(tag + n, sh, dt))
        psl = lambda n, sh, dt=F32: es.enter_context(nc.psum_tensor(tag + n, sh, dt))
        d = self.dr

        xc = sbl("xc", [128, KD, 512])
        xb = sbl("xb", [128, KD, 512], BF16)
        sq = sbl("sq", [128, KD, 512], BF16)
        hs = sbl("hs", [128, KD, 512], BF16)
        rows = sbl("rows", [1, 6, 512])
        rbf = sbl("rbf", [1, 2, 512], BF16)
        pstat = psl("pst", [1, 2, 512])
        pbc = psl("pbc", [128, 2, 512])

        ch = self.chain(tag + "ch")
        qsy = self.chain(tag + "qsy")
        qg = self.chain(tag + "qg")
        src_r = src.rearrange("(k p) t -> p k t", p=128)
        hsd_r = d["hsd"].rearrange("k p t -> p k t")
        pn = prev.n
        DL = 10

        @block.sync
        def _(sy):
            sy.wait_ge(prev.h, pn)
            with sy.Fori(0, TC) as i:
                sy.wait_ge(ch.h, i * DL)
                self.dstep(sy, qsy, ch, xc[:], src_r[:, :, bass.ts(i, 512)],
                           qv=16 * (i + 1))

        @block.vector
        def _(v):
            with v.Fori(0, TC) as i:
                it = i * DL
                v.wait_ge(ch.h, it + 1)
                for k in range(KD):
                    inst = v.tensor_scalar(xb[:, k], xc[:, k], 1.0, None, ALU.mult)
                v.drain()
                inst.then_inc(ch.h, 1)
                v.wait_ge(ch.h, it + 4)
                v.tensor_scalar(rows[:, 0], pstat[:, 0], 1.0 / D, None, ALU.mult)
                v.tensor_scalar(rows[:, 1], pstat[:, 1], 1.0 / D, None, ALU.mult)
                v.drain()
                v.tensor_tensor(rows[:, 2], rows[:, 0], rows[:, 0], ALU.mult)
                v.drain()
                v.tensor_tensor(rows[:, 3], rows[:, 1], rows[:, 2], ALU.subtract)
                v.drain()
                inst = v.tensor_scalar(rows[:, 3], rows[:, 3], LN_EPS, None, ALU.add)
                v.drain()
                inst.then_inc(ch.h, 1)
                v.wait_ge(ch.h, it + 6)
                v.reciprocal(rows[:, 4], rows[:, 4])
                v.drain()
                v.tensor_tensor(rows[:, 5], rows[:, 0], rows[:, 4], ALU.mult)
                v.drain()
                v.tensor_scalar(rbf[:, 0], rows[:, 4], 1.0, None, ALU.mult)
                inst = v.tensor_scalar(rbf[:, 1], rows[:, 5], 1.0, None, ALU.mult)
                v.drain()
                inst.then_inc(ch.h, 1)
                v.wait_ge(ch.h, it + 8)
                for k in range(KD):
                    v.tensor_tensor(hs[:, k], xb[:, k], pbc[:, 0], ALU.mult)
                v.drain()
                for k in range(KD):
                    inst = v.tensor_tensor(hs[:, k], hs[:, k], pbc[:, 1], ALU.subtract)
                v.drain()
                inst.then_inc(ch.h, 1)

        @block.scalar
        def _(sc):
            with sc.Fori(0, TC) as i:
                it = i * DL
                sc.wait_ge(ch.h, it + 2)
                for k in range(KD):
                    inst = sc.activation(sq[:, k], xb[:, k], AF.Square)
                sc.drain()
                inst.then_inc(ch.h, 1)
                sc.wait_ge(ch.h, it + 5)
                inst = sc.activation(rows[:, 4], rows[:, 3], AF.Sqrt)
                sc.drain()
                inst.then_inc(ch.h, 1)

        @block.tensor
        def _(te):
            with te.Fori(0, TC) as i:
                it = i * DL
                te.wait_ge(ch.h, it + 3)
                for k in range(KD):
                    te.matmul(pstat[:, 0], self.ones_c[:], xb[:, k],
                              start=(k == 0), stop=(k == KD - 1))
                for k in range(KD):
                    inst = te.matmul(pstat[:, 1], self.ones_c[:], sq[:, k],
                                     start=(k == 0), stop=(k == KD - 1))
                inst.then_inc(ch.h, 1)
                te.wait_ge(ch.h, it + 7)
                te.matmul(pbc[:, 0], self.ones_l[:, 0:128], rbf[:, 0],
                          start=True, stop=True)
                inst = te.matmul(pbc[:, 1], self.ones_l[:, 0:128], rbf[:, 1],
                                 start=True, stop=True)
                inst.then_inc(ch.h, 1)

        @block.gpsimd
        def _(g):
            with g.Fori(0, TC) as i:
                it = i * DL
                g.wait_ge(ch.h, it + 9)
                self.dstep(g, qg, ch, hsd_r[:, :, bass.ts(i, 512)], hs[:],
                           qv=16 * (i + 1))

        ch.n = TC * DL
        qsy.n = 16 * TC
        qg.n = 16 * TC
        es.close()
        return ch

    def emit_attn_a(self, block, xsrc, rdst, prev):
        nc, T = self.nc, self.T
        TT = T // 128
        TC = T // 512
        es = ExitStack()
        sbl = lambda n, sh, dt=F32: es.enter_context(nc.sbuf_tensor("aa" + n, sh, dt))
        psl = lambda n, sh, dt=F32: es.enter_context(nc.psum_tensor("aa" + n, sh, dt))
        d = self.dr

        wqt = sbl("wqt", [128, KD, D], BF16)
        wkt = sbl("wkt", [128, KD, D], BF16)
        wvt = sbl("wvt", [128, KD, D], BF16)
        wot = sbl("wot", [128, KD, D], BF16)
        bflat = sbl("bflat", [1, 4 * D], BF16)
        hst = sbl("hst", [128, KD, 128], BF16)
        qf = sbl("qf", [128, TT, D], BF16)
        kft = sbl("kft", [128, D], BF16)
        vft = sbl("vft", [128, D], BF16)
        etmp = sbl("etmp", [128, 2, 512])
        ebf = sbl("ebf", [128, 2, 512], BF16)
        kvp = sbl("kvp", [65, D])
        kvbf = sbl("kvbf", [128, D], BF16)
        ksbc = sbl("ksbc", [128, D], BF16)
        zt = sbl("zt", [128, D])
        zr = sbl("zr", [128, NH])
        qstg = sbl("qstg", [128, D], BF16)
        ztile = sbl("ztile", [128, KD, 128], BF16)
        zqc = sbl("zqc", [128, KD, 512], BF16)
        xc2 = sbl("xc2", [128, KD, 512])
        ybuf = sbl("ybuf", [128, KD, 512], BF16)
        zerob = sbl("zerob", [128, 512], BF16)

        es1 = ExitStack()
        pproj = es1.enter_context(nc.psum_tensor("aapp", [128, 2, 512], F32))
        pkv = es1.enter_context(nc.psum_tensor("aapkv", [64, D], F32))
        pksum = es1.enter_context(nc.psum_tensor("aapks", [1, D], F32))

        ch = self.chain("aach")
        qsy = self.chain("aaqsy")
        qg = self.chain("aaqg")
        ccs = self.chain("aacc")
        pn = prev.n
        hsd_r = d["hsd"].rearrange("k p t -> p k t")
        xsrc_r = xsrc.rearrange("(k p) t -> p k t", p=128)
        rdst_r = rdst.rearrange("(k p) t -> p k t", p=128)
        zq_r = d["zqfd"].rearrange("k p t -> p k t")

        @block.gpsimd
        def _(g):
            g.wait_ge(prev.h, pn)
            for nm, t_ in (("wq", wqt), ("wk", wkt), ("wv", wvt), ("wo", wot)):
                self.dstep(g, qg, ch, t_[:],
                           d[nm].rearrange("(k p) n -> p k n", p=128))
                ch.n += 1
            self.dstep(g, qg, ch, bflat[:], d["bqkvo"][:])
            ch.n += 1

        @block.vector
        def _(v):
            v.wait_ge(ch.h, ch.n)
            v.memset(zerob[:], 0.0)
            v.drain().then_inc(ch.h, 1)
        ch.n += 1

        @block.tensor
        def _(te):
            te.wait_ge(ch.h, ch.n)
            for hf in range(2):
                te.matmul(pkv[:, hf * 512:(hf + 1) * 512], zerob[:, 0:64],
                          zerob[:], start=True, stop=False)
            for hf in range(2):
                inst = te.matmul(pksum[:, hf * 512:(hf + 1) * 512], self.ones_c[:],
                                 zerob[:], start=True, stop=False)
            inst.then_inc(ch.h, 1)
        ch.n += 1

        base = ch.n
        DEL = 14

        @block.sync
        def _(sy):
            with sy.Fori(0, TT) as i:
                sy.wait_ge(ch.h, base + i * DEL)
                self.dstep(sy, qsy, ch, hst[:], hsd_r[:, :, bass.ts(i, 128)],
                           qv=16 * (i + 1))

        @block.tensor
        def _(te):
            with te.Fori(0, TT) as i:
                it = base + i * DEL
                for off, wt, boff in ((1, wqt, 0), (6, wkt, D), (11, wvt, 2 * D)):
                    te.wait_ge(ch.h, it + off)
                    for hf in range(2):
                        for k in range(KD):
                            te.matmul(pproj[:, hf], hst[:, k],
                                      wt[:, k, hf * 512:(hf + 1) * 512],
                                      start=(k == 0), stop=False)
                        inst = te.matmul(pproj[:, hf], self.ones_l[:, 0:128],
                                         bflat[:, boff + hf * 512:boff + (hf + 1) * 512],
                                         start=False, stop=True)
                    inst.then_inc(ch.h, 1)
                te.wait_ge(ch.h, it + 13)
                for h in range(NH):
                    te.matmul(pkv[:, h * 64:(h + 1) * 64], kft[:, h * 64:(h + 1) * 64],
                              vft[:, h * 64:(h + 1) * 64], start=False, stop=False)
                for hf in range(2):
                    inst = te.matmul(pksum[:, hf * 512:(hf + 1) * 512], self.ones_c[:],
                                     kft[:, hf * 512:(hf + 1) * 512],
                                     start=False, stop=False)
                inst.then_inc(ch.h, 1)

        @block.scalar
        def _(sc):
            with sc.Fori(0, TT) as i:
                it = base + i * DEL
                sc.wait_ge(ch.h, it + 2)
                for hf in range(2):
                    inst = sc.activation(qf[:, i, hf * 512:(hf + 1) * 512],
                                         pproj[:, hf], AF.Relu)
                sc.drain()
                inst.then_inc(ch.h, 1)
                sc.wait_ge(ch.h, it + 4)
                for hf in range(2):
                    inst = sc.activation(ebf[:, hf], etmp[:, hf], AF.Exp)
                sc.drain()
                inst.then_inc(ch.h, 1)
                sc.wait_ge(ch.h, it + 7)
                for hf in range(2):
                    inst = sc.activation(kft[:, hf * 512:(hf + 1) * 512],
                                         pproj[:, hf], AF.Relu)
                sc.drain()
                inst.then_inc(ch.h, 1)
                sc.wait_ge(ch.h, it + 9)
                for hf in range(2):
                    inst = sc.activation(ebf[:, hf], etmp[:, hf], AF.Exp)
                sc.drain()
                inst.then_inc(ch.h, 1)

        @block.vector
        def _(v):
            with v.Fori(0, TT) as i:
                it = base + i * DEL
                v.wait_ge(ch.h, it + 3)
                for hf in range(2):
                    inst = v.tensor_scalar(etmp[:, hf], pproj[:, hf], 0.0, None, ALU.min)
                v.drain()
                inst.then_inc(ch.h, 1)
                v.wait_ge(ch.h, it + 5)
                for hf in range(2):
                    inst = v.tensor_tensor(qf[:, i, hf * 512:(hf + 1) * 512],
                                           qf[:, i, hf * 512:(hf + 1) * 512],
                                           ebf[:, hf], ALU.add)
                v.drain()
                inst.then_inc(ch.h, 1)
                v.wait_ge(ch.h, it + 8)
                for hf in range(2):
                    inst = v.tensor_scalar(etmp[:, hf], pproj[:, hf], 0.0, None, ALU.min)
                v.drain()
                inst.then_inc(ch.h, 1)
                v.wait_ge(ch.h, it + 10)
                for hf in range(2):
                    inst = v.tensor_tensor(kft[:, hf * 512:(hf + 1) * 512],
                                           kft[:, hf * 512:(hf + 1) * 512],
                                           ebf[:, hf], ALU.add)
                v.drain()
                inst.then_inc(ch.h, 1)
                v.wait_ge(ch.h, it + 12)
                for hf in range(2):
                    inst = v.tensor_scalar(vft[:, hf * 512:(hf + 1) * 512],
                                           pproj[:, hf], 1.0, None, ALU.mult)
                v.drain()
                inst.then_inc(ch.h, 1)

        ch.n = base + TT * DEL
        qsy.n = 16 * TT

        @block.tensor
        def _(te):
            te.wait_ge(ch.h, ch.n)
            for hf in range(2):
                te.matmul(pkv[:, hf * 512:(hf + 1) * 512], zerob[:, 0:64],
                          zerob[:], start=False, stop=True)
            for hf in range(2):
                inst = te.matmul(pksum[:, hf * 512:(hf + 1) * 512], self.ones_c[:],
                                 zerob[:], start=False, stop=True)
            inst.then_inc(ch.h, 1)
        ch.n += 1

        @block.vector
        def _(v):
            v.wait_ge(ch.h, ch.n)
            v.tensor_scalar(kvp[0:64, :], pkv[:], 1.0, None, ALU.mult)
            inst = v.tensor_scalar(kvp[64:65, :], pksum[:], 1.0, None, ALU.mult)
            v.drain()
            inst.then_inc(ch.h, 1)
        ch.n += 1

        es1.close()
        pksb = psl("pksb", [128, 2, 512])
        ptr = psl("ptr", [128, KD, 128], BF16)
        pyt = psl("pyt", [128, 512])
        pout = psl("pout", [128, 512])

        n0 = ch.n

        @block.gpsimd
        def _(g):
            g.wait_ge(ch.h, n0)
            self.dstep(g, qg, ch, d["kvx_i"][:], kvp[:])
            g.collective_compute("AllReduce", ALU.add, self.pairs,
                                 [d["kvx_i"].opt()], [d["kvx_o"].opt()]
                                 ).then_inc(ccs.h, 1)
            g.wait_ge(ccs.h, 1)
            g.sem_inc(ch.h, 1)
            self.dstep(g, qg, ch, kvbf[0:64, :], d["kvx_o"][0:64, :])
            self.dstep(g, qg, ch, kvbf[64:128, :], d["kvx_o"][0:64, :])
            self.dstep(g, qg, ch, kvp[64:65, :], d["kvx_o"][64:65, :])
        ch.n += 5

        @block.vector
        def _(v):
            v.wait_ge(ch.h, ch.n)
            v.tensor_scalar(ebf[0:1, 0, :], kvp[64:65, 0:512], 1.0, None, ALU.mult)
            inst = v.tensor_scalar(ebf[0:1, 1, :], kvp[64:65, 512:1024], 1.0, None,
                                   ALU.mult)
            v.drain()
            inst.then_inc(ch.h, 1)
        ch.n += 1

        @block.tensor
        def _(te):
            te.wait_ge(ch.h, ch.n)
            te.matmul(pksb[:, 0], self.ones_l[:, 0:128], ebf[0:1, 0, :],
                      start=True, stop=True)
            inst = te.matmul(pksb[:, 1], self.ones_l[:, 0:128], ebf[0:1, 1, :],
                             start=True, stop=True)
            inst.then_inc(ch.h, 1)
        ch.n += 1

        @block.vector
        def _(v):
            v.wait_ge(ch.h, ch.n)
            v.tensor_scalar(ksbc[:, 0:512], pksb[:, 0], 1.0, None, ALU.mult)
            inst = v.tensor_scalar(ksbc[:, 512:1024], pksb[:, 1], 1.0, None, ALU.mult)
            v.drain()
            inst.then_inc(ch.h, 1)
        ch.n += 1

        basez = ch.n
        DZ = 2

        @block.vector
        def _(v):
            with v.Fori(0, TT) as i:
                it = basez + i * DZ
                v.wait_ge(ch.h, it)
                v.tensor_tensor(zt[:], qf[:, i, :], ksbc[:], ALU.mult)
                v.drain()
                inst = v.tensor_reduce(zr[:, :, None],
                                       zt[:].rearrange("p (h e) -> p h e", h=NH),
                                       mybir.AxisListType.X, ALU.add)
                v.drain()
                inst.then_inc(ch.h, 1)
                v.tensor_scalar(zr[:], zr[:], ATT_EPS, None, ALU.add)
                v.drain()
                v.reciprocal(zr[:], zr[:])
                v.drain()
                inst = v.tensor_tensor(qf[:, i, :].rearrange("p (h e) -> p h e", h=NH),
                                       qf[:, i, :].rearrange("p (h e) -> p h e", h=NH),
                                       zr[:, :, None].to_broadcast([128, NH, HDM]),
                                       ALU.mult)
                v.drain()
                inst.then_inc(ch.h, 1)

        ch.n = basez + TT * DZ

        baset = ch.n
        DT_ = 4

        @block.vector
        def _(v):
            with v.Fori(0, TT) as i:
                it = baset + i * DT_
                v.wait_ge(ch.h, it)
                inst = v.tensor_scalar(qstg[:], qf[:, i, :], 1.0, None, ALU.mult)
                v.drain()
                inst.then_inc(ch.h, 1)
                v.wait_ge(ch.h, it + 2)
                for k in range(KD):
                    inst = v.tensor_scalar(ztile[:, k], ptr[:, k], 1.0, None, ALU.mult)
                v.drain()
                inst.then_inc(ch.h, 1)

        @block.tensor
        def _(te):
            with te.Fori(0, TT) as i:
                it = baset + i * DT_
                te.wait_ge(ch.h, it + 1)
                for k in range(KD):
                    inst = te.transpose(ptr[:, k], qstg[:, k * 128:(k + 1) * 128],
                                        self.ident[:])
                inst.then_inc(ch.h, 1)

        @block.gpsimd
        def _(g, q0=qg.n):
            with g.Fori(0, TT) as i:
                it = baset + i * DT_
                g.wait_ge(ch.h, it + 3)
                self.dstep(g, qg, ch, zq_r[:, :, bass.ts(i, 128)], ztile[:],
                           qv=q0 + 16 * (i + 1))

        qg.n += 16 * TT
        ch.n = baset + TT * DT_

        basey = ch.n
        DY = 35

        @block.sync
        def _(sy, q0=qsy.n):
            with sy.Fori(0, TC) as i:
                sy.wait_ge(ch.h, basey + i * DY)
                self.dstep(sy, qsy, ch, zqc[:], zq_r[:, :, bass.ts(i, 512)],
                           qv=q0 + 16 * (2 * i + 1))
                self.dstep(sy, qsy, ch, xc2[:], xsrc_r[:, :, bass.ts(i, 512)],
                           qv=q0 + 16 * (2 * i + 2))

        @block.tensor
        def _(te):
            with te.Fori(0, TC) as i:
                it = basey + i * DY
                for hm in range(KD):
                    te.wait_ge(ch.h, it + 2 + 2 * hm)
                    for sub in range(2):
                        h = 2 * hm + sub
                        hb = (h % 2) * 64
                        inst = te.matmul(pyt[sub * 64:(sub + 1) * 64, :],
                                         kvbf[hb:hb + 64, h * 64:(h + 1) * 64],
                                         zqc[hb:hb + 64, h // 2, :],
                                         start=True, stop=True)
                    inst.then_inc(ch.h, 1)
                for dt in range(KD):
                    te.wait_ge(ch.h, it + 18 + 2 * dt)
                    for k in range(KD):
                        te.matmul(pout[:], wot[:, k, dt * 128:(dt + 1) * 128],
                                  ybuf[:, k, :], start=(k == 0), stop=False)
                    inst = te.matmul(pout[:],
                                     bflat[:, 3 * D + dt * 128:3 * D + dt * 128 + 128],
                                     self.ones_l[:], start=False, stop=True)
                    inst.then_inc(ch.h, 1)

        @block.vector
        def _(v):
            with v.Fori(0, TC) as i:
                it = basey + i * DY
                for hm in range(KD):
                    v.wait_ge(ch.h, it + 3 + 2 * hm)
                    inst = v.tensor_scalar(ybuf[:, hm, :], pyt[:], 1.0, None, ALU.mult)
                    v.drain()
                    inst.then_inc(ch.h, 1)
                for dt in range(KD):
                    v.wait_ge(ch.h, it + 19 + 2 * dt)
                    inst = v.tensor_tensor(xc2[:, dt], pout[:], xc2[:, dt], ALU.add)
                    v.drain()
                    inst.then_inc(ch.h, 1)

        @block.gpsimd
        def _(g, q0=qg.n):
            with g.Fori(0, TC) as i:
                it = basey + i * DY
                g.wait_ge(ch.h, it + 34)
                self.dstep(g, qg, ch, rdst_r[:, :, bass.ts(i, 512)], xc2[:],
                           qv=q0 + 16 * (i + 1))

        qsy.n += 32 * TC
        qg.n += 16 * TC
        ch.n = basey + TC * DY
        es.close()
        return ch

    def emit_ffn(self, block, src, dst, call, prev):
        nc, T = self.nc, self.T
        TH = T // 2
        THC = TH // 512
        tg = f"f{call}"
        es = ExitStack()
        sbl = lambda n, sh, dt=F32: es.enter_context(nc.sbuf_tensor(tg + n, sh, dt))
        psl = lambda n, sh, dt=F32: es.enter_context(nc.psum_tensor(tg + n, sh, dt))
        d = self.dr

        gwtt = sbl("gwt", [128, KD, 8], BF16)
        gbbc = sbl("gbbc", [128, 8])
        b1s = sbl("b1s", [1, HF], BF16)
        b3s = sbl("b3s", [1, HF], BF16)
        b2s = sbl("b2s", [1, D], BF16)
        xst = sbl("xst", [128, KD, TH], BF16)
        w1t = sbl("w1t", [128, KD, HF], BF16)
        w3t = sbl("w3t", [128, KD, HF], BF16)
        w2t = sbl("w2t", [128, KHF, D], BF16)
        yac = sbl("yac", [128, KD, TH])
        hbuf = sbl("hbuf", [128, KHF, 512], BF16)
        ubuf = sbl("ubuf", [128, 512], BF16)
        tbuf = sbl("tbuf", [128, 512], BF16)
        lgb = sbl("lgb", [8, 512], BF16)
        sfx = sbl("sfx", [128, 4, 40])
        rwt = sbl("rwt", [128, 4, NE], BF16)
        rwev = sbl("rwev", [NE, 512])
        rwre = sbl("rwre", [1, TH])
        rwrb = sbl("rwrb", [1, TH], BF16)
        rwbcb = sbl("rwbcb", [128, 512], BF16)
        rc = sbl("rc", [128, KD, 128])

        pg = psl("pg", [8, 512])
        pboth = psl("pboth", [128, 544], BF16)
        ptt = pboth[:, 0:32].rearrange("p (a b) -> p a b", b=8)
        prw = pboth[0:NE, 32:544]
        ps1 = psl("ps1", [128, 2, 512])
        ps2 = psl("ps2", [128, 2, 512])
        pmix = psl("pmix", [128, 2, 512])
        prb = pmix[:, 0]

        ch = self.chain(tg + "ch")
        qsy = self.chain(tg + "qsy")
        qg = self.chain(tg + "qg")
        pn = prev.n
        src_r = src.rearrange("(k p) t -> p k t", p=128)
        dst_r = dst.rearrange("(k p) t -> p k t", p=128)
        hsd_r = d["hsd"].rearrange("k p t -> p k t")

        @block.gpsimd
        def _(g):
            g.wait_ge(prev.h, pn)
            if call == 0:
                g.wait_ge(self.wcast.h, self.wcast.n)
            self.dstep(g, qg, ch, gwtt[:],
                       d["gwt"].rearrange("(k p) e -> p k e", p=128))
            self.dstep(g, qg, ch, lgb[0:1, 0:8], d["gbr"][:])
        ch.n += 2

        @block.tensor
        def _(te):
            te.wait_ge(ch.h, ch.n)
            te.matmul(prb[:, 0:8], self.ones_l[:, 0:128], lgb[0:1, 0:8],
                      start=True, stop=True).then_inc(ch.h, 1)
        ch.n += 1

        @block.vector
        def _(v):
            v.wait_ge(ch.h, ch.n)
            v.tensor_scalar(gbbc[:], prb[:, 0:8], 1.0, None, ALU.mult)
            v.drain().then_inc(ch.h, 1)
        ch.n += 1

        for half in range(2):
            t0 = half * TH
            n0 = ch.n

            @block.gpsimd
            def _(g, t0=t0, n0=n0):
                g.wait_ge(ch.h, n0)
                self.dstep(g, qg, ch, xst[:], hsd_r[:, :, t0:t0 + TH])
            ch.n += 1

            n1 = ch.n

            @block.vector
            def _(v, n1=n1):
                v.wait_ge(ch.h, n1)
                for k in range(KD):
                    v.memset(yac[:, k], 0.0)
                v.drain()
                inst = v.memset(rwt[:, :, 8:9], 1.0)
                v.drain()
                inst.then_inc(ch.h, 1)
            ch.n += 1

            bg = ch.n
            DG = 9

            @block.tensor
            def _(te, bg=bg):
                with te.Fori(0, THC) as i:
                    it = bg + i * DG
                    te.wait_ge(ch.h, it)
                    for k in range(KD):
                        inst = te.matmul(pg[:], gwtt[:, k], xst[:, k, bass.ts(i, 512)],
                                         start=(k == 0), stop=(k == KD - 1))
                    inst.then_inc(ch.h, 1)
                    te.wait_ge(ch.h, it + 2)
                    for q in range(4):
                        inst = te.transpose(ptt[:, q], lgb[:, q * 128:(q + 1) * 128],
                                            self.ident[0:8, 0:8])
                    inst.then_inc(ch.h, 1)
                    te.wait_ge(ch.h, it + 6)
                    for q in range(4):
                        inst = te.transpose(prw[:, bass.ts(q, 128)], rwt[:, q, :],
                                            self.ident[:])
                    inst.then_inc(ch.h, 1)

            @block.scalar
            def _(sc, bg=bg):
                with sc.Fori(0, THC) as i:
                    it = bg + i * DG
                    sc.wait_ge(ch.h, it + 4)
                    for q in range(4):
                        inst = sc.activation(sfx[:, q, 2:10], sfx[:, q, 2:10], AF.Exp)
                    sc.drain()
                    inst.then_inc(ch.h, 1)

            @block.vector
            def _(v, bg=bg):
                with v.Fori(0, THC) as i:
                    it = bg + i * DG
                    v.wait_ge(ch.h, it + 1)
                    inst = v.tensor_scalar(lgb[:], pg[:], 1.0, None, ALU.mult)
                    v.drain()
                    inst.then_inc(ch.h, 1)
                    v.wait_ge(ch.h, it + 3)
                    for q in range(4):
                        v.tensor_reduce(sfx[:, q, 0:1], ptt[:, q],
                                        mybir.AxisListType.X, ALU.max)
                    v.drain()
                    for q in range(4):
                        inst = v.tensor_scalar(sfx[:, q, 2:10], ptt[:, q],
                                               sfx[:, q, 0:1], None, ALU.subtract)
                    v.drain()
                    inst.then_inc(ch.h, 1)
                    v.wait_ge(ch.h, it + 5)
                    for q in range(4):
                        v.tensor_reduce(sfx[:, q, 0:1], sfx[:, q, 2:10],
                                        mybir.AxisListType.X, ALU.add)
                    v.drain()
                    for q in range(4):
                        v.reciprocal(sfx[:, q, 0:1], sfx[:, q, 0:1])
                    v.drain()
                    for q in range(4):
                        v.tensor_scalar(sfx[:, q, 10:18], sfx[:, q, 2:10],
                                        sfx[:, q, 0:1], None, ALU.mult)
                    v.drain()
                    for q in range(4):
                        v.tensor_tensor(sfx[:, q, 18:26], sfx[:, q, 10:18],
                                        gbbc[:], ALU.add)
                    v.drain()
                    for q in range(4):
                        v.tensor_reduce(sfx[:, q, 1:2], sfx[:, q, 18:26],
                                        mybir.AxisListType.X, ALU.max)
                    v.drain()
                    for q in range(4):
                        v.tensor_scalar(sfx[:, q, 26:34], sfx[:, q, 18:26],
                                        sfx[:, q, 1:2], None, ALU.is_equal)
                    v.drain()
                    for q in range(4):
                        v.scalar_tensor_tensor(sfx[:, q, 26:34], sfx[:, q, 26:34],
                                               -1e9, sfx[:, q, 18:26],
                                               ALU.mult, ALU.add)
                    v.drain()
                    for q in range(4):
                        v.tensor_reduce(sfx[:, q, 1:2], sfx[:, q, 26:34],
                                        mybir.AxisListType.X, ALU.max)
                    v.drain()
                    for q in range(4):
                        v.tensor_scalar(sfx[:, q, 26:34], sfx[:, q, 18:26],
                                        sfx[:, q, 1:2], None, ALU.is_ge)
                    v.drain()
                    for q in range(4):
                        inst = v.tensor_tensor(rwt[:, q, 0:8], sfx[:, q, 10:18],
                                               sfx[:, q, 26:34], ALU.mult)
                    v.drain()
                    inst.then_inc(ch.h, 1)
                    v.wait_ge(ch.h, it + 7)
                    inst = v.tensor_scalar(rwev[:], prw[:], 1.0, None, ALU.mult)
                    v.drain()
                    inst.then_inc(ch.h, 1)

            @block.gpsimd
            def _(g, bg=bg, t0=t0, q0=qg.n):
                with g.Fori(0, THC) as i:
                    it = bg + i * DG
                    g.wait_ge(ch.h, it + 8)
                    self.dstep(g, qg, ch, d["rwd"][:, t0:t0 + TH][:, bass.ts(i, 512)],
                               rwev[:], qv=q0 + 16 * (i + 1))

            qg.n += 16 * THC
            ch.n = bg + THC * DG

            be = ch.n
            DTC = 3
            DE = 2 + THC * DTC

            @block.gpsimd
            def _(g, be=be, t0=t0, q0=qg.n):
                with g.Fori(0, NE) as e:
                    it = be + e * DE
                    g.wait_ge(ch.h, it)
                    qv = q0 + e * 112
                    g.dma_start(out=w1t[:],
                                in_=d["w1b"].rearrange("e (k p) h -> e p k h", p=128)[e]
                                ).then_inc(qg.h, 16)
                    g.dma_start(out=w3t[:],
                                in_=d["w3b"].rearrange("e (k p) h -> e p k h", p=128)[e]
                                ).then_inc(qg.h, 16)
                    g.dma_start(out=w2t[:],
                                in_=d["w2b"].rearrange("e (k p) h -> e p k h", p=128)[e]
                                ).then_inc(qg.h, 16)
                    g.dma_start(out=rwre[:], in_=d["rwd"][e, t0:t0 + TH][None, :]
                                ).then_inc(qg.h, 16)
                    g.dma_start(out=b1s[:], in_=d["b1c"][:, bass.ts(e, HF)]
                                ).then_inc(qg.h, 16)
                    g.dma_start(out=b3s[:], in_=d["b3c"][:, bass.ts(e, HF)]
                                ).then_inc(qg.h, 16)
                    g.dma_start(out=b2s[:], in_=d["b2c"][:, bass.ts(e, D)]
                                ).then_inc(qg.h, 16)
                    g.wait_ge(qg.h, qv + 112)
                    g.sem_inc(ch.h, 1)

            shm = self.chain(tg + f"hm{half}")
            sha_ = self.chain(tg + f"ha{half}")
            shd = self.chain(tg + f"hd{half}")
            she = self.chain(tg + f"he{half}")
            swm = self.chain(tg + f"wm{half}")
            swe = self.chain(tg + f"we{half}")

            @block.vector
            def _(v, be=be):
                with v.Fori(0, NE) as e:
                    it = be + e * DE
                    v.wait_ge(ch.h, it + 1)
                    inst = v.tensor_scalar(rwrb[:], rwre[:], 1.0, None, ALU.mult)
                    v.drain()
                    inst.then_inc(ch.h, 1)
                    with v.Fori(0, THC) as i:
                        jt = it + 2 + i * DTC
                        u11 = (e * THC + i) * KHF
                        u8 = (e * THC + i) * KD
                        v.wait_ge(ch.h, jt + 1)
                        inst = v.tensor_scalar(rwbcb[:], prb[:], 1.0, None, ALU.mult)
                        v.drain()
                        inst.then_inc(ch.h, 1)
                        for ht in range(KHF):
                            par = ht % 2
                            v.wait_ge(sha_.h, u11 + ht + 1)
                            v.wait_ge(shm.h, u11 + ht + 1)
                            if ht >= 1:
                                v.wait_ge(she.h, u11 + ht)
                            inst = v.tensor_tensor(tbuf[:], ps1[:, par], ubuf[:],
                                                   ALU.mult)
                            v.drain()
                            inst.then_inc(shd.h, 1)
                        for dt in range(KD):
                            par = dt % 2
                            v.wait_ge(swm.h, u8 + dt + 1)
                            v.tensor_tensor(ubuf[:], pmix[:, par], rwbcb[:], ALU.mult)
                            v.drain()
                            inst = v.tensor_tensor(yac[:, dt, bass.ts(i, 512)],
                                                   yac[:, dt, bass.ts(i, 512)],
                                                   ubuf[:], ALU.add)
                            v.drain()
                            inst.then_inc(swe.h, 1)
                            if dt == KD - 1:
                                v.sem_inc(ch.h, 1)

            @block.scalar
            def _(sc, be=be):
                with sc.Fori(0, NE) as e:
                    with sc.Fori(0, THC) as i:
                        u11 = (e * THC + i) * KHF
                        for ht in range(KHF):
                            par = ht % 2
                            sc.wait_ge(shm.h, u11 + ht + 1)
                            if ht >= 1:
                                sc.wait_ge(shd.h, u11 + ht)
                            inst = sc.activation(ubuf[:], ps2[:, par], AF.Copy)
                            sc.drain()
                            inst.then_inc(sha_.h, 1)
                            sc.wait_ge(shd.h, u11 + ht + 1)
                            inst = sc.activation(hbuf[:, ht], tbuf[:], AF.Silu)
                            sc.drain()
                            inst.then_inc(she.h, 1)

            @block.tensor
            def _(te, be=be):
                with te.Fori(0, NE) as e:
                    it = be + e * DE
                    with te.Fori(0, THC) as i:
                        jt = it + 2 + i * DTC
                        u11 = (e * THC + i) * KHF
                        u8 = (e * THC + i) * KD
                        te.wait_ge(ch.h, jt)
                        te.matmul(prb[:], self.ones_l[:, 0:128],
                                  rwrb[:, bass.ts(i, 512)],
                                  start=True, stop=True).then_inc(ch.h, 1)
                        te.wait_ge(ch.h, jt + 2)
                        for ht in range(KHF):
                            par = ht % 2
                            if ht >= 2:
                                te.wait_ge(sha_.h, u11 + ht - 1)
                                te.wait_ge(shd.h, u11 + ht - 1)
                            for k in range(KD):
                                te.matmul(ps1[:, par],
                                          w1t[:, k, ht * 128:(ht + 1) * 128],
                                          xst[:, k, bass.ts(i, 512)],
                                          start=(k == 0), stop=False)
                            te.matmul(ps1[:, par], b1s[:, ht * 128:(ht + 1) * 128],
                                      self.ones_l[:], start=False, stop=True)
                            for k in range(KD):
                                te.matmul(ps2[:, par],
                                          w3t[:, k, ht * 128:(ht + 1) * 128],
                                          xst[:, k, bass.ts(i, 512)],
                                          start=(k == 0), stop=False)
                            inst = te.matmul(ps2[:, par],
                                             b3s[:, ht * 128:(ht + 1) * 128],
                                             self.ones_l[:], start=False, stop=True)
                            inst.then_inc(shm.h, 1)
                        te.wait_ge(she.h, u11 + KHF)
                        for dt in range(KD):
                            par = dt % 2
                            if dt >= 2:
                                te.wait_ge(swe.h, u8 + dt - 1)
                            for k in range(KHF):
                                te.matmul(pmix[:, par],
                                          w2t[:, k, dt * 128:(dt + 1) * 128],
                                          hbuf[:, k], start=(k == 0), stop=False)
                            inst = te.matmul(pmix[:, par],
                                             b2s[:, dt * 128:(dt + 1) * 128],
                                             self.ones_l[:], start=False, stop=True)
                            inst.then_inc(swm.h, 1)

            qg.n += 112 * NE
            ch.n = be + NE * DE

            br = ch.n
            DR = 3
            NRC = TH // 128

            @block.sync
            def _(sy, br=br, t0=t0, q0=qsy.n):
                with sy.Fori(0, NRC) as i:
                    sy.wait_ge(ch.h, br + i * DR)
                    self.dstep(sy, qsy, ch,
                               rc[:], src_r[:, :, t0:t0 + TH][:, :, bass.ts(i, 128)],
                               qv=q0 + 16 * (i + 1))

            @block.vector
            def _(v, br=br):
                with v.Fori(0, NRC) as i:
                    it = br + i * DR
                    v.wait_ge(ch.h, it + 1)
                    for k in range(KD):
                        inst = v.tensor_tensor(rc[:, k],
                                               yac[:, k, bass.ts(i, 128)],
                                               rc[:, k], ALU.add)
                    v.drain()
                    inst.then_inc(ch.h, 1)

            @block.gpsimd
            def _(g, br=br, t0=t0, q0=qg.n):
                with g.Fori(0, NRC) as i:
                    it = br + i * DR
                    g.wait_ge(ch.h, it + 2)
                    self.dstep(g, qg, ch,
                               dst_r[:, :, t0:t0 + TH][:, :, bass.ts(i, 128)], rc[:],
                               qv=q0 + 16 * (i + 1))

            qsy.n += 16 * NRC
            qg.n += 16 * NRC
            ch.n = br + NRC * DR

        es.close()
        return ch

    def emit_attn_b(self, block, src, dst, prev):
        nc, T = self.nc, self.T
        NC = T // 128
        POI = 8192.0
        BIG = 30000.0
        es = ExitStack()
        sbl = lambda n, sh, dt=F32: es.enter_context(nc.sbuf_tensor("ab" + n, sh, dt))
        d = self.dr

        bwr = sbl("bwr", [1, 128])
        bwb = sbl("bwb", [1, 128], BF16)
        i4b = sbl("i4b", [1, 4], BF16)
        iota5c = sbl("iota5c", [5, 1])
        mcol = sbl("mcol", [16, 2])
        mcol2 = sbl("mcol2", [16, 1])
        metab = sbl("metab", [1, 2], BF16)
        swbB = sbl("swbB", [1, 16])
        e0sw = sbl("e0sw", [1, NH], BF16)
        V_tm = sbl("V_tm", [128, NC, D], BF16)
        validr = sbl("validr", [16, T], BF16)
        zq16 = sbl("zq16", [16, 1])
        csum = sbl("csum", [16, T])
        csvS = sbl("csvS", [16, T // 128])
        csum_reb = sbl("csumr", [16, T], BF16)
        csv_reb = sbl("csvr", [16, T], BF16)
        arow = sbl("arow", [16, T], BF16)
        Cfr = sbl("Cfr", [16, T], BF16)
        trr = sbl("trr", [16, T], BF16)
        Cbr = trr
        tmpI = sbl("tmpI", [16, NC * 4])
        tmpJ = sbl("tmpJ", [16, NC * 4], BF16)
        tmpJ2 = sbl("tmpJ2", [16, NC * 4], BF16)
        cntst4f = sbl("cst4", [1, 16 * NC * 4], BF16)
        cntch4f = sbl("cch4", [1, 16 * NC * 4], BF16)
        Erow = sbl("Erow", [1, NC * 16], BF16)
        Cbs = sbl("Cbs", [5, NC * 16], BF16)
        Ccol_f = sbl("Ccf", [128, NC * 16])
        Ccol_b = sbl("Ccb", [128, NC * 16])
        r1t = sbl("r1t", [2, D], BF16)
        r1tb = sbl("r1tb", [1, D], BF16)
        ownl = sbl("ownl", [1, D], BF16)
        vshfix = sbl("vshf", [1, D], BF16)

        esA = ExitStack()
        pA = esA.enter_context(nc.psum_tensor("abpA", [128, 2048], F32))

        ch = self.chain("abch")
        qsy = self.chain("abqsy")
        qg = self.chain("abqg")
        ccs = self.chain("abcc")
        pn = prev.n
        hsd_r = d["hsd"].rearrange("k p t -> p k t")
        src_r = src.rearrange("(k p) t -> p k t", p=128)
        dst_r = dst.rearrange("(k p) t -> p k t", p=128)

        esP = ExitStack()
        pT = esP.enter_context(nc.psum_tensor("abpT", [128, 1024], BF16))
        sblP = lambda n, sh, dt=F32: esP.enter_context(nc.sbuf_tensor("ab" + n, sh, dt))
        bwkt = sblP("bwkt", [128, KD, D], BF16)
        bwvt = sblP("bwvt", [128, KD, D], BF16)
        bbf = sblP("bbf", [1, 2 * D], BF16)
        rhr = sblP("rhr", [1, D], BF16)
        rhbc = sblP("rhbc", [128, D], BF16)
        hst2 = sblP("hst2", [128, KD, 128], BF16)
        k1t = sblP("k1t", [128, D], BF16)
        kdt = sblP("kdt", [128, 512])
        dcol = sblP("dcol", [128, NH])
        vt_tm = sblP("vt_tm", [128, NH], BF16)
        @block.gpsimd
        def _(g):
            g.wait_ge(prev.h, pn)
            self.dstep(g, qg, ch, bwkt[:], d["bwk"].rearrange("(k p) n -> p k n", p=128))
            self.dstep(g, qg, ch, bwvt[:], d["bwv"].rearrange("(k p) n -> p k n", p=128))
            self.dstep(g, qg, ch, bbf[:], d["bbkv"][:])
            self.dstep(g, qg, ch, rhr[:], d["rhw"][:])
            self.dstep(g, qg, ch, bwr[:], d["bww"][:])
            g.iota(tmpI[:], pattern=[[0, NC], [1, 4]], base=0, channel_multiplier=0,
                   allow_small_or_imprecise_dtypes=True)
            g.iota(iota5c[:], pattern=[[0, 1]], base=0, channel_multiplier=1,
                   allow_small_or_imprecise_dtypes=True).then_inc(ch.h, 1)
        ch.n += 6

        @block.vector
        def _(v):
            v.wait_ge(ch.h, ch.n)
            v.tensor_scalar(metab[:], self.metat[:, 0:2], 1.0, None, ALU.mult)
            v.tensor_scalar(bwb[:], bwr[:], 1.0, None, ALU.mult)
            v.memset(i4b[:], 0.0)
            v.memset(zq16[:], 0.0)
            v.memset(e0sw[:], 0.0)
            v.drain()
            for j in range(4):
                v.memset(i4b[:, j:j + 1], float(j))
            inst = v.tensor_reduce(swbB[:, :, None],
                                   bwr[:, 64:128].rearrange("o (r c) -> o c r", c=16),
                                   mybir.AxisListType.X, ALU.add)
            v.drain()
            inst.then_inc(ch.h, 1)
        ch.n += 1

        @block.tensor
        def _(te):
            te.wait_ge(ch.h, ch.n)
            te.matmul(pA[:, 0:512], self.ones_l[:, 0:128], rhr[:, 0:512],
                      start=True, stop=True)
            te.matmul(pA[:, 512:1024], self.ones_l[:, 0:128], rhr[:, 512:1024],
                      start=True, stop=True)
            inst = te.matmul(pA[0:16, 1024:1026], self.ones_l[:, 0:16], metab[:],
                             start=True, stop=True)
            inst.then_inc(ch.h, 1)
        ch.n += 1

        @block.vector
        def _(v):
            v.wait_ge(ch.h, ch.n)
            v.tensor_scalar(rhbc[:, 0:512], pA[:, 0:512], 1.0, None, ALU.mult)
            v.tensor_scalar(rhbc[:, 512:1024], pA[:, 512:1024], 1.0, None, ALU.mult)
            v.tensor_scalar(mcol[:], pA[0:16, 1024:1026], 1.0, None, ALU.mult)
            v.drain()
            v.tensor_scalar(mcol2[:], mcol[:, 1:2], BIG, None, ALU.mult)
            v.drain()
            v.tensor_scalar(e0sw[:], swbB[:], self.metat[:, 1:2], None, ALU.mult)
            v.drain()
            v.sem_inc(ch.h, 1)
        ch.n += 1


        bp = ch.n
        DP = 9

        @block.sync
        def _(sy, q0=qsy.n):
            with sy.Fori(0, NC) as i:
                sy.wait_ge(ch.h, bp + i * DP)
                self.dstep(sy, qsy, ch, hst2[:], hsd_r[:, :, bass.ts(i, 128)],
                           qv=q0 + 16 * (i + 1))

        @block.tensor
        def _(te):
            with te.Fori(0, NC) as i:
                it = bp + i * DP
                te.wait_ge(ch.h, it + 1)
                for hf in range(2):
                    for k in range(KD):
                        te.matmul(pA[:, hf * 512:(hf + 1) * 512], hst2[:, k],
                                  bwvt[:, k, hf * 512:(hf + 1) * 512],
                                  start=(k == 0), stop=False)
                    inst = te.matmul(pA[:, hf * 512:(hf + 1) * 512],
                                     self.ones_l[:, 0:128],
                                     bbf[:, D + hf * 512:D + (hf + 1) * 512],
                                     start=False, stop=True)
                inst.then_inc(ch.h, 1)
                te.wait_ge(ch.h, it + 3)
                for hf in range(2):
                    for k in range(KD):
                        te.matmul(pA[:, hf * 512:(hf + 1) * 512], hst2[:, k],
                                  bwkt[:, k, hf * 512:(hf + 1) * 512],
                                  start=(k == 0), stop=False)
                    inst = te.matmul(pA[:, hf * 512:(hf + 1) * 512],
                                     self.ones_l[:, 0:128],
                                     bbf[:, hf * 512:(hf + 1) * 512],
                                     start=False, stop=True)
                inst.then_inc(ch.h, 1)
                te.wait_ge(ch.h, it + 7)
                inst = te.transpose(pT[0:16, 0:128], vt_tm[:], self.ident[:])
                inst.then_inc(ch.h, 1)

        @block.scalar
        def _(sc):
            with sc.Fori(0, NC) as i:
                it = bp + i * DP
                sc.wait_ge(ch.h, it + 2)
                for hf in range(2):
                    inst = sc.activation(V_tm[:, i, hf * 512:(hf + 1) * 512],
                                         pA[:, hf * 512:(hf + 1) * 512], AF.Relu)
                sc.drain()
                inst.then_inc(ch.h, 1)
                sc.wait_ge(ch.h, it + 4)
                for hf in range(2):
                    inst = sc.activation(k1t[:, hf * 512:(hf + 1) * 512],
                                         pA[:, hf * 512:(hf + 1) * 512], AF.Relu)
                sc.drain()
                inst.then_inc(ch.h, 1)

        @block.vector
        def _(v):
            with v.Fori(0, NC) as i:
                it = bp + i * DP
                v.wait_ge(ch.h, it + 5)
                inst = v.tensor_tensor(kdt[:], k1t[:, 0:512], rhbc[:, 0:512], ALU.mult)
                v.drain()
                inst = v.tensor_reduce(dcol[:, 0:8][:, :, None],
                                       kdt[:].rearrange("p (h e) -> p h e", h=8),
                                       mybir.AxisListType.X, ALU.add)
                v.drain()
                inst = v.tensor_tensor(kdt[:], k1t[:, 512:1024], rhbc[:, 512:1024], ALU.mult)
                v.drain()
                inst.then_inc(ch.h, 1)
                inst = v.tensor_reduce(dcol[:, 8:16][:, :, None],
                                       kdt[:].rearrange("p (h e) -> p h e", h=8),
                                       mybir.AxisListType.X, ALU.add)
                v.drain()
                inst = v.tensor_scalar(vt_tm[:], dcol[:], 0.5, None, ALU.is_gt)
                v.drain()
                inst.then_inc(ch.h, 1)
                v.wait_ge(ch.h, it + 8)
                inst = v.tensor_scalar(validr[:, bass.ts(i, 128)], pT[0:16, 0:128],
                                       1.0, None, ALU.mult)
                v.drain()
                inst.then_inc(ch.h, 1)

        qsy.n += 16 * NC
        ch.n = bp + NC * DP

        n0 = ch.n

        @block.gpsimd
        def _(g, n0=n0):
            g.wait_ge(ch.h, n0)
            self.dstep(g, qg, ch, d["r1x_i"][:], V_tm[0:1, 0, :])
            g.collective_compute("AllGather", ALU.bypass, self.pairs,
                                 [d["r1x_i"].opt()], [d["r1x_o"].opt()]
                                 ).then_inc(ccs.h, 1)
            g.wait_ge(ccs.h, 1)
            g.sem_inc(ch.h, 1)
            self.dstep(g, qg, ch, r1t[:], d["r1x_o"][:])
            self.dstep(g, qg, ch, r1tb[:], d["r1x_o"][1:2, :])
            self.dstep(g, qg, ch, ownl[:], V_tm[127:128, NC - 1, :])
        ch.n += 5

        @block.vector
        def _(v):
            v.wait_ge(ch.h, ch.n)
            v.tensor_scalar(vshfix[:], r1tb[:], self.metat[:, 1:2], None, ALU.mult)
            v.tensor_scalar(ownl[:], ownl[:], self.metat[:, 0:1], None, ALU.mult)
            v.drain()
            v.tensor_tensor(vshfix[:], vshfix[:], ownl[:], ALU.add)
            v.drain()
            inst = v.tensor_scalar(validr[:, 0:1], validr[:, 0:1],
                                   mcol[:, 0:1], None, ALU.mult)
            v.drain()
            inst.then_inc(ch.h, 1)
            v.tensor_tensor_scan(csum[:], validr[:],
                                 zq16[:].to_broadcast([16, T]), 0.0,
                                 ALU.add, ALU.add)
            v.drain()
            v.tensor_tensor(csvS[:], csum[:, 0:T:128], validr[:, 0:T:128],
                            ALU.subtract)
            v.drain()
            v.tensor_tensor(csum_reb[:].rearrange("p (c u) -> p c u", u=128),
                            csum[:].rearrange("p (c u) -> p c u", u=128),
                            csvS[:, :, None].to_broadcast([16, NC, 128]),
                            ALU.subtract)
            v.drain()
            v.tensor_tensor(csv_reb[:], csum_reb[:], validr[:], ALU.subtract)
            v.drain()
            v.tensor_scalar(arow[:], validr[:], -POI, POI, ALU.mult, ALU.add)
            v.drain()
            v.tensor_tensor(Cfr[:], csum_reb[:], arow[:], ALU.add)
            v.tensor_scalar(trr[:], csum_reb[:], 1.0, None, ALU.subtract)
            v.drain()
            v.tensor_tensor(trr[:], trr[:], arow[:], ALU.subtract)
            v.drain()
            v.tensor_scalar(csv_reb[:, 0:1], csv_reb[:, 0:1], mcol2[:, 0:1],
                            None, ALU.add)
            v.drain()
            v.tensor_tensor(tmpJ[:].rearrange("p (c j) -> p c j", j=4),
                            csum_reb[:, 127:T:128][:, :, None].to_broadcast([16, NC, 4]),
                            tmpI[:].rearrange("p (c j) -> p c j", j=4),
                            ALU.subtract)
            v.drain()
            inst = v.tensor_scalar(tmpJ2[:], tmpJ[:], -1.0, None, ALU.mult)
            v.drain()
            inst.then_inc(ch.h, 1)
        ch.n += 2

        @block.gpsimd
        def _(g):
            g.wait_ge(ch.h, ch.n)
            self.dstep(g, qg, ch, d["csrd"][0], csum_reb[:])
            self.dstep(g, qg, ch, d["csrd"][1], csv_reb[:])
            self.dstep(g, qg, ch, d["flA"][:], tmpJ[:])
            self.dstep(g, qg, ch, d["flB"][:], tmpJ2[:])
            g.dma_start(out=d["flE"].rearrange("o (c ch) -> o c ch", ch=16),
                        in_=d["csrd"][0].rearrange("ch (c u) -> u c ch", u=128)[127:128, :, :],
                        allow_slow_non_contiguous=True).then_inc(qg.h, 16)
            qg.n += 16
            g.wait_ge(qg.h, qg.n)
            g.sem_inc(ch.h, 1)
            self.dstep(g, qg, ch, cntst4f[:],
                       d["flA"].rearrange("a b -> (a b)")[None, :])
            self.dstep(g, qg, ch, cntch4f[:],
                       d["flB"].rearrange("a b -> (a b)")[None, :])
            self.dstep(g, qg, ch, Erow[:], d["flE"][:])
        ch.n += 8

        @block.tensor
        def _(te):
            te.wait_ge(ch.h, ch.n)
            inst = te.matmul(pA[0:5, 0:NC * 16], self.ones_l[:, 0:5], Erow[:],
                             start=True, stop=True)
            inst.then_inc(ch.h, 1)
        ch.n += 1

        @block.vector
        def _(v):
            v.wait_ge(ch.h, ch.n)
            inst = v.tensor_scalar(Cbs[:], pA[0:5, 0:NC * 16], iota5c[:], None,
                                   ALU.add)
            v.drain()
            inst.then_inc(ch.h, 1)
        ch.n += 1

        cstg = sblP("cstg", [16, 128], BF16)
        bc = ch.n
        DC = 6

        @block.tensor
        def _(te):
            with te.Fori(0, NC) as i:
                it = bc + i * DC
                te.wait_ge(ch.h, it + 1)
                te.transpose(pT[:, 0:16], cstg[:],
                             self.ident[0:16, 0:16]).then_inc(ch.h, 1)
                te.wait_ge(ch.h, it + 4)
                te.transpose(pT[:, 16:32], cstg[:],
                             self.ident[0:16, 0:16]).then_inc(ch.h, 1)

        @block.vector
        def _(v):
            with v.Fori(0, NC) as i:
                it = bc + i * DC
                v.wait_ge(ch.h, it)
                inst = v.tensor_scalar(cstg[:], Cfr[:, bass.ts(i, 128)],
                                       1.0, None, ALU.mult)
                v.drain()
                inst.then_inc(ch.h, 1)
                v.wait_ge(ch.h, it + 2)
                inst = v.tensor_scalar(Ccol_f[:, bass.ts(i, 16)], pT[:, 0:16],
                                       1.0, None, ALU.mult)
                v.drain()
                inst.then_inc(ch.h, 1)
                inst = v.tensor_scalar(cstg[:], Cbr[:, bass.ts(i, 128)],
                                       1.0, None, ALU.mult)
                v.drain()
                inst.then_inc(ch.h, 1)
                v.wait_ge(ch.h, it + 5)
                inst = v.tensor_scalar(Ccol_b[:, bass.ts(i, 16)], pT[:, 16:32],
                                       1.0, None, ALU.mult)
                v.drain()
                inst.then_inc(ch.h, 1)

        ch.n = bc + NC * DC
        esP.close()

        Scur_f = sbl("Scf", [5, NH * HDM], BF16)
        Scur_b = sbl("Scb", [5, NH * HDM], BF16)
        msf = sbl("msf", [128, NH * 4], BF16)
        mss = sbl("mss", [5, NH * 4], BF16)
        Vsb = sbl("Vsb", [128, D], BF16)
        Vtmp = sbl("Vtmp", [128, D], BF16)
        Vsf = sbl("Vsf", [128, D], BF16)
        sfs = sbl("sfs", [5, NH * HDM], BF16)
        sbs = sbl("sbs", [5, NH * HDM], BF16)
        stage5f = sbl("st5f", [5, 16 * 128], BF16)
        stage5b = sbl("st5b", [5, 16 * 128], BF16)
        Dlf = sbl("Dlf", [128, 2048], BF16)
        Mf = sbl("Mf", [128, 2048], BF16)
        Dfs = sbl("Dfs", [5, 2048], BF16)
        Mfs = sbl("Mfs", [5, 2048], BF16)
        tmp5 = sbl("tmp5", [5, 2048], BF16)
        wex = [sbl(f"wex{k}", [128, 16], BF16) for k in range(8)]
        zg = sbl("zg", [128, 512], BF16)
        l0f = sbl("l0f", [4, D], BF16)
        l1b = sbl("l1b", [4, D], BF16)
        rstg = sbl("rstg", [128, KD, 128])
        padrep = sbl("padrep", [4, D], BF16)

        for k in range(8):
            nk = ch.n

            @block.tensor
            def _(te, k=k, nk=nk):
                te.wait_ge(ch.h, nk)
                inst = te.matmul(pA[:, 0:16], self.ones_l[:, 0:128],
                                 bwb[:, k * 16:(k + 1) * 16],
                                 start=True, stop=True)
                inst.then_inc(ch.h, 1)
            ch.n += 1
            nk = ch.n

            @block.vector
            def _(v, k=k, nk=nk):
                v.wait_ge(ch.h, nk)
                inst = v.tensor_scalar(wex[k][:], pA[:, 0:16], 1.0, None, ALU.mult)
                v.drain()
                inst.then_inc(ch.h, 1)
            ch.n += 1

        esC = ExitStack()
        psS = esC.enter_context(nc.psum_tensor("abpsS", [4, 1024], F32))

        n0 = ch.n

        @block.gpsimd
        def _(g, n0=n0):
            g.wait_ge(ch.h, n0)
            self.dstep(g, qg, ch, padrep[:, None, :],
                       d["r1x_o"][None, 0:1, :].to_broadcast([4, 1, D]))
            self.dstep(g, qg, ch, Scur_f[:, None, :],
                       d["r1x_o"][None, 0:1, :].to_broadcast([5, 1, D]))
            self.dstep(g, qg, ch, Scur_b[:, None, :],
                       d["r1x_o"][None, 0:1, :].to_broadcast([5, 1, D]))
        ch.n += 3

        def emit_chain(fwd):
            b0 = ch.n
            DCH = 10 if not fwd else 7
            Scur = Scur_f if fwd else Scur_b
            dirIdx = 0 if fwd else 1
            Ccol = Ccol_f if fwd else Ccol_b
            Vget = V_tm if fwd else Vsb

            @block.gpsimd
            def _(g, b0=b0, q0=qg.n):
                with g.Fori(0, NC) as c:
                    it = b0 + c * DCH
                    cc = nc.s_assert_le(nc.s_assert_ge(NC - 1 - c, 0), NC - 1) if not fwd else c
                    g.wait_ge(ch.h, it)
                    npd = 4 if not fwd else 1
                    self.dstep(g, qg, ch, d["sind"][dirIdx, cc], Scur[:],
                               qv=q0 + 16 * (npd * c + 1))
                    if not fwd:
                        self.dstep(g, qg, ch, Vtmp[:], V_tm[:, cc, :],
                                   qv=q0 + 16 * (npd * c + 2))
                        self.dstep(g, qg, ch, Vsb[0:127, :], Vtmp[1:128, :],
                                   qv=q0 + 16 * (npd * c + 3))
                        with g.If(cc < NC - 1):
                            ccp = nc.s_assert_le(cc + 1, NC - 1)
                            g.dma_start(out=Vsb[127:128, :], in_=V_tm[0:1, ccp, :]
                                        ).then_inc(qg.h, 16)
                        with g.Else():
                            g.dma_start(out=Vsb[127:128, :], in_=vshfix[:]
                                        ).then_inc(qg.h, 16)
                        g.wait_ge(qg.h, q0 + 16 * (npd * c + 4))
                        g.sem_inc(ch.h, 1)

            of = 1 if fwd else 4

            @block.tensor
            def _(te, b0=b0, of=of):
                with te.Fori(0, NC) as c:
                    it = b0 + c * DCH
                    cc = nc.s_assert_le(nc.s_assert_ge(NC - 1 - c, 0), NC - 1) if not fwd else c
                    te.wait_ge(ch.h, it + of)
                    if fwd:
                        for c16 in range(16):
                            inst = te.matmul(pA[:, c16 * 4:(c16 + 1) * 4],
                                             self.ones_l[:, 0:128],
                                             cntst4f[:, bass.ts(c16 * NC + cc, 4)],
                                             start=(c16 == 0), stop=(c16 == 15))
                    else:
                        inst = te.matmul(pA[:, 0:4], self.ones_l[:, 0:128],
                                         i4b[:], start=True, stop=True)
                    inst.then_inc(ch.h, 1)
                    te.wait_ge(ch.h, it + of + 2)
                    for c16 in range(16):
                        inst = te.matmul(pA[0:5, 512 + c16 * 4:512 + (c16 + 1) * 4],
                                         self.ones_l[:, 0:5],
                                         cntch4f[:, bass.ts(c16 * NC + cc, 4)],
                                         start=(c16 == 0), stop=(c16 == 15))
                    inst.then_inc(ch.h, 1)
                    te.wait_ge(ch.h, it + of + 4)
                    for c16 in range(16):
                        te.matmul(psS[:, c16 * 64:(c16 + 1) * 64],
                                  msf[:, c16 * 4:(c16 + 1) * 4][:, 0:4] if False
                                  else msf[:, c16 * 4:(c16 + 1) * 4],
                                  Vget[:, cc, c16 * 64:(c16 + 1) * 64] if fwd
                                  else Vsb[:, c16 * 64:(c16 + 1) * 64],
                                  start=(c16 == 0 or c16 == 8), stop=False)
                        inst = te.matmul(psS[:, c16 * 64:(c16 + 1) * 64],
                                         mss[:, c16 * 4:(c16 + 1) * 4],
                                         Scur[:, c16 * 64:(c16 + 1) * 64],
                                         start=False,
                                         stop=(c16 == 7 or c16 == 15))
                    inst.then_inc(ch.h, 1)

            @block.vector
            def _(v, b0=b0, of=of):
                with v.Fori(0, NC) as c:
                    it = b0 + c * DCH
                    cc = nc.s_assert_le(nc.s_assert_ge(NC - 1 - c, 0), NC - 1) if not fwd else c
                    v.wait_ge(ch.h, it + of + 1)
                    if fwd:
                        for c16 in range(16):
                            inst = v.tensor_scalar(
                                msf[:, c16 * 4:(c16 + 1) * 4],
                                pA[:, c16 * 4:(c16 + 1) * 4],
                                Ccol[:, cc * 16 + c16][:, None], 0.0,
                                ALU.subtract, ALU.is_equal)
                    else:
                        for c16 in range(16):
                            inst = v.tensor_scalar(
                                msf[:, c16 * 4:(c16 + 1) * 4],
                                pA[:, 0:4],
                                Ccol[:, cc * 16 + c16][:, None], 0.0,
                                ALU.subtract, ALU.is_equal)
                    v.drain()
                    inst.then_inc(ch.h, 1)
                    v.wait_ge(ch.h, it + of + 3)
                    for c16 in range(16):
                        inst = v.tensor_scalar(
                            mss[:, c16 * 4:(c16 + 1) * 4],
                            pA[0:5, 512 + c16 * 4:512 + (c16 + 1) * 4],
                            iota5c[:], 0.0, ALU.subtract, ALU.is_equal)
                    v.drain()
                    inst.then_inc(ch.h, 1)
                    v.wait_ge(ch.h, it + of + 5)
                    inst = v.tensor_scalar(Scur[0:4, :], psS[:], 1.0, None, ALU.mult)
                    v.drain()
                    inst.then_inc(ch.h, 1)

            if fwd:
                qg.n += 16 * NC
            else:
                qg.n += 64 * NC
            ch.n = b0 + NC * DCH

        emit_chain(True)
        emit_chain(False)

        n0 = ch.n

        @block.gpsimd
        def _(g, n0=n0):
            g.wait_ge(ch.h, n0)
            self.dstep(g, qg, ch, d["r2f_i"][:], Scur_f[0:4, :])
            self.dstep(g, qg, ch, d["r2b_i"][:], Scur_b[0:4, :])
            g.collective_compute("AllGather", ALU.bypass, self.pairs,
                                 [d["r2f_i"].opt()], [d["r2f_o"].opt()]
                                 ).then_inc(ccs.h, 1)
            g.wait_ge(ccs.h, 2)
            g.sem_inc(ch.h, 1)
            g.collective_compute("AllGather", ALU.bypass, self.pairs,
                                 [d["r2b_i"].opt()], [d["r2b_o"].opt()]
                                 ).then_inc(ccs.h, 1)
            g.wait_ge(ccs.h, 3)
            g.sem_inc(ch.h, 1)
            self.dstep(g, qg, ch, l0f[:], d["r2f_o"][0:4, :])
            self.dstep(g, qg, ch, l1b[:], d["r2b_o"][4:8, :])
        ch.n += 6

        @block.vector
        def _(v):
            v.wait_ge(ch.h, ch.n)
            v.tensor_scalar(l0f[:], l0f[:], mcol[0:4, 0:1], None, ALU.mult)
            v.tensor_scalar(Vsf[0:4, 0:D], padrep[:], mcol[0:4, 1:2], None, ALU.mult)
            v.drain()
            v.tensor_tensor(Scur_f[0:4, :], l0f[:], Vsf[0:4, 0:D], ALU.add)
            v.drain()
            v.tensor_scalar(l1b[:], l1b[:], mcol[0:4, 1:2], None, ALU.mult)
            v.tensor_scalar(Vsf[0:4, 0:D], padrep[:], mcol[0:4, 0:1], None, ALU.mult)
            v.drain()
            inst = v.tensor_tensor(Scur_b[0:4, :], l1b[:], Vsf[0:4, 0:D], ALU.add)
            v.drain()
            inst.then_inc(ch.h, 1)
        ch.n += 1

        emit_chain(True)
        emit_chain(False)
        esC.close()

        esG = ExitStack()
        pO = esG.enter_context(nc.psum_tensor("abpO", [64, 2048], F32))

        n0 = ch.n

        @block.vector
        def _(v, n0=n0):
            v.wait_ge(ch.h, n0)
            v.memset(zg[:], 0.0)
            v.drain()
            v.sem_inc(ch.h, 1)
        ch.n += 1

        bgt = ch.n
        DGT = 25

        @block.vector
        def _(v):
            with v.Fori(0, NC) as c:
                it = bgt + c * DGT
                v.wait_ge(ch.h, it)
                inst = v.tensor_scalar(Vsf[:], V_tm[:, c, :], 1.0, None, ALU.mult)
                v.drain()
                inst.then_inc(ch.h, 1)
                v.wait_ge(ch.h, it + 6)
                inst = v.tensor_tensor(
                    Dlf[:].rearrange("p (a b) -> p a b", b=128),
                    pA[:].rearrange("p (a b) -> p a b", b=128),
                    Ccol_f[:, bass.ts(c, 16)][:, :, None].to_broadcast([128, 16, 128]),
                    ALU.subtract)
                v.drain()
                inst.then_inc(ch.h, 1)
                v.wait_ge(ch.h, it + 7)
                inst = v.scalar_tensor_tensor(
                    Mf[:].rearrange("p (a b) -> p a b", b=128),
                    Dlf[:].rearrange("p (a b) -> p a b", b=128), 0.0,
                    wex[0][:, :, None].to_broadcast([128, 16, 128]),
                    ALU.is_equal, ALU.mult)
                v.drain()
                inst.then_inc(ch.h, 1)
                for k in range(1, 4):
                    v.wait_ge(ch.h, it + 7 + k)
                    v.scalar_tensor_tensor(
                        pA[:].rearrange("p (a b) -> p a b", b=128),
                        Dlf[:].rearrange("p (a b) -> p a b", b=128), float(k),
                        wex[k][:, :, None].to_broadcast([128, 16, 128]),
                        ALU.is_equal, ALU.mult)
                    v.drain()
                    inst = v.tensor_tensor(Mf[:], Mf[:], pA[:], ALU.add)
                    v.drain()
                    inst.then_inc(ch.h, 1)
                v.wait_ge(ch.h, it + 11)
                v.tensor_scalar(Dfs[:], stage5f[:], iota5c[:], None, ALU.add)
                v.drain()
                v.scalar_tensor_tensor(
                    Mfs[:].rearrange("p (a b) -> p a b", b=128),
                    Dfs[:].rearrange("p (a b) -> p a b", b=128), 0.0,
                    wex[0][0:5, :, None].to_broadcast([5, 16, 128]),
                    ALU.is_equal, ALU.mult)
                v.drain()
                for k in range(1, 4):
                    v.scalar_tensor_tensor(
                        tmp5[:].rearrange("p (a b) -> p a b", b=128),
                        Dfs[:].rearrange("p (a b) -> p a b", b=128), float(k),
                        wex[k][0:5, :, None].to_broadcast([5, 16, 128]),
                        ALU.is_equal, ALU.mult)
                    v.drain()
                    v.tensor_tensor(Mfs[:], Mfs[:], tmp5[:], ALU.add)
                    v.drain()
                v.sem_inc(ch.h, 1)
                v.wait_ge(ch.h, it + 15)
                inst = v.tensor_tensor(
                    Dlf[:].rearrange("p (a b) -> p a b", b=128),
                    Ccol_b[:, bass.ts(c, 16)][:, :, None].to_broadcast([128, 16, 128]),
                    pA[:].rearrange("p (a b) -> p a b", b=128),
                    ALU.subtract)
                v.drain()
                inst.then_inc(ch.h, 1)
                v.wait_ge(ch.h, it + 16)
                inst = v.scalar_tensor_tensor(
                    Mf[:].rearrange("p (a b) -> p a b", b=128),
                    Dlf[:].rearrange("p (a b) -> p a b", b=128), 0.0,
                    wex[4][:, :, None].to_broadcast([128, 16, 128]),
                    ALU.is_equal, ALU.mult)
                v.drain()
                inst.then_inc(ch.h, 1)
                for k in range(1, 4):
                    v.wait_ge(ch.h, it + 16 + k)
                    v.scalar_tensor_tensor(
                        pA[:].rearrange("p (a b) -> p a b", b=128),
                        Dlf[:].rearrange("p (a b) -> p a b", b=128), float(k),
                        wex[4 + k][:, :, None].to_broadcast([128, 16, 128]),
                        ALU.is_equal, ALU.mult)
                    v.drain()
                    inst = v.tensor_tensor(Mf[:], Mf[:], pA[:], ALU.add)
                    v.drain()
                    inst.then_inc(ch.h, 1)
                v.wait_ge(ch.h, it + 20)
                v.tensor_tensor(Dfs[:].rearrange("p (a b) -> p a b", b=128),
                                Cbs[:, bass.ts(c, 16)][:, :, None]
                                .to_broadcast([5, 16, 128]),
                                stage5b[:].rearrange("p (a b) -> p a b", b=128),
                                ALU.subtract)
                v.drain()
                v.scalar_tensor_tensor(
                    Mfs[:].rearrange("p (a b) -> p a b", b=128),
                    Dfs[:].rearrange("p (a b) -> p a b", b=128), 0.0,
                    wex[4][0:5, :, None].to_broadcast([5, 16, 128]),
                    ALU.is_equal, ALU.mult)
                v.drain()
                for k in range(1, 4):
                    v.scalar_tensor_tensor(
                        tmp5[:].rearrange("p (a b) -> p a b", b=128),
                        Dfs[:].rearrange("p (a b) -> p a b", b=128), float(k),
                        wex[4 + k][0:5, :, None].to_broadcast([5, 16, 128]),
                        ALU.is_equal, ALU.mult)
                    v.drain()
                    v.tensor_tensor(Mfs[:], Mfs[:], tmp5[:], ALU.add)
                    v.drain()
                v.sem_inc(ch.h, 1)
                v.wait_ge(ch.h, it + 23)
                for c16 in range(16):
                    inst = v.tensor_tensor(
                        rstg[(c16 % 2) * 64:(c16 % 2) * 64 + 64, c16 // 2, :],
                        rstg[(c16 % 2) * 64:(c16 % 2) * 64 + 64, c16 // 2, :],
                        pO[:, c16 * 128:(c16 + 1) * 128], ALU.add)
                v.drain()
                inst.then_inc(ch.h, 1)

        @block.gpsimd
        def _(g, q0=qg.n):
            with g.Fori(0, NC) as c:
                it = bgt + c * DGT
                g.wait_ge(ch.h, it + 1)
                self.dstep(g, qg, ch, Vtmp[:], V_tm[:, c, :],
                           qv=q0 + 16 * (8 * c + 1))
                self.dstep(g, qg, ch, Vsb[0:127, :], Vtmp[1:128, :],
                           qv=q0 + 16 * (8 * c + 2))
                with g.If(c < NC - 1):
                    ccp = nc.s_assert_le(c + 1, NC - 1)
                    g.dma_start(out=Vsb[127:128, :], in_=V_tm[0:1, ccp, :]
                                ).then_inc(qg.h, 16)
                with g.Else():
                    g.dma_start(out=Vsb[127:128, :], in_=vshfix[:]
                                ).then_inc(qg.h, 16)
                g.wait_ge(qg.h, q0 + 16 * (8 * c + 3))
                g.sem_inc(ch.h, 1)
                g.dma_start(out=sfs[:], in_=d["sind"][0, c]).then_inc(qg.h, 16)
                g.dma_start(out=sbs[:], in_=d["sind"][1, c]).then_inc(qg.h, 16)
                g.dma_start(out=stage5f[:],
                            in_=d["csrd"][0][None, :, bass.ts(c, 128)]
                            .to_broadcast([5, 16, 128])).then_inc(qg.h, 16)
                g.dma_start(out=stage5b[:],
                            in_=d["csrd"][1][None, :, bass.ts(c, 128)]
                            .to_broadcast([5, 16, 128])).then_inc(qg.h, 16)
                g.wait_ge(qg.h, q0 + 16 * (8 * c + 7))
                g.sem_inc(ch.h, 1)
                g.wait_ge(ch.h, it + 24)
                self.dstep(g, qg, ch, dst_r[:, :, bass.ts(c, 128)], rstg[:],
                           qv=q0 + 16 * (8 * c + 8))

        @block.tensor
        def _(te):
            with te.Fori(0, NC) as c:
                it = bgt + c * DGT
                te.wait_ge(ch.h, it + 5)
                for q in range(4):
                    inst = te.matmul(pA[:, q * 512:(q + 1) * 512],
                                     self.ones_l[:, 0:128],
                                     stage5f[0:1, q * 512:(q + 1) * 512],
                                     start=True, stop=True)
                inst.then_inc(ch.h, 1)
                te.wait_ge(ch.h, it + 12)
                for q in range(4):
                    inst = te.matmul(pO[:, q * 512:(q + 1) * 512], zg[:, 0:64], zg[:],
                                     start=True, stop=False)
                inst.then_inc(ch.h, 1)
                te.wait_ge(ch.h, it + 13)
                for c16 in range(16):
                    te.matmul(pO[:, c16 * 128:(c16 + 1) * 128],
                              Vsf[:, c16 * 64:(c16 + 1) * 64],
                              Mf[:, c16 * 128:(c16 + 1) * 128],
                              start=False, stop=False)
                    inst = te.matmul(pO[:, c16 * 128:(c16 + 1) * 128],
                                     sfs[:, c16 * 64:(c16 + 1) * 64],
                                     Mfs[:, c16 * 128:(c16 + 1) * 128],
                                     start=False, stop=False)
                inst.then_inc(ch.h, 1)
                te.wait_ge(ch.h, it + 14)
                for q in range(4):
                    inst = te.matmul(pA[:, q * 512:(q + 1) * 512],
                                     self.ones_l[:, 0:128],
                                     stage5b[0:1, q * 512:(q + 1) * 512],
                                     start=True, stop=True)
                inst.then_inc(ch.h, 1)
                te.wait_ge(ch.h, it + 21)
                for c16 in range(16):
                    te.matmul(pO[:, c16 * 128:(c16 + 1) * 128],
                              Vsb[:, c16 * 64:(c16 + 1) * 64],
                              Mf[:, c16 * 128:(c16 + 1) * 128],
                              start=False, stop=False)
                    inst = te.matmul(pO[:, c16 * 128:(c16 + 1) * 128],
                                     sbs[:, c16 * 64:(c16 + 1) * 64],
                                     Mfs[:, c16 * 128:(c16 + 1) * 128],
                                     start=False, stop=False)
                with te.If(c == 0):
                    for c16 in range(16):
                        te.matmul(pO[:, c16 * 128:c16 * 128 + 1],
                                  r1t[0:1, c16 * 64:(c16 + 1) * 64],
                                  e0sw[:, c16:c16 + 1],
                                  start=False, stop=False)
                for q in range(4):
                    inst = te.matmul(pO[:, q * 512:(q + 1) * 512], zg[:, 0:64],
                                     zg[:], start=False, stop=True)
                inst.then_inc(ch.h, 1)

        @block.sync
        def _(sy, q0=qsy.n):
            with sy.Fori(0, NC) as c:
                it = bgt + c * DGT
                sy.wait_ge(ch.h, it + 22)
                self.dstep(sy, qsy, ch, rstg[:], src_r[:, :, bass.ts(c, 128)],
                           qv=q0 + 16 * (c + 1))

        qsy.n += 16 * NC
        qg.n += 112 * NC + 16 * NC
        ch.n = bgt + NC * DGT
        esG.close()
        es.close()
        esA.close()
        return ch


def _prep_inputs(inputs, T, n_cores):
    f32 = np.float32
    x = inputs["L0_attn_a_x"]
    per_core = []
    wq = np.ascontiguousarray(inputs["L0_attn_a_Wq"].reshape(D, D))
    wk = np.ascontiguousarray(inputs["L0_attn_a_Wk"].reshape(D, D))
    wv = np.ascontiguousarray(inputs["L0_attn_a_Wv"].reshape(D, D))
    wo = np.ascontiguousarray(inputs["L0_attn_a_Wo"].reshape(D, D))
    bqkvo = np.concatenate([inputs["L0_attn_a_bq"].reshape(-1),
                            inputs["L0_attn_a_bk"].reshape(-1),
                            inputs["L0_attn_a_bv"].reshape(-1),
                            inputs["L0_attn_a_bo"].reshape(-1)])[None, :]
    bwk = np.ascontiguousarray(inputs["L0_attn_b_Wk"])
    bwv = np.ascontiguousarray(inputs["L0_attn_b_Wv"])
    bbkv = np.concatenate([inputs["L0_attn_b_bk"].reshape(-1),
                           inputs["L0_attn_b_bv"].reshape(-1)])[None, :]
    rhw = inputs["L0_attn_b_ReadingHead"].reshape(1, D)
    bww = np.ascontiguousarray(inputs["L0_attn_b_bidirection_weight"].T).reshape(1, -1)
    gwt = np.ascontiguousarray(inputs["L0_ffn_gate_w"].T)
    gbr = inputs["L0_ffn_gate_b"][None, :]
    w1c = np.concatenate([inputs["L0_ffn_w1"], inputs["L0_ffn_sw1"][None]], 0)
    w3c = np.concatenate([inputs["L0_ffn_w3"], inputs["L0_ffn_sw3"][None]], 0)
    w2c = np.concatenate([inputs["L0_ffn_w2"], inputs["L0_ffn_sw2"][None]], 0)
    b1c = np.concatenate([inputs["L0_ffn_b1"].reshape(-1),
                          inputs["L0_ffn_sb1"].reshape(-1)])[None, :]
    b3c = np.concatenate([inputs["L0_ffn_b3"].reshape(-1),
                          inputs["L0_ffn_sb3"].reshape(-1)])[None, :]
    b2c = np.concatenate([inputs["L0_ffn_b2"].reshape(-1),
                          inputs["L0_ffn_sb2"].reshape(-1)])[None, :]
    shared = dict(wq=wq, wk=wk, wv=wv, wo=wo, bqkvo=bqkvo, bwk=bwk, bwv=bwv,
                  bbkv=bbkv, rhw=rhw, bww=bww, gwt=gwt, gbr=gbr,
                  w1c=w1c, w3c=w3c, w2c=w2c, b1c=b1c, b3c=b3c, b2c=b2c)
    shared = {k: np.ascontiguousarray(v, dtype=f32) for k, v in shared.items()}
    for core in range(n_cores):
        b, half = core // 2, core % 2
        xin = np.ascontiguousarray(x[b, half * T:(half + 1) * T, :].T, dtype=f32)
        m = np.zeros((1, 16), f32)
        m[0, 0] = float(half)
        m[0, 1] = 1.0 - float(half)
        per_core.append(dict(shared, xin=xin, meta=m))
    return per_core


def kernel(**inputs):
    T = 4096
    n_cores = 8
    b = B(T, n_cores)
    nc = b.build()
    in_maps = _prep_inputs(inputs, T, n_cores)
    res = run_bass_kernel_spmd(nc, in_maps, core_ids=list(range(n_cores)))
    x = inputs["L0_attn_a_x"]
    Bsz, S, _ = x.shape
    out = np.empty((Bsz, S, D), np.float32)
    for core in range(n_cores):
        bb, half = core // 2, core % 2
        out[bb, half * T:(half + 1) * T, :] = np.asarray(res.results[core]["out"]).T
    return out


# version 04 (final)
# speedup vs baseline: 1.3454x; 1.0229x over previous
import numpy as np
import ml_dtypes
from contextlib import ExitStack

import concourse.bass as bass
from concourse.bacc import Bacc
import concourse.mybir as mybir
from concourse.bass_utils import run_bass_kernel_spmd

F32 = mybir.dt.float32
BF16 = mybir.dt.bfloat16
AF = mybir.ActivationFunctionType
ALU = mybir.AluOpType

D = 1024
KD = 8
NH = 16
HDM = 64
HF = 1408
KHF = 11
NE = 9
LN_EPS = 1e-5
ATT_EPS = 1e-6


class Chain:
    def __init__(self, nc, es, name):
        self.h = es.enter_context(nc.semaphore(name))
        self.n = 0


class B:
    def __init__(self, T, n_cores):
        self.T = T
        self.n_cores = n_cores
        self.pairs = [[2 * i, 2 * i + 1] for i in range(max(1, n_cores // 2))]
        self.nc = Bacc("TRN2", target_bir_lowering=False)
        self.es = ExitStack()
        self.dr = {}

    def sb(self, name, shape, dt=F32):
        return self.es.enter_context(self.nc.sbuf_tensor(name, shape, dt))

    def chain(self, name):
        return Chain(self.nc, self.es, name)

    def din(self, n, s, dt=F32):
        self.dr[n] = self.nc.dram_tensor(n, s, dt, kind="ExternalInput").ap()
        return self.dr[n]

    def dint(self, n, s, dt=F32):
        self.dr[n] = self.nc.dram_tensor(n, s, dt, kind="Internal").ap()
        return self.dr[n]

    def dstep(self, eng, q, ch, out, in_, qv=None):
        eng.dma_start(out=out, in_=in_).then_inc(q.h, 16)
        if qv is None:
            q.n += 16
            qv = q.n
        eng.wait_ge(q.h, qv)
        eng.sem_inc(ch.h, 1)

    def build(self):
        nc, T = self.nc, self.T
        self.din("xin", [D, T])
        self.out = nc.dram_tensor("out", [D, T], F32, kind="ExternalOutput").ap()
        self.din("meta", [1, 16])
        for n in ("wq", "wk", "wv", "wo", "bwk", "bwv"):
            self.din(n, [D, D])
        self.din("bqkvo", [1, 4 * D])
        self.din("bbkv", [1, 2 * D])
        self.din("rhw", [1, D])
        self.din("bww", [1, 8 * NH])
        self.din("gwt", [D, 8])
        self.din("gbr", [1, 8])
        self.din("w1c", [NE, D, HF]); self.din("w3c", [NE, D, HF])
        self.din("w2c", [NE, HF, D])
        self.din("b1c", [1, NE * HF]); self.din("b3c", [1, NE * HF])
        self.din("b2c", [1, NE * D])

        self.dint("rA", [D, T])
        self.dint("rB", [D, T])
        self.dint("hsd", [KD, 128, T], BF16)
        self.dint("zqfd", [KD, 128, T], BF16)
        self.dint("w1b", [NE, D, HF], BF16)
        self.dint("w3b", [NE, D, HF], BF16)
        self.dint("w2b", [NE, HF, D], BF16)
        self.dint("rwd", [NE, T])
        self.dint("kvx_i", [65, D]); self.dint("kvx_o", [65, D])
        NC = T // 128
        self.dint("r1x_i", [1, D]); self.dint("r1x_o", [2, D])
        self.dint("r2f_i", [4, D]); self.dint("r2f_o", [8, D])
        self.dint("r2b_i", [4, D]); self.dint("r2b_o", [8, D])
        self.dint("sind", [2, NC, 5, NH * HDM], BF16)
        self.dint("csrd", [2, 16, T], BF16)
        self.dint("flA", [16, NC * 4], BF16)
        self.dint("flB", [16, NC * 4], BF16)
        self.dint("flE", [1, NC * 16], BF16)

        self.ones_l = self.sb("ones_l", [1, 512], BF16)
        self.ones_c = self.sb("ones_c", [128, 1], BF16)
        self.ident = self.sb("ident", [128, 128], BF16)
        self.metat = self.sb("metat", [1, 16], F32)

        with nc.Block() as block:
            self.emit_setup(block)
            prev = self.setup_ch
            prev = self.emit_ln(block, self.dr["xin"], "n1", prev)
            prev = self.emit_attn_a(block, self.dr["xin"], self.dr["rA"], prev)
            prev = self.emit_ln(block, self.dr["rA"], "n2", prev)
            prev = self.emit_ffn(block, self.dr["rA"], self.dr["rB"], 0, prev)
            prev = self.emit_ln(block, self.dr["rB"], "n3", prev)
            prev = self.emit_attn_b(block, self.dr["rB"], self.dr["rA"], prev)
            prev = self.emit_ln(block, self.dr["rA"], "n4", prev)
            prev = self.emit_ffn(block, self.dr["rA"], self.out, 1, prev)
            self.final_ch = prev

        self.es.close()
        nc.finalize()
        return nc

    def emit_setup(self, block):
        nc = self.nc
        ch = self.chain("setup")
        self.setup_ch = ch
        self.wcast = self.chain("wcast")
        qsy = self.chain("su_qsy")
        d = self.dr

        @block.sync
        def _(sy):
            self.dstep(sy, qsy, ch, self.metat[:], d["meta"][:])
        ch.n += 1

        @block.gpsimd
        def _(g):
            g.iota(self.ident[:], pattern=[[-1, 128]], base=0, channel_multiplier=1,
                   allow_small_or_imprecise_dtypes=True).then_inc(ch.h, 1)
        ch.n += 1

        @block.vector
        def _(v):
            v.memset(self.ones_l[:], 1.0)
            v.memset(self.ones_c[:], 1.0)
            v.drain()
            v.wait_ge(ch.h, ch.n)
            v.tensor_scalar(self.ident[:], self.ident[:], 0.0, 1.0,
                            ALU.is_equal, ALU.mult)
            v.drain().then_inc(ch.h, 1)
        ch.n += 1

        @block.gpsimd
        def _(g):
            for e in range(NE):
                for a, b_ in (("w1c", "w1b"), ("w3c", "w3b"), ("w2c", "w2b")):
                    g.dma_start(out=d[b_][e], in_=d[a][e]).then_inc(self.wcast.h, 16)
                    self.wcast.n += 16

    def emit_ln(self, block, src, tag, prev):
        nc, T = self.nc, self.T
        TC = T // 512
        es = ExitStack()
        sbl = lambda n, sh, dt=F32: es.enter_context(nc.sbuf_tensor(tag + n, sh, dt))
        psl = lambda n, sh, dt=F32: es.enter_context(nc.psum_tensor(tag + n, sh, dt))
        d = self.dr

        xc = sbl("xc", [128, KD, 512])
        xb = sbl("xb", [128, KD, 512], BF16)
        sq = sbl("sq", [128, KD, 512], BF16)
        hs = sbl("hs", [128, KD, 512], BF16)
        rows = sbl("rows", [1, 6, 512])
        rbf = sbl("rbf", [1, 2, 512], BF16)
        pstat = psl("pst", [1, 2, 512])
        pbc = psl("pbc", [128, 2, 512])

        ch = self.chain(tag + "ch")
        qsy = self.chain(tag + "qsy")
        qg = self.chain(tag + "qg")
        src_r = src.rearrange("(k p) t -> p k t", p=128)
        hsd_r = d["hsd"].rearrange("k p t -> p k t")
        pn = prev.n
        DL = 10

        @block.sync
        def _(sy):
            sy.wait_ge(prev.h, pn)
            with sy.Fori(0, TC) as i:
                sy.wait_ge(ch.h, i * DL)
                self.dstep(sy, qsy, ch, xc[:], src_r[:, :, bass.ts(i, 512)],
                           qv=16 * (i + 1))

        @block.vector
        def _(v):
            with v.Fori(0, TC) as i:
                it = i * DL
                v.wait_ge(ch.h, it + 1)
                for k in range(KD):
                    inst = v.tensor_scalar(xb[:, k], xc[:, k], 1.0, None, ALU.mult)
                v.drain()
                inst.then_inc(ch.h, 1)
                v.wait_ge(ch.h, it + 4)
                v.tensor_scalar(rows[:, 0], pstat[:, 0], 1.0 / D, None, ALU.mult)
                v.tensor_scalar(rows[:, 1], pstat[:, 1], 1.0 / D, None, ALU.mult)
                v.drain()
                v.tensor_tensor(rows[:, 2], rows[:, 0], rows[:, 0], ALU.mult)
                v.drain()
                v.tensor_tensor(rows[:, 3], rows[:, 1], rows[:, 2], ALU.subtract)
                v.drain()
                inst = v.tensor_scalar(rows[:, 3], rows[:, 3], LN_EPS, None, ALU.add)
                v.drain()
                inst.then_inc(ch.h, 1)
                v.wait_ge(ch.h, it + 6)
                v.reciprocal(rows[:, 4], rows[:, 4])
                v.drain()
                v.tensor_tensor(rows[:, 5], rows[:, 0], rows[:, 4], ALU.mult)
                v.drain()
                v.tensor_scalar(rbf[:, 0], rows[:, 4], 1.0, None, ALU.mult)
                inst = v.tensor_scalar(rbf[:, 1], rows[:, 5], 1.0, None, ALU.mult)
                v.drain()
                inst.then_inc(ch.h, 1)
                v.wait_ge(ch.h, it + 8)
                for k in range(KD):
                    v.tensor_tensor(hs[:, k], xb[:, k], pbc[:, 0], ALU.mult)
                v.drain()
                for k in range(KD):
                    inst = v.tensor_tensor(hs[:, k], hs[:, k], pbc[:, 1], ALU.subtract)
                v.drain()
                inst.then_inc(ch.h, 1)

        @block.scalar
        def _(sc):
            with sc.Fori(0, TC) as i:
                it = i * DL
                sc.wait_ge(ch.h, it + 2)
                for k in range(KD):
                    inst = sc.activation(sq[:, k], xb[:, k], AF.Square)
                sc.drain()
                inst.then_inc(ch.h, 1)
                sc.wait_ge(ch.h, it + 5)
                inst = sc.activation(rows[:, 4], rows[:, 3], AF.Sqrt)
                sc.drain()
                inst.then_inc(ch.h, 1)

        @block.tensor
        def _(te):
            with te.Fori(0, TC) as i:
                it = i * DL
                te.wait_ge(ch.h, it + 3)
                for k in range(KD):
                    te.matmul(pstat[:, 0], self.ones_c[:], xb[:, k],
                              start=(k == 0), stop=(k == KD - 1))
                for k in range(KD):
                    inst = te.matmul(pstat[:, 1], self.ones_c[:], sq[:, k],
                                     start=(k == 0), stop=(k == KD - 1))
                inst.then_inc(ch.h, 1)
                te.wait_ge(ch.h, it + 7)
                te.matmul(pbc[:, 0], self.ones_l[:, 0:128], rbf[:, 0],
                          start=True, stop=True)
                inst = te.matmul(pbc[:, 1], self.ones_l[:, 0:128], rbf[:, 1],
                                 start=True, stop=True)
                inst.then_inc(ch.h, 1)

        @block.gpsimd
        def _(g):
            with g.Fori(0, TC) as i:
                it = i * DL
                g.wait_ge(ch.h, it + 9)
                self.dstep(g, qg, ch, hsd_r[:, :, bass.ts(i, 512)], hs[:],
                           qv=16 * (i + 1))

        ch.n = TC * DL
        qsy.n = 16 * TC
        qg.n = 16 * TC
        es.close()
        return ch

    def emit_attn_a(self, block, xsrc, rdst, prev):
        nc, T = self.nc, self.T
        TT = T // 128
        TC = T // 512
        es = ExitStack()
        sbl = lambda n, sh, dt=F32: es.enter_context(nc.sbuf_tensor("aa" + n, sh, dt))
        psl = lambda n, sh, dt=F32: es.enter_context(nc.psum_tensor("aa" + n, sh, dt))
        d = self.dr

        wqt = sbl("wqt", [128, KD, D], BF16)
        wkt = sbl("wkt", [128, KD, D], BF16)
        wvt = sbl("wvt", [128, KD, D], BF16)
        wot = sbl("wot", [128, KD, D], BF16)
        bflat = sbl("bflat", [1, 4 * D], BF16)
        hst = sbl("hst", [128, KD, 128], BF16)
        qf = sbl("qf", [128, TT, D], BF16)
        kft = sbl("kft", [128, D], BF16)
        vft = sbl("vft", [128, D], BF16)
        etmp = sbl("etmp", [128, 2, 512])
        ebf = sbl("ebf", [128, 2, 512], BF16)
        kvp = sbl("kvp", [65, D])
        kvbf = sbl("kvbf", [128, D], BF16)
        ksbc = sbl("ksbc", [128, D], BF16)
        zt = sbl("zt", [128, D])
        zr = sbl("zr", [128, NH])
        qstg = sbl("qstg", [128, D], BF16)
        ztile = sbl("ztile", [128, KD, 128], BF16)
        zqc = sbl("zqc", [128, KD, 512], BF16)
        xc2 = sbl("xc2", [128, KD, 512])
        ybuf = sbl("ybuf", [128, KD, 512], BF16)
        zerob = sbl("zerob", [128, 512], BF16)

        es1 = ExitStack()
        pproj = es1.enter_context(nc.psum_tensor("aapp", [128, 2, 512], F32))
        pkv = es1.enter_context(nc.psum_tensor("aapkv", [64, D], F32))
        pksum = es1.enter_context(nc.psum_tensor("aapks", [1, D], F32))

        ch = self.chain("aach")
        qsy = self.chain("aaqsy")
        qg = self.chain("aaqg")
        ccs = self.chain("aacc")
        pn = prev.n
        hsd_r = d["hsd"].rearrange("k p t -> p k t")
        xsrc_r = xsrc.rearrange("(k p) t -> p k t", p=128)
        rdst_r = rdst.rearrange("(k p) t -> p k t", p=128)
        zq_r = d["zqfd"].rearrange("k p t -> p k t")

        @block.gpsimd
        def _(g):
            g.wait_ge(prev.h, pn)
            for nm, t_ in (("wq", wqt), ("wk", wkt), ("wv", wvt), ("wo", wot)):
                self.dstep(g, qg, ch, t_[:],
                           d[nm].rearrange("(k p) n -> p k n", p=128))
                ch.n += 1
            self.dstep(g, qg, ch, bflat[:], d["bqkvo"][:])
            ch.n += 1

        @block.vector
        def _(v):
            v.wait_ge(ch.h, ch.n)
            v.memset(zerob[:], 0.0)
            v.drain().then_inc(ch.h, 1)
        ch.n += 1

        @block.tensor
        def _(te):
            te.wait_ge(ch.h, ch.n)
            for hf in range(2):
                te.matmul(pkv[:, hf * 512:(hf + 1) * 512], zerob[:, 0:64],
                          zerob[:], start=True, stop=False)
            for hf in range(2):
                inst = te.matmul(pksum[:, hf * 512:(hf + 1) * 512], self.ones_c[:],
                                 zerob[:], start=True, stop=False)
            inst.then_inc(ch.h, 1)
        ch.n += 1

        base = ch.n
        DEL = 14

        @block.sync
        def _(sy):
            with sy.Fori(0, TT) as i:
                sy.wait_ge(ch.h, base + i * DEL)
                self.dstep(sy, qsy, ch, hst[:], hsd_r[:, :, bass.ts(i, 128)],
                           qv=16 * (i + 1))

        @block.tensor
        def _(te):
            with te.Fori(0, TT) as i:
                it = base + i * DEL
                for off, wt, boff in ((1, wqt, 0), (6, wkt, D), (11, wvt, 2 * D)):
                    te.wait_ge(ch.h, it + off)
                    for hf in range(2):
                        for k in range(KD):
                            te.matmul(pproj[:, hf], hst[:, k],
                                      wt[:, k, hf * 512:(hf + 1) * 512],
                                      start=(k == 0), stop=False)
                        inst = te.matmul(pproj[:, hf], self.ones_l[:, 0:128],
                                         bflat[:, boff + hf * 512:boff + (hf + 1) * 512],
                                         start=False, stop=True)
                    inst.then_inc(ch.h, 1)
                te.wait_ge(ch.h, it + 13)
                for h in range(NH):
                    te.matmul(pkv[:, h * 64:(h + 1) * 64], kft[:, h * 64:(h + 1) * 64],
                              vft[:, h * 64:(h + 1) * 64], start=False, stop=False)
                for hf in range(2):
                    inst = te.matmul(pksum[:, hf * 512:(hf + 1) * 512], self.ones_c[:],
                                     kft[:, hf * 512:(hf + 1) * 512],
                                     start=False, stop=False)
                inst.then_inc(ch.h, 1)

        @block.scalar
        def _(sc):
            with sc.Fori(0, TT) as i:
                it = base + i * DEL
                sc.wait_ge(ch.h, it + 2)
                for hf in range(2):
                    inst = sc.activation(qf[:, i, hf * 512:(hf + 1) * 512],
                                         pproj[:, hf], AF.Relu)
                sc.drain()
                inst.then_inc(ch.h, 1)
                sc.wait_ge(ch.h, it + 4)
                for hf in range(2):
                    inst = sc.activation(ebf[:, hf], etmp[:, hf], AF.Exp)
                sc.drain()
                inst.then_inc(ch.h, 1)
                sc.wait_ge(ch.h, it + 7)
                for hf in range(2):
                    inst = sc.activation(kft[:, hf * 512:(hf + 1) * 512],
                                         pproj[:, hf], AF.Relu)
                sc.drain()
                inst.then_inc(ch.h, 1)
                sc.wait_ge(ch.h, it + 9)
                for hf in range(2):
                    inst = sc.activation(ebf[:, hf], etmp[:, hf], AF.Exp)
                sc.drain()
                inst.then_inc(ch.h, 1)

        @block.vector
        def _(v):
            with v.Fori(0, TT) as i:
                it = base + i * DEL
                v.wait_ge(ch.h, it + 3)
                for hf in range(2):
                    inst = v.tensor_scalar(etmp[:, hf], pproj[:, hf], 0.0, None, ALU.min)
                v.drain()
                inst.then_inc(ch.h, 1)
                v.wait_ge(ch.h, it + 5)
                for hf in range(2):
                    inst = v.tensor_tensor(qf[:, i, hf * 512:(hf + 1) * 512],
                                           qf[:, i, hf * 512:(hf + 1) * 512],
                                           ebf[:, hf], ALU.add)
                v.drain()
                inst.then_inc(ch.h, 1)
                v.wait_ge(ch.h, it + 8)
                for hf in range(2):
                    inst = v.tensor_scalar(etmp[:, hf], pproj[:, hf], 0.0, None, ALU.min)
                v.drain()
                inst.then_inc(ch.h, 1)
                v.wait_ge(ch.h, it + 10)
                for hf in range(2):
                    inst = v.tensor_tensor(kft[:, hf * 512:(hf + 1) * 512],
                                           kft[:, hf * 512:(hf + 1) * 512],
                                           ebf[:, hf], ALU.add)
                v.drain()
                inst.then_inc(ch.h, 1)
                v.wait_ge(ch.h, it + 12)
                for hf in range(2):
                    inst = v.tensor_scalar(vft[:, hf * 512:(hf + 1) * 512],
                                           pproj[:, hf], 1.0, None, ALU.mult)
                v.drain()
                inst.then_inc(ch.h, 1)

        ch.n = base + TT * DEL
        qsy.n = 16 * TT

        @block.tensor
        def _(te):
            te.wait_ge(ch.h, ch.n)
            for hf in range(2):
                te.matmul(pkv[:, hf * 512:(hf + 1) * 512], zerob[:, 0:64],
                          zerob[:], start=False, stop=True)
            for hf in range(2):
                inst = te.matmul(pksum[:, hf * 512:(hf + 1) * 512], self.ones_c[:],
                                 zerob[:], start=False, stop=True)
            inst.then_inc(ch.h, 1)
        ch.n += 1

        @block.vector
        def _(v):
            v.wait_ge(ch.h, ch.n)
            v.tensor_scalar(kvp[0:64, :], pkv[:], 1.0, None, ALU.mult)
            inst = v.tensor_scalar(kvp[64:65, :], pksum[:], 1.0, None, ALU.mult)
            v.drain()
            inst.then_inc(ch.h, 1)
        ch.n += 1

        es1.close()
        pksb = psl("pksb", [128, 2, 512])
        ptr = psl("ptr", [128, KD, 128], BF16)
        pyt = psl("pyt", [128, 512])
        pout = psl("pout", [128, 512])

        n0 = ch.n

        @block.gpsimd
        def _(g):
            g.wait_ge(ch.h, n0)
            self.dstep(g, qg, ch, d["kvx_i"][:], kvp[:])
            g.collective_compute("AllReduce", ALU.add, self.pairs,
                                 [d["kvx_i"].opt()], [d["kvx_o"].opt()]
                                 ).then_inc(ccs.h, 1)
            g.wait_ge(ccs.h, 1)
            g.sem_inc(ch.h, 1)
            self.dstep(g, qg, ch, kvbf[0:64, :], d["kvx_o"][0:64, :])
            self.dstep(g, qg, ch, kvbf[64:128, :], d["kvx_o"][0:64, :])
            self.dstep(g, qg, ch, kvp[64:65, :], d["kvx_o"][64:65, :])
        ch.n += 5

        @block.vector
        def _(v):
            v.wait_ge(ch.h, ch.n)
            v.tensor_scalar(ebf[0:1, 0, :], kvp[64:65, 0:512], 1.0, None, ALU.mult)
            inst = v.tensor_scalar(ebf[0:1, 1, :], kvp[64:65, 512:1024], 1.0, None,
                                   ALU.mult)
            v.drain()
            inst.then_inc(ch.h, 1)
        ch.n += 1

        @block.tensor
        def _(te):
            te.wait_ge(ch.h, ch.n)
            te.matmul(pksb[:, 0], self.ones_l[:, 0:128], ebf[0:1, 0, :],
                      start=True, stop=True)
            inst = te.matmul(pksb[:, 1], self.ones_l[:, 0:128], ebf[0:1, 1, :],
                             start=True, stop=True)
            inst.then_inc(ch.h, 1)
        ch.n += 1

        @block.vector
        def _(v):
            v.wait_ge(ch.h, ch.n)
            v.tensor_scalar(ksbc[:, 0:512], pksb[:, 0], 1.0, None, ALU.mult)
            inst = v.tensor_scalar(ksbc[:, 512:1024], pksb[:, 1], 1.0, None, ALU.mult)
            v.drain()
            inst.then_inc(ch.h, 1)
        ch.n += 1

        basez = ch.n
        DZ = 2

        @block.vector
        def _(v):
            with v.Fori(0, TT) as i:
                it = basez + i * DZ
                v.wait_ge(ch.h, it)
                v.tensor_tensor(zt[:], qf[:, i, :], ksbc[:], ALU.mult)
                v.drain()
                inst = v.tensor_reduce(zr[:, :, None],
                                       zt[:].rearrange("p (h e) -> p h e", h=NH),
                                       mybir.AxisListType.X, ALU.add)
                v.drain()
                inst.then_inc(ch.h, 1)
                v.tensor_scalar(zr[:], zr[:], ATT_EPS, None, ALU.add)
                v.drain()
                v.reciprocal(zr[:], zr[:])
                v.drain()
                inst = v.tensor_tensor(qf[:, i, :].rearrange("p (h e) -> p h e", h=NH),
                                       qf[:, i, :].rearrange("p (h e) -> p h e", h=NH),
                                       zr[:, :, None].to_broadcast([128, NH, HDM]),
                                       ALU.mult)
                v.drain()
                inst.then_inc(ch.h, 1)

        ch.n = basez + TT * DZ

        baset = ch.n
        DT_ = 4

        @block.vector
        def _(v):
            with v.Fori(0, TT) as i:
                it = baset + i * DT_
                v.wait_ge(ch.h, it)
                inst = v.tensor_scalar(qstg[:], qf[:, i, :], 1.0, None, ALU.mult)
                v.drain()
                inst.then_inc(ch.h, 1)
                v.wait_ge(ch.h, it + 2)
                for k in range(KD):
                    inst = v.tensor_scalar(ztile[:, k], ptr[:, k], 1.0, None, ALU.mult)
                v.drain()
                inst.then_inc(ch.h, 1)

        @block.tensor
        def _(te):
            with te.Fori(0, TT) as i:
                it = baset + i * DT_
                te.wait_ge(ch.h, it + 1)
                for k in range(KD):
                    inst = te.transpose(ptr[:, k], qstg[:, k * 128:(k + 1) * 128],
                                        self.ident[:])
                inst.then_inc(ch.h, 1)

        @block.gpsimd
        def _(g, q0=qg.n):
            with g.Fori(0, TT) as i:
                it = baset + i * DT_
                g.wait_ge(ch.h, it + 3)
                self.dstep(g, qg, ch, zq_r[:, :, bass.ts(i, 128)], ztile[:],
                           qv=q0 + 16 * (i + 1))

        qg.n += 16 * TT
        ch.n = baset + TT * DT_

        basey = ch.n
        DY = 35

        @block.sync
        def _(sy, q0=qsy.n):
            with sy.Fori(0, TC) as i:
                sy.wait_ge(ch.h, basey + i * DY)
                self.dstep(sy, qsy, ch, zqc[:], zq_r[:, :, bass.ts(i, 512)],
                           qv=q0 + 16 * (2 * i + 1))
                self.dstep(sy, qsy, ch, xc2[:], xsrc_r[:, :, bass.ts(i, 512)],
                           qv=q0 + 16 * (2 * i + 2))

        @block.tensor
        def _(te):
            with te.Fori(0, TC) as i:
                it = basey + i * DY
                for hm in range(KD):
                    te.wait_ge(ch.h, it + 2 + 2 * hm)
                    for sub in range(2):
                        h = 2 * hm + sub
                        hb = (h % 2) * 64
                        inst = te.matmul(pyt[sub * 64:(sub + 1) * 64, :],
                                         kvbf[hb:hb + 64, h * 64:(h + 1) * 64],
                                         zqc[hb:hb + 64, h // 2, :],
                                         start=True, stop=True)
                    inst.then_inc(ch.h, 1)
                for dt in range(KD):
                    te.wait_ge(ch.h, it + 18 + 2 * dt)
                    for k in range(KD):
                        te.matmul(pout[:], wot[:, k, dt * 128:(dt + 1) * 128],
                                  ybuf[:, k, :], start=(k == 0), stop=False)
                    inst = te.matmul(pout[:],
                                     bflat[:, 3 * D + dt * 128:3 * D + dt * 128 + 128],
                                     self.ones_l[:], start=False, stop=True)
                    inst.then_inc(ch.h, 1)

        @block.vector
        def _(v):
            with v.Fori(0, TC) as i:
                it = basey + i * DY
                for hm in range(KD):
                    v.wait_ge(ch.h, it + 3 + 2 * hm)
                    inst = v.tensor_scalar(ybuf[:, hm, :], pyt[:], 1.0, None, ALU.mult)
                    v.drain()
                    inst.then_inc(ch.h, 1)
                for dt in range(KD):
                    v.wait_ge(ch.h, it + 19 + 2 * dt)
                    inst = v.tensor_tensor(xc2[:, dt], pout[:], xc2[:, dt], ALU.add)
                    v.drain()
                    inst.then_inc(ch.h, 1)

        @block.gpsimd
        def _(g, q0=qg.n):
            with g.Fori(0, TC) as i:
                it = basey + i * DY
                g.wait_ge(ch.h, it + 34)
                self.dstep(g, qg, ch, rdst_r[:, :, bass.ts(i, 512)], xc2[:],
                           qv=q0 + 16 * (i + 1))

        qsy.n += 32 * TC
        qg.n += 16 * TC
        ch.n = basey + TC * DY
        es.close()
        return ch

    def emit_ffn(self, block, src, dst, call, prev):
        nc, T = self.nc, self.T
        TH = T // 2
        THC = TH // 512
        tg = f"f{call}"
        es = ExitStack()
        sbl = lambda n, sh, dt=F32: es.enter_context(nc.sbuf_tensor(tg + n, sh, dt))
        psl = lambda n, sh, dt=F32: es.enter_context(nc.psum_tensor(tg + n, sh, dt))
        d = self.dr

        gwtt = sbl("gwt", [128, KD, 8], BF16)
        gbbc = sbl("gbbc", [128, 8])
        b1s = sbl("b1s", [1, HF], BF16)
        b3s = sbl("b3s", [1, HF], BF16)
        b2s = sbl("b2s", [1, D], BF16)
        xst = sbl("xst", [128, KD, TH], BF16)
        w1t = sbl("w1t", [128, KD, HF], BF16)
        w3t = sbl("w3t", [128, KD, HF], BF16)
        w2t = sbl("w2t", [128, KHF, D], BF16)
        yac = sbl("yac", [128, KD, TH])
        hbuf = sbl("hbuf", [128, KHF, 512], BF16)
        ubuf = sbl("ubuf", [128, 512], BF16)
        tbuf = sbl("tbuf", [128, 512], BF16)
        lgb = sbl("lgb", [8, 512], BF16)
        sfx = sbl("sfx", [128, 4, 40])
        rwt = sbl("rwt", [128, 4, NE], BF16)
        rwev = sbl("rwev", [NE, 512])
        rwre = sbl("rwre", [1, TH])
        rwrb = sbl("rwrb", [1, TH], BF16)
        rwbcb = sbl("rwbcb", [128, 512], BF16)
        rc = sbl("rc", [128, KD, 128])

        pg = psl("pg", [8, 512])
        pboth = psl("pboth", [128, 544], BF16)
        ptt = pboth[:, 0:32].rearrange("p (a b) -> p a b", b=8)
        prw = pboth[0:NE, 32:544]
        ps1 = psl("ps1", [128, 2, 512])
        ps2 = psl("ps2", [128, 2, 512])
        pmix = psl("pmix", [128, 2, 512])
        prb = pmix[:, 0]

        ch = self.chain(tg + "ch")
        qsy = self.chain(tg + "qsy")
        qg = self.chain(tg + "qg")
        pn = prev.n
        src_r = src.rearrange("(k p) t -> p k t", p=128)
        dst_r = dst.rearrange("(k p) t -> p k t", p=128)
        hsd_r = d["hsd"].rearrange("k p t -> p k t")

        @block.gpsimd
        def _(g):
            g.wait_ge(prev.h, pn)
            if call == 0:
                g.wait_ge(self.wcast.h, self.wcast.n)
            self.dstep(g, qg, ch, gwtt[:],
                       d["gwt"].rearrange("(k p) e -> p k e", p=128))
            self.dstep(g, qg, ch, lgb[0:1, 0:8], d["gbr"][:])
        ch.n += 2

        @block.tensor
        def _(te):
            te.wait_ge(ch.h, ch.n)
            te.matmul(prb[:, 0:8], self.ones_l[:, 0:128], lgb[0:1, 0:8],
                      start=True, stop=True).then_inc(ch.h, 1)
        ch.n += 1

        @block.vector
        def _(v):
            v.wait_ge(ch.h, ch.n)
            v.tensor_scalar(gbbc[:], prb[:, 0:8], 1.0, None, ALU.mult)
            v.drain().then_inc(ch.h, 1)
        ch.n += 1

        for half in range(2):
            t0 = half * TH
            n0 = ch.n

            @block.gpsimd
            def _(g, t0=t0, n0=n0):
                g.wait_ge(ch.h, n0)
                self.dstep(g, qg, ch, xst[:], hsd_r[:, :, t0:t0 + TH])
            ch.n += 1

            n1 = ch.n

            @block.vector
            def _(v, n1=n1):
                v.wait_ge(ch.h, n1)
                for k in range(KD):
                    v.memset(yac[:, k], 0.0)
                v.drain()
                inst = v.memset(rwt[:, :, 8:9], 1.0)
                v.drain()
                inst.then_inc(ch.h, 1)
            ch.n += 1

            bg = ch.n
            DG = 9

            @block.tensor
            def _(te, bg=bg):
                with te.Fori(0, THC) as i:
                    it = bg + i * DG
                    te.wait_ge(ch.h, it)
                    for k in range(KD):
                        inst = te.matmul(pg[:], gwtt[:, k], xst[:, k, bass.ts(i, 512)],
                                         start=(k == 0), stop=(k == KD - 1))
                    inst.then_inc(ch.h, 1)
                    te.wait_ge(ch.h, it + 2)
                    for q in range(4):
                        inst = te.transpose(ptt[:, q], lgb[:, q * 128:(q + 1) * 128],
                                            self.ident[0:8, 0:8])
                    inst.then_inc(ch.h, 1)
                    te.wait_ge(ch.h, it + 6)
                    for q in range(4):
                        inst = te.transpose(prw[:, bass.ts(q, 128)], rwt[:, q, :],
                                            self.ident[:])
                    inst.then_inc(ch.h, 1)

            @block.scalar
            def _(sc, bg=bg):
                with sc.Fori(0, THC) as i:
                    it = bg + i * DG
                    sc.wait_ge(ch.h, it + 4)
                    for q in range(4):
                        inst = sc.activation(sfx[:, q, 2:10], sfx[:, q, 2:10], AF.Exp)
                    sc.drain()
                    inst.then_inc(ch.h, 1)

            @block.vector
            def _(v, bg=bg):
                with v.Fori(0, THC) as i:
                    it = bg + i * DG
                    v.wait_ge(ch.h, it + 1)
                    inst = v.tensor_scalar(lgb[:], pg[:], 1.0, None, ALU.mult)
                    v.drain()
                    inst.then_inc(ch.h, 1)
                    v.wait_ge(ch.h, it + 3)
                    for q in range(4):
                        v.tensor_reduce(sfx[:, q, 0:1], ptt[:, q],
                                        mybir.AxisListType.X, ALU.max)
                    v.drain()
                    for q in range(4):
                        inst = v.tensor_scalar(sfx[:, q, 2:10], ptt[:, q],
                                               sfx[:, q, 0:1], None, ALU.subtract)
                    v.drain()
                    inst.then_inc(ch.h, 1)
                    v.wait_ge(ch.h, it + 5)
                    for q in range(4):
                        v.tensor_reduce(sfx[:, q, 0:1], sfx[:, q, 2:10],
                                        mybir.AxisListType.X, ALU.add)
                    v.drain()
                    for q in range(4):
                        v.reciprocal(sfx[:, q, 0:1], sfx[:, q, 0:1])
                    v.drain()
                    for q in range(4):
                        v.tensor_scalar(sfx[:, q, 10:18], sfx[:, q, 2:10],
                                        sfx[:, q, 0:1], None, ALU.mult)
                    v.drain()
                    for q in range(4):
                        v.tensor_tensor(sfx[:, q, 18:26], sfx[:, q, 10:18],
                                        gbbc[:], ALU.add)
                    v.drain()
                    for q in range(4):
                        v.tensor_reduce(sfx[:, q, 1:2], sfx[:, q, 18:26],
                                        mybir.AxisListType.X, ALU.max)
                    v.drain()
                    for q in range(4):
                        v.tensor_scalar(sfx[:, q, 26:34], sfx[:, q, 18:26],
                                        sfx[:, q, 1:2], None, ALU.is_equal)
                    v.drain()
                    for q in range(4):
                        v.scalar_tensor_tensor(sfx[:, q, 26:34], sfx[:, q, 26:34],
                                               -1e9, sfx[:, q, 18:26],
                                               ALU.mult, ALU.add)
                    v.drain()
                    for q in range(4):
                        v.tensor_reduce(sfx[:, q, 1:2], sfx[:, q, 26:34],
                                        mybir.AxisListType.X, ALU.max)
                    v.drain()
                    for q in range(4):
                        v.tensor_scalar(sfx[:, q, 26:34], sfx[:, q, 18:26],
                                        sfx[:, q, 1:2], None, ALU.is_ge)
                    v.drain()
                    for q in range(4):
                        inst = v.tensor_tensor(rwt[:, q, 0:8], sfx[:, q, 10:18],
                                               sfx[:, q, 26:34], ALU.mult)
                    v.drain()
                    inst.then_inc(ch.h, 1)
                    v.wait_ge(ch.h, it + 7)
                    inst = v.tensor_scalar(rwev[:], prw[:], 1.0, None, ALU.mult)
                    v.drain()
                    inst.then_inc(ch.h, 1)

            @block.gpsimd
            def _(g, bg=bg, t0=t0, q0=qg.n):
                with g.Fori(0, THC) as i:
                    it = bg + i * DG
                    g.wait_ge(ch.h, it + 8)
                    self.dstep(g, qg, ch, d["rwd"][:, t0:t0 + TH][:, bass.ts(i, 512)],
                               rwev[:], qv=q0 + 16 * (i + 1))

            qg.n += 16 * THC
            ch.n = bg + THC * DG

            be = ch.n
            DTC = 3
            DE = 2 + THC * DTC

            @block.gpsimd
            def _(g, be=be, t0=t0, q0=qg.n):
                with g.Fori(0, NE) as e:
                    it = be + e * DE
                    g.wait_ge(ch.h, it)
                    qv = q0 + e * 112
                    g.dma_start(out=w1t[:],
                                in_=d["w1b"].rearrange("e (k p) h -> e p k h", p=128)[e]
                                ).then_inc(qg.h, 16)
                    g.dma_start(out=w3t[:],
                                in_=d["w3b"].rearrange("e (k p) h -> e p k h", p=128)[e]
                                ).then_inc(qg.h, 16)
                    g.dma_start(out=w2t[:],
                                in_=d["w2b"].rearrange("e (k p) h -> e p k h", p=128)[e]
                                ).then_inc(qg.h, 16)
                    g.dma_start(out=rwre[:], in_=d["rwd"][e, t0:t0 + TH][None, :]
                                ).then_inc(qg.h, 16)
                    g.dma_start(out=b1s[:], in_=d["b1c"][:, bass.ts(e, HF)]
                                ).then_inc(qg.h, 16)
                    g.dma_start(out=b3s[:], in_=d["b3c"][:, bass.ts(e, HF)]
                                ).then_inc(qg.h, 16)
                    g.dma_start(out=b2s[:], in_=d["b2c"][:, bass.ts(e, D)]
                                ).then_inc(qg.h, 16)
                    g.wait_ge(qg.h, qv + 112)
                    g.sem_inc(ch.h, 1)

            shm = self.chain(tg + f"hm{half}")
            sha_ = self.chain(tg + f"ha{half}")
            shd = self.chain(tg + f"hd{half}")
            she = self.chain(tg + f"he{half}")
            swm = self.chain(tg + f"wm{half}")
            swe = self.chain(tg + f"we{half}")

            @block.vector
            def _(v, be=be):
                with v.Fori(0, NE) as e:
                    it = be + e * DE
                    v.wait_ge(ch.h, it + 1)
                    inst = v.tensor_scalar(rwrb[:], rwre[:], 1.0, None, ALU.mult)
                    v.drain()
                    inst.then_inc(ch.h, 1)
                    with v.Fori(0, THC) as i:
                        jt = it + 2 + i * DTC
                        u11 = (e * THC + i) * KHF
                        u8 = (e * THC + i) * KD
                        v.wait_ge(ch.h, jt + 1)
                        inst = v.tensor_scalar(rwbcb[:], prb[:], 1.0, None, ALU.mult)
                        v.drain()
                        inst.then_inc(ch.h, 1)
                        for ht in range(KHF):
                            par = ht % 2
                            v.wait_ge(sha_.h, u11 + ht + 1)
                            v.wait_ge(shm.h, u11 + ht + 1)
                            if ht >= 1:
                                v.wait_ge(she.h, u11 + ht)
                            inst = v.tensor_tensor(tbuf[:], ps1[:, par], ubuf[:],
                                                   ALU.mult)
                            v.drain()
                            inst.then_inc(shd.h, 1)
                        for dt in range(KD):
                            par = dt % 2
                            v.wait_ge(swm.h, u8 + dt + 1)
                            v.tensor_tensor(ubuf[:], pmix[:, par], rwbcb[:], ALU.mult)
                            v.drain()
                            inst = v.tensor_tensor(yac[:, dt, bass.ts(i, 512)],
                                                   yac[:, dt, bass.ts(i, 512)],
                                                   ubuf[:], ALU.add)
                            v.drain()
                            inst.then_inc(swe.h, 1)
                            if dt == KD - 1:
                                v.sem_inc(ch.h, 1)

            @block.scalar
            def _(sc, be=be):
                with sc.Fori(0, NE) as e:
                    with sc.Fori(0, THC) as i:
                        u11 = (e * THC + i) * KHF
                        for ht in range(KHF):
                            par = ht % 2
                            sc.wait_ge(shm.h, u11 + ht + 1)
                            if ht >= 1:
                                sc.wait_ge(shd.h, u11 + ht)
                            inst = sc.activation(ubuf[:], ps2[:, par], AF.Copy)
                            sc.drain()
                            inst.then_inc(sha_.h, 1)
                            sc.wait_ge(shd.h, u11 + ht + 1)
                            inst = sc.activation(hbuf[:, ht], tbuf[:], AF.Silu)
                            sc.drain()
                            inst.then_inc(she.h, 1)

            @block.tensor
            def _(te, be=be):
                with te.Fori(0, NE) as e:
                    it = be + e * DE
                    with te.Fori(0, THC) as i:
                        jt = it + 2 + i * DTC
                        u11 = (e * THC + i) * KHF
                        u8 = (e * THC + i) * KD
                        te.wait_ge(ch.h, jt)
                        te.matmul(prb[:], self.ones_l[:, 0:128],
                                  rwrb[:, bass.ts(i, 512)],
                                  start=True, stop=True).then_inc(ch.h, 1)
                        te.wait_ge(ch.h, jt + 2)
                        for ht in range(KHF):
                            par = ht % 2
                            if ht >= 2:
                                te.wait_ge(sha_.h, u11 + ht - 1)
                                te.wait_ge(shd.h, u11 + ht - 1)
                            for k in range(KD):
                                te.matmul(ps1[:, par],
                                          w1t[:, k, ht * 128:(ht + 1) * 128],
                                          xst[:, k, bass.ts(i, 512)],
                                          start=(k == 0), stop=False)
                            te.matmul(ps1[:, par], b1s[:, ht * 128:(ht + 1) * 128],
                                      self.ones_l[:], start=False, stop=True)
                            for k in range(KD):
                                te.matmul(ps2[:, par],
                                          w3t[:, k, ht * 128:(ht + 1) * 128],
                                          xst[:, k, bass.ts(i, 512)],
                                          start=(k == 0), stop=False)
                            inst = te.matmul(ps2[:, par],
                                             b3s[:, ht * 128:(ht + 1) * 128],
                                             self.ones_l[:], start=False, stop=True)
                            inst.then_inc(shm.h, 1)
                        te.wait_ge(she.h, u11 + KHF)
                        for dt in range(KD):
                            par = dt % 2
                            if dt >= 2:
                                te.wait_ge(swe.h, u8 + dt - 1)
                            for k in range(KHF):
                                te.matmul(pmix[:, par],
                                          w2t[:, k, dt * 128:(dt + 1) * 128],
                                          hbuf[:, k], start=(k == 0), stop=False)
                            inst = te.matmul(pmix[:, par],
                                             b2s[:, dt * 128:(dt + 1) * 128],
                                             self.ones_l[:], start=False, stop=True)
                            inst.then_inc(swm.h, 1)

            qg.n += 112 * NE
            ch.n = be + NE * DE

            br = ch.n
            DR = 3
            NRC = TH // 128

            @block.sync
            def _(sy, br=br, t0=t0, q0=qsy.n):
                with sy.Fori(0, NRC) as i:
                    sy.wait_ge(ch.h, br + i * DR)
                    self.dstep(sy, qsy, ch,
                               rc[:], src_r[:, :, t0:t0 + TH][:, :, bass.ts(i, 128)],
                               qv=q0 + 16 * (i + 1))

            @block.vector
            def _(v, br=br):
                with v.Fori(0, NRC) as i:
                    it = br + i * DR
                    v.wait_ge(ch.h, it + 1)
                    for k in range(KD):
                        inst = v.tensor_tensor(rc[:, k],
                                               yac[:, k, bass.ts(i, 128)],
                                               rc[:, k], ALU.add)
                    v.drain()
                    inst.then_inc(ch.h, 1)

            @block.gpsimd
            def _(g, br=br, t0=t0, q0=qg.n):
                with g.Fori(0, NRC) as i:
                    it = br + i * DR
                    g.wait_ge(ch.h, it + 2)
                    self.dstep(g, qg, ch,
                               dst_r[:, :, t0:t0 + TH][:, :, bass.ts(i, 128)], rc[:],
                               qv=q0 + 16 * (i + 1))

            qsy.n += 16 * NRC
            qg.n += 16 * NRC
            ch.n = br + NRC * DR

        es.close()
        return ch

    def emit_attn_b(self, block, src, dst, prev):
        nc, T = self.nc, self.T
        NC = T // 128
        POI = 8192.0
        BIG = 30000.0
        es = ExitStack()
        sbl = lambda n, sh, dt=F32: es.enter_context(nc.sbuf_tensor("ab" + n, sh, dt))
        d = self.dr

        bwr = sbl("bwr", [1, 128])
        bwb = sbl("bwb", [1, 128], BF16)
        i4b = sbl("i4b", [1, 4], BF16)
        iota5c = sbl("iota5c", [5, 1])
        mcol = sbl("mcol", [16, 2])
        mcol2 = sbl("mcol2", [16, 1])
        metab = sbl("metab", [1, 2], BF16)
        swbB = sbl("swbB", [1, 16])
        e0sw = sbl("e0sw", [1, NH], BF16)
        V_tm = sbl("V_tm", [128, NC, D], BF16)
        validr = sbl("validr", [16, T], BF16)
        zq16 = sbl("zq16", [16, 1])
        csum = sbl("csum", [16, T])
        csvS = sbl("csvS", [16, T // 128])
        csum_reb = sbl("csumr", [16, T], BF16)
        csv_reb = sbl("csvr", [16, T], BF16)
        arow = sbl("arow", [16, T], BF16)
        Cfr = sbl("Cfr", [16, T], BF16)
        trr = sbl("trr", [16, T], BF16)
        Cbr = trr
        tmpI = sbl("tmpI", [16, NC * 4])
        tmpJ = sbl("tmpJ", [16, NC * 4], BF16)
        tmpJ2 = sbl("tmpJ2", [16, NC * 4], BF16)
        cntst4f = sbl("cst4", [1, 16 * NC * 4], BF16)
        cntch4f = sbl("cch4", [1, 16 * NC * 4], BF16)
        Erow = sbl("Erow", [1, NC * 16], BF16)
        Cbs = sbl("Cbs", [5, NC * 16], BF16)
        Ccol_f = sbl("Ccf", [128, NC * 16])
        Ccol_b = sbl("Ccb", [128, NC * 16])
        r1t = sbl("r1t", [2, D], BF16)
        r1tb = sbl("r1tb", [1, D], BF16)
        ownl = sbl("ownl", [1, D], BF16)
        vshfix = sbl("vshf", [1, D], BF16)

        esA = ExitStack()
        pA = esA.enter_context(nc.psum_tensor("abpA", [128, 2048], F32))

        ch = self.chain("abch")
        qsy = self.chain("abqsy")
        qg = self.chain("abqg")
        ccs = self.chain("abcc")
        pn = prev.n
        hsd_r = d["hsd"].rearrange("k p t -> p k t")
        src_r = src.rearrange("(k p) t -> p k t", p=128)
        dst_r = dst.rearrange("(k p) t -> p k t", p=128)

        esP = ExitStack()
        pT = esP.enter_context(nc.psum_tensor("abpT", [128, 1024], BF16))
        sblP = lambda n, sh, dt=F32: esP.enter_context(nc.sbuf_tensor("ab" + n, sh, dt))
        bwkt = sblP("bwkt", [128, KD, D], BF16)
        bwvt = sblP("bwvt", [128, KD, D], BF16)
        bbf = sblP("bbf", [1, 2 * D], BF16)
        rhr = sblP("rhr", [1, D], BF16)
        rhbc = sblP("rhbc", [128, D], BF16)
        hst2 = sblP("hst2", [128, KD, 128], BF16)
        k1t = sblP("k1t", [128, D], BF16)
        kdt = sblP("kdt", [128, 512])
        dcol = sblP("dcol", [128, NH])
        vt_tm = sblP("vt_tm", [128, NH], BF16)
        @block.gpsimd
        def _(g):
            g.wait_ge(prev.h, pn)
            self.dstep(g, qg, ch, bwkt[:], d["bwk"].rearrange("(k p) n -> p k n", p=128))
            self.dstep(g, qg, ch, bwvt[:], d["bwv"].rearrange("(k p) n -> p k n", p=128))
            self.dstep(g, qg, ch, bbf[:], d["bbkv"][:])
            self.dstep(g, qg, ch, rhr[:], d["rhw"][:])
            self.dstep(g, qg, ch, bwr[:], d["bww"][:])
            g.iota(tmpI[:], pattern=[[0, NC], [1, 4]], base=0, channel_multiplier=0,
                   allow_small_or_imprecise_dtypes=True)
            g.iota(iota5c[:], pattern=[[0, 1]], base=0, channel_multiplier=1,
                   allow_small_or_imprecise_dtypes=True).then_inc(ch.h, 1)
        ch.n += 6

        @block.vector
        def _(v):
            v.wait_ge(ch.h, ch.n)
            v.tensor_scalar(metab[:], self.metat[:, 0:2], 1.0, None, ALU.mult)
            v.tensor_scalar(bwb[:], bwr[:], 1.0, None, ALU.mult)
            v.memset(i4b[:], 0.0)
            v.memset(zq16[:], 0.0)
            v.memset(e0sw[:], 0.0)
            v.drain()
            for j in range(4):
                v.memset(i4b[:, j:j + 1], float(j))
            inst = v.tensor_reduce(swbB[:, :, None],
                                   bwr[:, 64:128].rearrange("o (r c) -> o c r", c=16),
                                   mybir.AxisListType.X, ALU.add)
            v.drain()
            inst.then_inc(ch.h, 1)
        ch.n += 1

        @block.tensor
        def _(te):
            te.wait_ge(ch.h, ch.n)
            te.matmul(pA[:, 0:512], self.ones_l[:, 0:128], rhr[:, 0:512],
                      start=True, stop=True)
            te.matmul(pA[:, 512:1024], self.ones_l[:, 0:128], rhr[:, 512:1024],
                      start=True, stop=True)
            inst = te.matmul(pA[0:16, 1024:1026], self.ones_l[:, 0:16], metab[:],
                             start=True, stop=True)
            inst.then_inc(ch.h, 1)
        ch.n += 1

        @block.vector
        def _(v):
            v.wait_ge(ch.h, ch.n)
            v.tensor_scalar(rhbc[:, 0:512], pA[:, 0:512], 1.0, None, ALU.mult)
            v.tensor_scalar(rhbc[:, 512:1024], pA[:, 512:1024], 1.0, None, ALU.mult)
            v.tensor_scalar(mcol[:], pA[0:16, 1024:1026], 1.0, None, ALU.mult)
            v.drain()
            v.tensor_scalar(mcol2[:], mcol[:, 1:2], BIG, None, ALU.mult)
            v.drain()
            v.tensor_scalar(e0sw[:], swbB[:], self.metat[:, 1:2], None, ALU.mult)
            v.drain()
            v.sem_inc(ch.h, 1)
        ch.n += 1


        bp = ch.n
        DP = 9

        @block.sync
        def _(sy, q0=qsy.n):
            with sy.Fori(0, NC) as i:
                sy.wait_ge(ch.h, bp + i * DP)
                self.dstep(sy, qsy, ch, hst2[:], hsd_r[:, :, bass.ts(i, 128)],
                           qv=q0 + 16 * (i + 1))

        @block.tensor
        def _(te):
            with te.Fori(0, NC) as i:
                it = bp + i * DP
                te.wait_ge(ch.h, it + 1)
                for hf in range(2):
                    for k in range(KD):
                        te.matmul(pA[:, hf * 512:(hf + 1) * 512], hst2[:, k],
                                  bwvt[:, k, hf * 512:(hf + 1) * 512],
                                  start=(k == 0), stop=False)
                    inst = te.matmul(pA[:, hf * 512:(hf + 1) * 512],
                                     self.ones_l[:, 0:128],
                                     bbf[:, D + hf * 512:D + (hf + 1) * 512],
                                     start=False, stop=True)
                inst.then_inc(ch.h, 1)
                te.wait_ge(ch.h, it + 3)
                for hf in range(2):
                    for k in range(KD):
                        te.matmul(pA[:, hf * 512:(hf + 1) * 512], hst2[:, k],
                                  bwkt[:, k, hf * 512:(hf + 1) * 512],
                                  start=(k == 0), stop=False)
                    inst = te.matmul(pA[:, hf * 512:(hf + 1) * 512],
                                     self.ones_l[:, 0:128],
                                     bbf[:, hf * 512:(hf + 1) * 512],
                                     start=False, stop=True)
                inst.then_inc(ch.h, 1)
                te.wait_ge(ch.h, it + 7)
                inst = te.transpose(pT[0:16, 0:128], vt_tm[:], self.ident[:])
                inst.then_inc(ch.h, 1)

        @block.scalar
        def _(sc):
            with sc.Fori(0, NC) as i:
                it = bp + i * DP
                sc.wait_ge(ch.h, it + 2)
                for hf in range(2):
                    inst = sc.activation(V_tm[:, i, hf * 512:(hf + 1) * 512],
                                         pA[:, hf * 512:(hf + 1) * 512], AF.Relu)
                sc.drain()
                inst.then_inc(ch.h, 1)
                sc.wait_ge(ch.h, it + 4)
                for hf in range(2):
                    inst = sc.activation(k1t[:, hf * 512:(hf + 1) * 512],
                                         pA[:, hf * 512:(hf + 1) * 512], AF.Relu)
                sc.drain()
                inst.then_inc(ch.h, 1)

        @block.vector
        def _(v):
            with v.Fori(0, NC) as i:
                it = bp + i * DP
                v.wait_ge(ch.h, it + 5)
                inst = v.tensor_tensor(kdt[:], k1t[:, 0:512], rhbc[:, 0:512], ALU.mult)
                v.drain()
                inst = v.tensor_reduce(dcol[:, 0:8][:, :, None],
                                       kdt[:].rearrange("p (h e) -> p h e", h=8),
                                       mybir.AxisListType.X, ALU.add)
                v.drain()
                inst = v.tensor_tensor(kdt[:], k1t[:, 512:1024], rhbc[:, 512:1024], ALU.mult)
                v.drain()
                inst.then_inc(ch.h, 1)
                inst = v.tensor_reduce(dcol[:, 8:16][:, :, None],
                                       kdt[:].rearrange("p (h e) -> p h e", h=8),
                                       mybir.AxisListType.X, ALU.add)
                v.drain()
                inst = v.tensor_scalar(vt_tm[:], dcol[:], 0.5, None, ALU.is_gt)
                v.drain()
                inst.then_inc(ch.h, 1)
                v.wait_ge(ch.h, it + 8)
                inst = v.tensor_scalar(validr[:, bass.ts(i, 128)], pT[0:16, 0:128],
                                       1.0, None, ALU.mult)
                v.drain()
                inst.then_inc(ch.h, 1)

        qsy.n += 16 * NC
        ch.n = bp + NC * DP

        n0 = ch.n

        @block.gpsimd
        def _(g, n0=n0):
            g.wait_ge(ch.h, n0)
            self.dstep(g, qg, ch, d["r1x_i"][:], V_tm[0:1, 0, :])
            g.collective_compute("AllGather", ALU.bypass, self.pairs,
                                 [d["r1x_i"].opt()], [d["r1x_o"].opt()]
                                 ).then_inc(ccs.h, 1)
            g.wait_ge(ccs.h, 1)
            g.sem_inc(ch.h, 1)
            self.dstep(g, qg, ch, r1t[:], d["r1x_o"][:])
            self.dstep(g, qg, ch, r1tb[:], d["r1x_o"][1:2, :])
            self.dstep(g, qg, ch, ownl[:], V_tm[127:128, NC - 1, :])
        ch.n += 5

        @block.vector
        def _(v):
            v.wait_ge(ch.h, ch.n)
            v.tensor_scalar(vshfix[:], r1tb[:], self.metat[:, 1:2], None, ALU.mult)
            v.tensor_scalar(ownl[:], ownl[:], self.metat[:, 0:1], None, ALU.mult)
            v.drain()
            v.tensor_tensor(vshfix[:], vshfix[:], ownl[:], ALU.add)
            v.drain()
            inst = v.tensor_scalar(validr[:, 0:1], validr[:, 0:1],
                                   mcol[:, 0:1], None, ALU.mult)
            v.drain()
            inst.then_inc(ch.h, 1)
            v.tensor_tensor_scan(csum[:], validr[:],
                                 zq16[:].to_broadcast([16, T]), 0.0,
                                 ALU.add, ALU.add)
            v.drain()
            v.tensor_tensor(csvS[:], csum[:, 0:T:128], validr[:, 0:T:128],
                            ALU.subtract)
            v.drain()
            v.tensor_tensor(csum_reb[:].rearrange("p (c u) -> p c u", u=128),
                            csum[:].rearrange("p (c u) -> p c u", u=128),
                            csvS[:, :, None].to_broadcast([16, NC, 128]),
                            ALU.subtract)
            v.drain()
            v.tensor_tensor(csv_reb[:], csum_reb[:], validr[:], ALU.subtract)
            v.drain()
            v.tensor_scalar(arow[:], validr[:], -POI, POI, ALU.mult, ALU.add)
            v.drain()
            v.tensor_tensor(Cfr[:], csum_reb[:], arow[:], ALU.add)
            v.tensor_scalar(trr[:], csum_reb[:], 1.0, None, ALU.subtract)
            v.drain()
            v.tensor_tensor(trr[:], trr[:], arow[:], ALU.subtract)
            v.drain()
            v.tensor_scalar(csv_reb[:, 0:1], csv_reb[:, 0:1], mcol2[:, 0:1],
                            None, ALU.add)
            v.drain()
            v.tensor_tensor(tmpJ[:].rearrange("p (c j) -> p c j", j=4),
                            csum_reb[:, 127:T:128][:, :, None].to_broadcast([16, NC, 4]),
                            tmpI[:].rearrange("p (c j) -> p c j", j=4),
                            ALU.subtract)
            v.drain()
            inst = v.tensor_scalar(tmpJ2[:], tmpJ[:], -1.0, None, ALU.mult)
            v.drain()
            inst.then_inc(ch.h, 1)
        ch.n += 2

        @block.gpsimd
        def _(g):
            g.wait_ge(ch.h, ch.n)
            self.dstep(g, qg, ch, d["csrd"][0], csum_reb[:])
            self.dstep(g, qg, ch, d["csrd"][1], csv_reb[:])
            self.dstep(g, qg, ch, d["flA"][:], tmpJ[:])
            self.dstep(g, qg, ch, d["flB"][:], tmpJ2[:])
            g.dma_start(out=d["flE"].rearrange("o (c ch) -> o c ch", ch=16),
                        in_=d["csrd"][0].rearrange("ch (c u) -> u c ch", u=128)[127:128, :, :],
                        allow_slow_non_contiguous=True).then_inc(qg.h, 16)
            qg.n += 16
            g.wait_ge(qg.h, qg.n)
            g.sem_inc(ch.h, 1)
            self.dstep(g, qg, ch, cntst4f[:],
                       d["flA"].rearrange("a b -> (a b)")[None, :])
            self.dstep(g, qg, ch, cntch4f[:],
                       d["flB"].rearrange("a b -> (a b)")[None, :])
            self.dstep(g, qg, ch, Erow[:], d["flE"][:])
        ch.n += 8

        @block.tensor
        def _(te):
            te.wait_ge(ch.h, ch.n)
            inst = te.matmul(pA[0:5, 0:NC * 16], self.ones_l[:, 0:5], Erow[:],
                             start=True, stop=True)
            inst.then_inc(ch.h, 1)
        ch.n += 1

        @block.vector
        def _(v):
            v.wait_ge(ch.h, ch.n)
            inst = v.tensor_scalar(Cbs[:], pA[0:5, 0:NC * 16], iota5c[:], None,
                                   ALU.add)
            v.drain()
            inst.then_inc(ch.h, 1)
        ch.n += 1

        cstg = sblP("cstg", [16, 128], BF16)
        bc = ch.n
        DC = 6

        @block.tensor
        def _(te):
            with te.Fori(0, NC) as i:
                it = bc + i * DC
                te.wait_ge(ch.h, it + 1)
                te.transpose(pT[:, 0:16], cstg[:],
                             self.ident[0:16, 0:16]).then_inc(ch.h, 1)
                te.wait_ge(ch.h, it + 4)
                te.transpose(pT[:, 16:32], cstg[:],
                             self.ident[0:16, 0:16]).then_inc(ch.h, 1)

        @block.vector
        def _(v):
            with v.Fori(0, NC) as i:
                it = bc + i * DC
                v.wait_ge(ch.h, it)
                inst = v.tensor_scalar(cstg[:], Cfr[:, bass.ts(i, 128)],
                                       1.0, None, ALU.mult)
                v.drain()
                inst.then_inc(ch.h, 1)
                v.wait_ge(ch.h, it + 2)
                inst = v.tensor_scalar(Ccol_f[:, bass.ts(i, 16)], pT[:, 0:16],
                                       1.0, None, ALU.mult)
                v.drain()
                inst.then_inc(ch.h, 1)
                inst = v.tensor_scalar(cstg[:], Cbr[:, bass.ts(i, 128)],
                                       1.0, None, ALU.mult)
                v.drain()
                inst.then_inc(ch.h, 1)
                v.wait_ge(ch.h, it + 5)
                inst = v.tensor_scalar(Ccol_b[:, bass.ts(i, 16)], pT[:, 16:32],
                                       1.0, None, ALU.mult)
                v.drain()
                inst.then_inc(ch.h, 1)

        ch.n = bc + NC * DC
        esP.close()

        Scur_f = sbl("Scf", [5, NH * HDM], BF16)
        Scur_b = sbl("Scb", [5, NH * HDM], BF16)
        msf = sbl("msf", [128, NH * 4], BF16)
        mss = sbl("mss", [5, NH * 4], BF16)
        Vsb = sbl("Vsb", [128, D], BF16)
        Vtmp = sbl("Vtmp", [128, D], BF16)
        Vsf = sbl("Vsf", [128, D], BF16)
        sfs = sbl("sfs", [5, NH * HDM], BF16)
        sbs = sbl("sbs", [5, NH * HDM], BF16)
        stage5f = sbl("st5f", [5, 16 * 128], BF16)
        stage5b = sbl("st5b", [5, 16 * 128], BF16)
        Dlf = sbl("Dlf", [128, 2048], BF16)
        Mf = sbl("Mf", [128, 2048], BF16)
        Dfs = sbl("Dfs", [5, 2048], BF16)
        Mfs = sbl("Mfs", [5, 2048], BF16)
        tmp5 = sbl("tmp5", [5, 2048], BF16)
        wex = [sbl(f"wex{k}", [128, 16], BF16) for k in range(8)]
        zg = sbl("zg", [128, 512], BF16)
        l0f = sbl("l0f", [4, D], BF16)
        l1b = sbl("l1b", [4, D], BF16)
        rstg = sbl("rstg", [128, KD, 128])
        padrep = sbl("padrep", [4, D], BF16)

        for k in range(8):
            nk = ch.n

            @block.tensor
            def _(te, k=k, nk=nk):
                te.wait_ge(ch.h, nk)
                inst = te.matmul(pA[:, 0:16], self.ones_l[:, 0:128],
                                 bwb[:, k * 16:(k + 1) * 16],
                                 start=True, stop=True)
                inst.then_inc(ch.h, 1)
            ch.n += 1
            nk = ch.n

            @block.vector
            def _(v, k=k, nk=nk):
                v.wait_ge(ch.h, nk)
                inst = v.tensor_scalar(wex[k][:], pA[:, 0:16], 1.0, None, ALU.mult)
                v.drain()
                inst.then_inc(ch.h, 1)
            ch.n += 1

        esC = ExitStack()
        psS = esC.enter_context(nc.psum_tensor("abpsS", [4, 1024], F32))

        n0 = ch.n

        @block.gpsimd
        def _(g, n0=n0):
            g.wait_ge(ch.h, n0)
            self.dstep(g, qg, ch, padrep[:, None, :],
                       d["r1x_o"][None, 0:1, :].to_broadcast([4, 1, D]))
            self.dstep(g, qg, ch, Scur_f[:, None, :],
                       d["r1x_o"][None, 0:1, :].to_broadcast([5, 1, D]))
            self.dstep(g, qg, ch, Scur_b[:, None, :],
                       d["r1x_o"][None, 0:1, :].to_broadcast([5, 1, D]))
        ch.n += 3

        def emit_chain(fwd):
            b0 = ch.n
            DCH = 8 if not fwd else 7
            Scur = Scur_f if fwd else Scur_b
            dirIdx = 0 if fwd else 1
            Ccol = Ccol_f if fwd else Ccol_b
            Vget = V_tm if fwd else Vsb

            @block.gpsimd
            def _(g, b0=b0, q0=qg.n):
                with g.Fori(0, NC) as c:
                    it = b0 + c * DCH
                    cc = nc.s_assert_le(nc.s_assert_ge(NC - 1 - c, 0), NC - 1) if not fwd else c
                    g.wait_ge(ch.h, it)
                    npd = 4 if not fwd else 1
                    if fwd:
                        self.dstep(g, qg, ch, d["sind"][dirIdx, cc], Scur[:],
                                   qv=q0 + 16 * (npd * c + 1))
                    else:
                        g.dma_start(out=d["sind"][dirIdx, cc], in_=Scur[:]
                                    ).then_inc(qg.h, 16)
                        g.dma_start(out=Vtmp[:], in_=V_tm[:, cc, :]
                                    ).then_inc(qg.h, 16)
                        g.wait_ge(qg.h, q0 + 16 * (npd * c + 2))
                        g.sem_inc(ch.h, 1)
                        g.dma_start(out=Vsb[0:127, :], in_=Vtmp[1:128, :]
                                    ).then_inc(qg.h, 16)
                        with g.If(cc < NC - 1):
                            ccp = nc.s_assert_le(cc + 1, NC - 1)
                            g.dma_start(out=Vsb[127:128, :], in_=V_tm[0:1, ccp, :]
                                        ).then_inc(qg.h, 16)
                        with g.Else():
                            g.dma_start(out=Vsb[127:128, :], in_=vshfix[:]
                                        ).then_inc(qg.h, 16)
                        g.wait_ge(qg.h, q0 + 16 * (npd * c + 4))
                        g.sem_inc(ch.h, 1)

            of = 1 if fwd else 2

            @block.tensor
            def _(te, b0=b0, of=of):
                with te.Fori(0, NC) as c:
                    it = b0 + c * DCH
                    cc = nc.s_assert_le(nc.s_assert_ge(NC - 1 - c, 0), NC - 1) if not fwd else c
                    te.wait_ge(ch.h, it + of)
                    if fwd:
                        for c16 in range(16):
                            inst = te.matmul(pA[:, c16 * 4:(c16 + 1) * 4],
                                             self.ones_l[:, 0:128],
                                             cntst4f[:, bass.ts(c16 * NC + cc, 4)],
                                             start=(c16 == 0), stop=(c16 == 15))
                    else:
                        inst = te.matmul(pA[:, 0:4], self.ones_l[:, 0:128],
                                         i4b[:], start=True, stop=True)
                    inst.then_inc(ch.h, 1)
                    te.wait_ge(ch.h, it + of + 2)
                    for c16 in range(16):
                        inst = te.matmul(pA[0:5, 512 + c16 * 4:512 + (c16 + 1) * 4],
                                         self.ones_l[:, 0:5],
                                         cntch4f[:, bass.ts(c16 * NC + cc, 4)],
                                         start=(c16 == 0), stop=(c16 == 15))
                    inst.then_inc(ch.h, 1)
                    te.wait_ge(ch.h, it + of + 4)
                    for c16 in range(16):
                        te.matmul(psS[:, c16 * 64:(c16 + 1) * 64],
                                  msf[:, c16 * 4:(c16 + 1) * 4][:, 0:4] if False
                                  else msf[:, c16 * 4:(c16 + 1) * 4],
                                  Vget[:, cc, c16 * 64:(c16 + 1) * 64] if fwd
                                  else Vsb[:, c16 * 64:(c16 + 1) * 64],
                                  start=(c16 == 0 or c16 == 8), stop=False)
                        inst = te.matmul(psS[:, c16 * 64:(c16 + 1) * 64],
                                         mss[:, c16 * 4:(c16 + 1) * 4],
                                         Scur[:, c16 * 64:(c16 + 1) * 64],
                                         start=False,
                                         stop=(c16 == 7 or c16 == 15))
                    inst.then_inc(ch.h, 1)

            @block.vector
            def _(v, b0=b0, of=of):
                with v.Fori(0, NC) as c:
                    it = b0 + c * DCH
                    cc = nc.s_assert_le(nc.s_assert_ge(NC - 1 - c, 0), NC - 1) if not fwd else c
                    v.wait_ge(ch.h, it + of + 1)
                    if fwd:
                        for c16 in range(16):
                            inst = v.tensor_scalar(
                                msf[:, c16 * 4:(c16 + 1) * 4],
                                pA[:, c16 * 4:(c16 + 1) * 4],
                                Ccol[:, cc * 16 + c16][:, None], 0.0,
                                ALU.subtract, ALU.is_equal)
                    else:
                        for c16 in range(16):
                            inst = v.tensor_scalar(
                                msf[:, c16 * 4:(c16 + 1) * 4],
                                pA[:, 0:4],
                                Ccol[:, cc * 16 + c16][:, None], 0.0,
                                ALU.subtract, ALU.is_equal)
                    v.drain()
                    inst.then_inc(ch.h, 1)
                    v.wait_ge(ch.h, it + of + 3)
                    for c16 in range(16):
                        inst = v.tensor_scalar(
                            mss[:, c16 * 4:(c16 + 1) * 4],
                            pA[0:5, 512 + c16 * 4:512 + (c16 + 1) * 4],
                            iota5c[:], 0.0, ALU.subtract, ALU.is_equal)
                    v.drain()
                    inst.then_inc(ch.h, 1)
                    v.wait_ge(ch.h, it + of + 5)
                    inst = v.tensor_scalar(Scur[0:4, :], psS[:], 1.0, None, ALU.mult)
                    v.drain()
                    inst.then_inc(ch.h, 1)

            if fwd:
                qg.n += 16 * NC
            else:
                qg.n += 64 * NC
            ch.n = b0 + NC * DCH

        emit_chain(True)
        emit_chain(False)

        n0 = ch.n

        @block.gpsimd
        def _(g, n0=n0):
            g.wait_ge(ch.h, n0)
            self.dstep(g, qg, ch, d["r2f_i"][:], Scur_f[0:4, :])
            self.dstep(g, qg, ch, d["r2b_i"][:], Scur_b[0:4, :])
            g.collective_compute("AllGather", ALU.bypass, self.pairs,
                                 [d["r2f_i"].opt()], [d["r2f_o"].opt()]
                                 ).then_inc(ccs.h, 1)
            g.wait_ge(ccs.h, 2)
            g.sem_inc(ch.h, 1)
            g.collective_compute("AllGather", ALU.bypass, self.pairs,
                                 [d["r2b_i"].opt()], [d["r2b_o"].opt()]
                                 ).then_inc(ccs.h, 1)
            g.wait_ge(ccs.h, 3)
            g.sem_inc(ch.h, 1)
            self.dstep(g, qg, ch, l0f[:], d["r2f_o"][0:4, :])
            self.dstep(g, qg, ch, l1b[:], d["r2b_o"][4:8, :])
        ch.n += 6

        @block.vector
        def _(v):
            v.wait_ge(ch.h, ch.n)
            v.tensor_scalar(l0f[:], l0f[:], mcol[0:4, 0:1], None, ALU.mult)
            v.tensor_scalar(Vsf[0:4, 0:D], padrep[:], mcol[0:4, 1:2], None, ALU.mult)
            v.drain()
            v.tensor_tensor(Scur_f[0:4, :], l0f[:], Vsf[0:4, 0:D], ALU.add)
            v.drain()
            v.tensor_scalar(l1b[:], l1b[:], mcol[0:4, 1:2], None, ALU.mult)
            v.tensor_scalar(Vsf[0:4, 0:D], padrep[:], mcol[0:4, 0:1], None, ALU.mult)
            v.drain()
            inst = v.tensor_tensor(Scur_b[0:4, :], l1b[:], Vsf[0:4, 0:D], ALU.add)
            v.drain()
            inst.then_inc(ch.h, 1)
        ch.n += 1

        emit_chain(True)
        emit_chain(False)
        esC.close()

        esG = ExitStack()
        pO = esG.enter_context(nc.psum_tensor("abpO", [64, 2048], F32))

        n0 = ch.n

        @block.vector
        def _(v, n0=n0):
            v.wait_ge(ch.h, n0)
            v.memset(zg[:], 0.0)
            v.drain()
            v.sem_inc(ch.h, 1)
        ch.n += 1

        bgt = ch.n
        DGT = 23

        @block.vector
        def _(v):
            with v.Fori(0, NC) as c:
                it = bgt + c * DGT
                v.wait_ge(ch.h, it)
                inst = v.tensor_scalar(Vsf[:], V_tm[:, c, :], 1.0, None, ALU.mult)
                v.drain()
                inst.then_inc(ch.h, 1)
                v.wait_ge(ch.h, it + 4)
                inst = v.tensor_tensor(
                    Dlf[:].rearrange("p (a b) -> p a b", b=128),
                    pA[:].rearrange("p (a b) -> p a b", b=128),
                    Ccol_f[:, bass.ts(c, 16)][:, :, None].to_broadcast([128, 16, 128]),
                    ALU.subtract)
                v.drain()
                inst.then_inc(ch.h, 1)
                v.wait_ge(ch.h, it + 5)
                inst = v.scalar_tensor_tensor(
                    Mf[:].rearrange("p (a b) -> p a b", b=128),
                    Dlf[:].rearrange("p (a b) -> p a b", b=128), 0.0,
                    wex[0][:, :, None].to_broadcast([128, 16, 128]),
                    ALU.is_equal, ALU.mult)
                v.drain()
                inst.then_inc(ch.h, 1)
                for k in range(1, 4):
                    v.wait_ge(ch.h, it + 5 + k)
                    v.scalar_tensor_tensor(
                        pA[:].rearrange("p (a b) -> p a b", b=128),
                        Dlf[:].rearrange("p (a b) -> p a b", b=128), float(k),
                        wex[k][:, :, None].to_broadcast([128, 16, 128]),
                        ALU.is_equal, ALU.mult)
                    v.drain()
                    inst = v.tensor_tensor(Mf[:], Mf[:], pA[:], ALU.add)
                    v.drain()
                    inst.then_inc(ch.h, 1)
                v.wait_ge(ch.h, it + 9)
                v.tensor_scalar(Dfs[:], stage5f[:], iota5c[:], None, ALU.add)
                v.drain()
                v.scalar_tensor_tensor(
                    Mfs[:].rearrange("p (a b) -> p a b", b=128),
                    Dfs[:].rearrange("p (a b) -> p a b", b=128), 0.0,
                    wex[0][0:5, :, None].to_broadcast([5, 16, 128]),
                    ALU.is_equal, ALU.mult)
                v.drain()
                for k in range(1, 4):
                    v.scalar_tensor_tensor(
                        tmp5[:].rearrange("p (a b) -> p a b", b=128),
                        Dfs[:].rearrange("p (a b) -> p a b", b=128), float(k),
                        wex[k][0:5, :, None].to_broadcast([5, 16, 128]),
                        ALU.is_equal, ALU.mult)
                    v.drain()
                    v.tensor_tensor(Mfs[:], Mfs[:], tmp5[:], ALU.add)
                    v.drain()
                v.sem_inc(ch.h, 1)
                v.wait_ge(ch.h, it + 13)
                inst = v.tensor_tensor(
                    Dlf[:].rearrange("p (a b) -> p a b", b=128),
                    Ccol_b[:, bass.ts(c, 16)][:, :, None].to_broadcast([128, 16, 128]),
                    pA[:].rearrange("p (a b) -> p a b", b=128),
                    ALU.subtract)
                v.drain()
                inst.then_inc(ch.h, 1)
                v.wait_ge(ch.h, it + 14)
                inst = v.scalar_tensor_tensor(
                    Mf[:].rearrange("p (a b) -> p a b", b=128),
                    Dlf[:].rearrange("p (a b) -> p a b", b=128), 0.0,
                    wex[4][:, :, None].to_broadcast([128, 16, 128]),
                    ALU.is_equal, ALU.mult)
                v.drain()
                inst.then_inc(ch.h, 1)
                for k in range(1, 4):
                    v.wait_ge(ch.h, it + 14 + k)
                    v.scalar_tensor_tensor(
                        pA[:].rearrange("p (a b) -> p a b", b=128),
                        Dlf[:].rearrange("p (a b) -> p a b", b=128), float(k),
                        wex[4 + k][:, :, None].to_broadcast([128, 16, 128]),
                        ALU.is_equal, ALU.mult)
                    v.drain()
                    inst = v.tensor_tensor(Mf[:], Mf[:], pA[:], ALU.add)
                    v.drain()
                    inst.then_inc(ch.h, 1)
                v.wait_ge(ch.h, it + 18)
                v.tensor_tensor(Dfs[:].rearrange("p (a b) -> p a b", b=128),
                                Cbs[:, bass.ts(c, 16)][:, :, None]
                                .to_broadcast([5, 16, 128]),
                                stage5b[:].rearrange("p (a b) -> p a b", b=128),
                                ALU.subtract)
                v.drain()
                v.scalar_tensor_tensor(
                    Mfs[:].rearrange("p (a b) -> p a b", b=128),
                    Dfs[:].rearrange("p (a b) -> p a b", b=128), 0.0,
                    wex[4][0:5, :, None].to_broadcast([5, 16, 128]),
                    ALU.is_equal, ALU.mult)
                v.drain()
                for k in range(1, 4):
                    v.scalar_tensor_tensor(
                        tmp5[:].rearrange("p (a b) -> p a b", b=128),
                        Dfs[:].rearrange("p (a b) -> p a b", b=128), float(k),
                        wex[4 + k][0:5, :, None].to_broadcast([5, 16, 128]),
                        ALU.is_equal, ALU.mult)
                    v.drain()
                    v.tensor_tensor(Mfs[:], Mfs[:], tmp5[:], ALU.add)
                    v.drain()
                v.sem_inc(ch.h, 1)
                v.wait_ge(ch.h, it + 21)
                for c16 in range(16):
                    inst = v.tensor_tensor(
                        rstg[(c16 % 2) * 64:(c16 % 2) * 64 + 64, c16 // 2, :],
                        rstg[(c16 % 2) * 64:(c16 % 2) * 64 + 64, c16 // 2, :],
                        pO[:, c16 * 128:(c16 + 1) * 128], ALU.add)
                v.drain()
                inst.then_inc(ch.h, 1)

        @block.gpsimd
        def _(g, q0=qg.n):
            with g.Fori(0, NC) as c:
                it = bgt + c * DGT
                g.wait_ge(ch.h, it + 1)
                g.dma_start(out=Vtmp[:], in_=V_tm[:, c, :]).then_inc(qg.h, 16)
                g.dma_start(out=sfs[:], in_=d["sind"][0, c]).then_inc(qg.h, 16)
                g.dma_start(out=sbs[:], in_=d["sind"][1, c]).then_inc(qg.h, 16)
                g.dma_start(out=stage5f[:],
                            in_=d["csrd"][0][None, :, bass.ts(c, 128)]
                            .to_broadcast([5, 16, 128])).then_inc(qg.h, 16)
                g.dma_start(out=stage5b[:],
                            in_=d["csrd"][1][None, :, bass.ts(c, 128)]
                            .to_broadcast([5, 16, 128])).then_inc(qg.h, 16)
                g.wait_ge(qg.h, q0 + 16 * (8 * c + 5))
                g.sem_inc(ch.h, 1)
                g.dma_start(out=Vsb[0:127, :], in_=Vtmp[1:128, :]).then_inc(qg.h, 16)
                with g.If(c < NC - 1):
                    ccp = nc.s_assert_le(c + 1, NC - 1)
                    g.dma_start(out=Vsb[127:128, :], in_=V_tm[0:1, ccp, :]
                                ).then_inc(qg.h, 16)
                with g.Else():
                    g.dma_start(out=Vsb[127:128, :], in_=vshfix[:]
                                ).then_inc(qg.h, 16)
                g.wait_ge(qg.h, q0 + 16 * (8 * c + 7))
                g.sem_inc(ch.h, 1)
                g.wait_ge(ch.h, it + 22)
                self.dstep(g, qg, ch, dst_r[:, :, bass.ts(c, 128)], rstg[:],
                           qv=q0 + 16 * (8 * c + 8))

        @block.tensor
        def _(te):
            with te.Fori(0, NC) as c:
                it = bgt + c * DGT
                te.wait_ge(ch.h, it + 2)
                for q in range(4):
                    inst = te.matmul(pA[:, q * 512:(q + 1) * 512],
                                     self.ones_l[:, 0:128],
                                     stage5f[0:1, q * 512:(q + 1) * 512],
                                     start=True, stop=True)
                inst.then_inc(ch.h, 1)
                te.wait_ge(ch.h, it + 10)
                for q in range(4):
                    inst = te.matmul(pO[:, q * 512:(q + 1) * 512], zg[:, 0:64], zg[:],
                                     start=True, stop=False)
                inst.then_inc(ch.h, 1)
                te.wait_ge(ch.h, it + 11)
                for c16 in range(16):
                    te.matmul(pO[:, c16 * 128:(c16 + 1) * 128],
                              Vsf[:, c16 * 64:(c16 + 1) * 64],
                              Mf[:, c16 * 128:(c16 + 1) * 128],
                              start=False, stop=False)
                    inst = te.matmul(pO[:, c16 * 128:(c16 + 1) * 128],
                                     sfs[:, c16 * 64:(c16 + 1) * 64],
                                     Mfs[:, c16 * 128:(c16 + 1) * 128],
                                     start=False, stop=False)
                inst.then_inc(ch.h, 1)
                te.wait_ge(ch.h, it + 12)
                for q in range(4):
                    inst = te.matmul(pA[:, q * 512:(q + 1) * 512],
                                     self.ones_l[:, 0:128],
                                     stage5b[0:1, q * 512:(q + 1) * 512],
                                     start=True, stop=True)
                inst.then_inc(ch.h, 1)
                te.wait_ge(ch.h, it + 19)
                for c16 in range(16):
                    te.matmul(pO[:, c16 * 128:(c16 + 1) * 128],
                              Vsb[:, c16 * 64:(c16 + 1) * 64],
                              Mf[:, c16 * 128:(c16 + 1) * 128],
                              start=False, stop=False)
                    inst = te.matmul(pO[:, c16 * 128:(c16 + 1) * 128],
                                     sbs[:, c16 * 64:(c16 + 1) * 64],
                                     Mfs[:, c16 * 128:(c16 + 1) * 128],
                                     start=False, stop=False)
                with te.If(c == 0):
                    for c16 in range(16):
                        te.matmul(pO[:, c16 * 128:c16 * 128 + 1],
                                  r1t[0:1, c16 * 64:(c16 + 1) * 64],
                                  e0sw[:, c16:c16 + 1],
                                  start=False, stop=False)
                for q in range(4):
                    inst = te.matmul(pO[:, q * 512:(q + 1) * 512], zg[:, 0:64],
                                     zg[:], start=False, stop=True)
                inst.then_inc(ch.h, 1)

        @block.sync
        def _(sy, q0=qsy.n):
            with sy.Fori(0, NC) as c:
                it = bgt + c * DGT
                sy.wait_ge(ch.h, it + 20)
                self.dstep(sy, qsy, ch, rstg[:], src_r[:, :, bass.ts(c, 128)],
                           qv=q0 + 16 * (c + 1))

        qsy.n += 16 * NC
        qg.n += 112 * NC + 16 * NC
        ch.n = bgt + NC * DGT
        esG.close()
        es.close()
        esA.close()
        return ch


def _prep_inputs(inputs, T, n_cores):
    f32 = np.float32
    x = inputs["L0_attn_a_x"]
    per_core = []
    wq = np.ascontiguousarray(inputs["L0_attn_a_Wq"].reshape(D, D))
    wk = np.ascontiguousarray(inputs["L0_attn_a_Wk"].reshape(D, D))
    wv = np.ascontiguousarray(inputs["L0_attn_a_Wv"].reshape(D, D))
    wo = np.ascontiguousarray(inputs["L0_attn_a_Wo"].reshape(D, D))
    bqkvo = np.concatenate([inputs["L0_attn_a_bq"].reshape(-1),
                            inputs["L0_attn_a_bk"].reshape(-1),
                            inputs["L0_attn_a_bv"].reshape(-1),
                            inputs["L0_attn_a_bo"].reshape(-1)])[None, :]
    bwk = np.ascontiguousarray(inputs["L0_attn_b_Wk"])
    bwv = np.ascontiguousarray(inputs["L0_attn_b_Wv"])
    bbkv = np.concatenate([inputs["L0_attn_b_bk"].reshape(-1),
                           inputs["L0_attn_b_bv"].reshape(-1)])[None, :]
    rhw = inputs["L0_attn_b_ReadingHead"].reshape(1, D)
    bww = np.ascontiguousarray(inputs["L0_attn_b_bidirection_weight"].T).reshape(1, -1)
    gwt = np.ascontiguousarray(inputs["L0_ffn_gate_w"].T)
    gbr = inputs["L0_ffn_gate_b"][None, :]
    w1c = np.concatenate([inputs["L0_ffn_w1"], inputs["L0_ffn_sw1"][None]], 0)
    w3c = np.concatenate([inputs["L0_ffn_w3"], inputs["L0_ffn_sw3"][None]], 0)
    w2c = np.concatenate([inputs["L0_ffn_w2"], inputs["L0_ffn_sw2"][None]], 0)
    b1c = np.concatenate([inputs["L0_ffn_b1"].reshape(-1),
                          inputs["L0_ffn_sb1"].reshape(-1)])[None, :]
    b3c = np.concatenate([inputs["L0_ffn_b3"].reshape(-1),
                          inputs["L0_ffn_sb3"].reshape(-1)])[None, :]
    b2c = np.concatenate([inputs["L0_ffn_b2"].reshape(-1),
                          inputs["L0_ffn_sb2"].reshape(-1)])[None, :]
    shared = dict(wq=wq, wk=wk, wv=wv, wo=wo, bqkvo=bqkvo, bwk=bwk, bwv=bwv,
                  bbkv=bbkv, rhw=rhw, bww=bww, gwt=gwt, gbr=gbr,
                  w1c=w1c, w3c=w3c, w2c=w2c, b1c=b1c, b3c=b3c, b2c=b2c)
    shared = {k: np.ascontiguousarray(v, dtype=f32) for k, v in shared.items()}
    for core in range(n_cores):
        b, half = core // 2, core % 2
        xin = np.ascontiguousarray(x[b, half * T:(half + 1) * T, :].T, dtype=f32)
        m = np.zeros((1, 16), f32)
        m[0, 0] = float(half)
        m[0, 1] = 1.0 - float(half)
        per_core.append(dict(shared, xin=xin, meta=m))
    return per_core


def kernel(**inputs):
    T = 4096
    n_cores = 8
    b = B(T, n_cores)
    nc = b.build()
    in_maps = _prep_inputs(inputs, T, n_cores)
    res = run_bass_kernel_spmd(nc, in_maps, core_ids=list(range(n_cores)))
    x = inputs["L0_attn_a_x"]
    Bsz, S, _ = x.shape
    out = np.empty((Bsz, S, D), np.float32)
    for core in range(n_cores):
        bb, half = core // 2, core % 2
        out[bb, half * T:(half + 1) * T, :] = np.asarray(res.results[core]["out"]).T
    return out
